# version 01
# baseline (speedup 1.0000x reference)
import sys
if "/opt/trn_rl_repo" not in sys.path:
    sys.path.insert(0, "/opt/trn_rl_repo")
import contextlib
import numpy as np
import concourse.bass as bass
import concourse.mybir as mybir

F32 = mybir.dt.float32
BF16 = mybir.dt.bfloat16
I8 = mybir.dt.int8
AF = mybir.ActivationFunctionType
ALU = mybir.AluOpType
AX = mybir.AxisListType

D, T, NCH, TB, NTB = 1024, 2048, 8, 512, 4
I0, NI0 = 2816, 22
QMAX = 127.0
EPS = 1e-5


class B:
    def __init__(self, nc, es):
        self.nc, self.es = nc, es
        self.sems, self.cnt, self.uid = {}, {}, 0

    def name(self, base):
        self.uid += 1
        return f"{base}_{self.uid}"

    def sem(self, name):
        if name not in self.sems:
            self.sems[name] = self.es.enter_context(self.nc.semaphore(name))
            self.cnt[name] = 0
        return self.sems[name]

    def inc(self, instr, name, n=1):
        self.sem(name)
        self.cnt[name] += n
        instr.then_inc(self.sem(name), n)
        return self.cnt[name]

    def dma(self, eng, out, in_, name):
        return self.inc(eng.dma_start(out=out, in_=in_), name, 16)

    def mark(self, eng, tag, name, n=1):
        self.sem(name)
        eng.wait_ge(self.sems[name], self.cnt[name])
        self.inc(eng.memzero(tag[:]), name, n)

    def wait(self, eng, name, val=None):
        self.sem(name)
        eng.wait_ge(self.sems[name], self.cnt[name] if val is None else val)

    def base(self, *names):
        return tuple(self.cnt.setdefault(n, (self.sem(n), 0)[1]) for n in names)

    def sb(self, scope, base, shape, dt):
        return scope.enter_context(self.nc.sbuf_tensor(self.name(base), shape, dt))

    def ps(self, scope, base, shape, dt=F32):
        return scope.enter_context(self.nc.psum_tensor(self.name(base), shape, dt))


def emit_stats(b, X, K, kind):
    nc, R = b.nc, K["R"]
    with contextlib.ExitStack() as sc:
        R = dict(R)
        R["t4"] = b.sb(sc, "row_t4", [1, T], F32)
        R["t5"] = b.sb(sc, "row_t5", [1, T], F32)
        xbf = [b.sb(sc, "st_xbf", [128, T], BF16) for _ in range(2)]
        ebf = [b.sb(sc, "st_ebf", [128, T], BF16) for _ in range(2)]
        xsq32 = [b.sb(sc, "st_xs32", [128, T], F32) for _ in range(2)]
        xsq = [b.sb(sc, "st_xsq", [128, T], BF16) for _ in range(2)]
        e2bf = [b.sb(sc, "st_e2", [128, T], BF16) for _ in range(2)]
        pst_mu = [b.ps(sc, "st_pmu", [1, TB]) for _ in range(NTB)]
        pst_sq = [b.ps(sc, "st_psq", [1, TB]) for _ in range(NTB)]
        (te0, sc0, ve0) = b.base("te", "sc", "ve")
        with nc.Block() as block:
            @block.scalar
            def _(scalar):
                for c in range(NCH):
                    if c >= 2:
                        scalar.wait_ge(b.sem("te"), te0 + c - 1)
                    scalar.activation(out=xsq32[c % 2][:], in_=X[c][:], func=AF.Square)
                    scalar.drain()
                    scalar.copy(out=xsq[c % 2][:], in_=xsq32[c % 2][:])
                    if kind == "ln":
                        scalar.copy(out=xbf[c % 2][:], in_=X[c][:])
                    b.mark(scalar, b.tags, "sc", 1)

            @block.vector
            def _(vector):
                for c in range(NCH):
                    vector.wait_ge(b.sem("sc"), sc0 + c + 1)
                    if c >= 2:
                        vector.wait_ge(b.sem("te"), te0 + c - 1)
                    if kind == "ln":
                        vector.tensor_sub(out=ebf[c % 2][:], in0=X[c][:], in1=xbf[c % 2][:])
                        vector.drain()
                    vector.tensor_sub(out=e2bf[c % 2][:], in0=xsq32[c % 2][:],
                                      in1=xsq[c % 2][:])
                    vector.drain()
                    b.mark(vector, b.tagv, "ve", 1)

            @block.tensor
            def _(tensor):
                for c in range(NCH):
                    tensor.wait_ge(b.sem("ve"), ve0 + c + 1)
                    for tb in range(NTB):
                        sl = slice(tb * TB, (tb + 1) * TB)
                        if kind == "ln":
                            tensor.matmul(pst_mu[tb][0:1, :], K["onesm"][:],
                                          xbf[c % 2][:, sl],
                                          start=(c == 0), stop=False)
                            tensor.matmul(pst_mu[tb][0:1, :], K["onesm"][:],
                                          ebf[c % 2][:, sl],
                                          start=False, stop=(c == NCH - 1))
                        tensor.matmul(pst_sq[tb][0:1, :], K["onesm"][:],
                                      xsq[c % 2][:, sl],
                                      start=(c == 0), stop=False)
                        ins = tensor.matmul(pst_sq[tb][0:1, :], K["onesm"][:],
                                            e2bf[c % 2][:, sl],
                                            start=False, stop=(c == NCH - 1))
                    b.inc(ins, "te", 1)

        with nc.Block() as block:
            @block.vector
            def _(vector):
                for tb in range(NTB):
                    if kind == "ln":
                        vector.tensor_copy(out=R["mu"][0:1, tb * TB:(tb + 1) * TB],
                                           in_=pst_mu[tb][0:1, :])
                    vector.tensor_copy(out=R["msq"][0:1, tb * TB:(tb + 1) * TB],
                                       in_=pst_sq[tb][0:1, :])
                vector.drain()
                if kind == "ln":
                    vector.tensor_mul(out=R["t4"][0:1, :], in0=R["mu"][0:1, :],
                                      in1=R["mu"][0:1, :])
                    vector.drain()
                    vector.tensor_sub(out=R["t4"][0:1, :], in0=R["msq"][0:1, :],
                                      in1=R["t4"][0:1, :])
                    vector.drain()
                b.mark(vector, b.tagv, "ve", 1)

            @block.scalar
            def _(scalar):
                b.wait(scalar, "ve")
                if kind == "ln":
                    scalar.activation(out=R["t5"][0:1, :], in_=R["t4"][0:1, :],
                                      func=AF.Sqrt, bias=K["eps11"][:], scale=1.0)
                else:
                    scalar.activation(out=R["t5"][0:1, :], in_=R["msq"][0:1, :],
                                      func=AF.Sqrt)
                scalar.drain()
                b.mark(scalar, b.tags, "sc", 1)

        with nc.Block() as block:
            @block.vector
            def _(vector):
                vector.reciprocal(out=R["r"][0:1, :], in_=R["t5"][0:1, :])
                vector.drain()
                if kind == "ln":
                    vector.tensor_mul(out=R["mur"][0:1, :], in0=R["mu"][0:1, :],
                                      in1=R["r"][0:1, :])
                    vector.drain()
                b.mark(vector, b.tagv, "ve", 1)


def emit_normalize(b, K, X, dst32, dst_bf, kind="ln", wcol=None):
    nc, R = b.nc, K["R"]
    with contextlib.ExitStack() as sc:
        pr = [b.ps(sc, "no_pr", [128, TB]) for _ in range(2)]
        pb = [b.ps(sc, "no_pb", [128, TB]) for _ in range(2)]
        rbs = [b.sb(sc, "no_rbs", [128, TB], F32) for _ in range(2)]
        (te0, sc0, ve0) = b.base("te", "sc", "ve")
        with nc.Block() as block:
            @block.tensor
            def _(tensor):
                for tb in range(NTB):
                    if tb >= 2:
                        tensor.wait_ge(b.sem("ve"), ve0 + tb - 1)
                    ins = tensor.matmul(pr[tb % 2][:, :], K["ones_row32"][:],
                                        R["r"][0:1, tb * TB:(tb + 1) * TB],
                                        start=True, stop=True)
                    if kind == "ln":
                        ins = tensor.matmul(pb[tb % 2][:, :], K["ones_row32"][:],
                                            R["mur"][0:1, tb * TB:(tb + 1) * TB],
                                            start=True, stop=True)
                    b.inc(ins, "te", 1)

            @block.scalar
            def _(scalar):
                if kind == "ln":
                    for tb in range(NTB):
                        scalar.wait_ge(b.sem("te"), te0 + tb + 1)
                        if tb >= 2:
                            scalar.wait_ge(b.sem("ve"), ve0 + tb - 1)
                        scalar.copy(out=rbs[tb % 2][:], in_=pr[tb % 2][:, :])
                        b.mark(scalar, b.tags, "sc", 1)

            @block.vector
            def _(vector):
                for tb in range(NTB):
                    sl = slice(tb * TB, (tb + 1) * TB)
                    if kind == "ln":
                        vector.wait_ge(b.sem("sc"), sc0 + tb + 1)
                    else:
                        vector.wait_ge(b.sem("te"), te0 + tb + 1)
                    for c in range(NCH):
                        if kind == "ln":
                            tgt = (dst32[c] if dst32 else dst_bf[c])[:, sl]
                            vector.tensor_mul(out=tgt, in0=X[c][:, sl], in1=rbs[tb % 2][:])
                            vector.drain()
                            vector.scalar_tensor_tensor(out=tgt, in0=tgt, scalar=1.0,
                                                        in1=pb[tb % 2][:, :],
                                                        op0=ALU.mult, op1=ALU.subtract)
                            vector.drain()
                            if dst32 is not None and dst_bf is not None:
                                vector.tensor_copy(out=dst_bf[c][:, sl], in_=tgt)
                                vector.drain()
                        else:
                            vector.scalar_tensor_tensor(out=dst_bf[c][:, sl],
                                                        in0=X[c][:, sl], scalar=wcol[c],
                                                        in1=pr[tb % 2][:, :],
                                                        op0=ALU.mult, op1=ALU.mult)
                            vector.drain()
                    b.mark(vector, b.tagv, "ve", 1)



def emit_bcast_full(b, K, row_ap, dst_full):
    nc = b.nc
    with contextlib.ExitStack() as sc:
        pbc = [b.ps(sc, "bc_ps", [128, TB]) for _ in range(2)]
        (te0, sc0) = b.base("te", "sc")
        with nc.Block() as block:
            @block.tensor
            def _(tensor):
                for tb in range(NTB):
                    if tb >= 2:
                        tensor.wait_ge(b.sem("sc"), sc0 + tb - 1)
                    ins = tensor.matmul(pbc[tb % 2][:, :], K["ones_row32"][:],
                                        row_ap[0:1, tb * TB:(tb + 1) * TB],
                                        start=True, stop=True)
                    b.inc(ins, "te", 1)

            @block.scalar
            def _(scalar):
                for tb in range(NTB):
                    scalar.wait_ge(b.sem("te"), te0 + tb + 1)
                    scalar.copy(out=dst_full[:, tb * TB:(tb + 1) * TB],
                                in_=pbc[tb % 2][:, :])
                    b.mark(scalar, b.tags, "sc", 1)


def emit_global_max(b, K, col, dst11, cc_in, cc_out, n_cores, sc_):
    nc = b.nc
    tp = b.ps(sc_, "gm_tp", [128, 128])
    loc = b.sb(sc_, "gm_loc", [1, 1], F32)
    row = b.sb(sc_, "gm_row", [1, 128], F32)
    with nc.Block() as block:
        @block.tensor
        def _(tensor):
            ins = tensor.transpose(tp[0:1, 0:128], col[:], K["ident32"][:])
            b.inc(ins, "te", 1)

        @block.vector
        def _(vector):
            b.wait(vector, "te")
            vector.tensor_copy(out=row[:], in_=tp[0:1, 0:128])
            vector.drain()
            vector.tensor_reduce(out=loc[:], in_=row[:], axis=AX.X, op=ALU.max)
            vector.drain()
            b.mark(vector, b.tagv, "ve", 1)

    with nc.Block() as block:
        @block.sync
        def _(sync):
            b.dma(sync, cc_in, loc[:], "ccd")

        @block.gpsimd
        def _(gpsimd):
            b.wait(gpsimd, "ccd")
            ins = gpsimd.collective_compute("AllReduce", ALU.max,
                                            replica_groups=[list(range(n_cores))],
                                            ins=[cc_in], outs=[cc_out])
            b.inc(ins, "cc", 1)
            b.wait(gpsimd, "cc")

    with nc.Block() as block:
        @block.sync
        def _(sync):
            b.dma(sync, dst11[:], cc_out, "ccd")
            b.wait(sync, "ccd")


def emit_scale_rows(b, K, clip11, gmax11, inv_col, s_col, sc_):
    nc = b.nc
    t = b.sb(sc_, "sr_t", [1, 4], F32)
    pbc = b.ps(sc_, "sr_ps", [128, 2])
    with nc.Block() as block:
        @block.scalar
        def _(scalar):
            scalar.activation(out=t[:, 0:1], in_=clip11, func=AF.Sigmoid)
            scalar.drain()
            b.mark(scalar, b.tags, "sc", 1)

        @block.vector
        def _(vector):
            b.wait(vector, "sc")
            vector.tensor_mul(out=t[:, 1:2], in0=t[:, 0:1], in1=gmax11[:])
            vector.drain()
            vector.tensor_scalar_mul(out=t[:, 1:2], in0=t[:, 1:2], scalar1=1.0 / QMAX)
            vector.drain()
            vector.reciprocal(out=t[:, 2:3], in_=t[:, 1:2])
            vector.drain()
            b.mark(vector, b.tagv, "ve", 1)

    with nc.Block() as block:
        @block.tensor
        def _(tensor):
            tensor.matmul(pbc[:, 0:1], K["ones_row32"][:], t[:, 1:2], start=True, stop=True)
            ins = tensor.matmul(pbc[:, 1:2], K["ones_row32"][:], t[:, 2:3], start=True, stop=True)
            b.inc(ins, "te", 1)

        @block.vector
        def _(vector):
            b.wait(vector, "te")
            vector.tensor_copy(out=s_col[:], in_=pbc[:, 0:1])
            vector.drain()
            vector.tensor_copy(out=inv_col[:], in_=pbc[:, 1:2])
            vector.drain()
            b.mark(vector, b.tagv, "ve", 1)


def emit_quant_away(b, src, dst_bf, W, Q, inv_col):
    nc = b.nc
    (ve0,) = b.base("ve")
    with nc.Block() as block:
        @block.scalar
        def _(scalar):
            scalar.activation(out=Q["a"][:, 0:W], in_=src, func=AF.Abs, scale=inv_col[:])
            scalar.activation(out=Q["sgn"][:, 0:W], in_=src, func=AF.Sign)
            b.mark(scalar, b.tags, "sc", 1)
            scalar.wait_ge(b.sem("ve"), ve0 + 1)
            scalar.copy(out=Q["c1f"][:, 0:W], in_=Q["i8"][:, 0:W])
            b.mark(scalar, b.tags, "sc", 1)

        @block.vector
        def _(vector):
            b.wait(vector, "sc", b.cnt["sc"] - 1)
            vector.tensor_scalar(out=Q["i8"][:, 0:W], in0=Q["a"][:, 0:W],
                                 scalar1=127.49, scalar2=None, op0=ALU.min)
            vector.drain()
            b.mark(vector, b.tagv, "ve", 1)

    with nc.Block() as block:
        @block.vector
        def _(vector):
            vector.scalar_tensor_tensor(out=Q["t1"][:, 0:W], in0=Q["a"][:, 0:W],
                                        scalar=1.0, in1=Q["c1f"][:, 0:W],
                                        op0=ALU.mult, op1=ALU.subtract)
            vector.drain()
            vector.scalar_tensor_tensor(out=Q["t1"][:, 0:W], in0=Q["t1"][:, 0:W],
                                        scalar=0.5, in1=Q["c1f"][:, 0:W],
                                        op0=ALU.is_ge, op1=ALU.add)
            vector.drain()
            vector.scalar_tensor_tensor(out=Q["t1"][:, 0:W], in0=Q["t1"][:, 0:W],
                                        scalar=128.0, in1=Q["sgn"][:, 0:W],
                                        op0=ALU.min, op1=ALU.mult)
            vector.drain()
            vector.tensor_scalar(out=dst_bf, in0=Q["t1"][:, 0:W],
                                 scalar1=127.0, scalar2=-128.0, op0=ALU.min, op1=ALU.max)
            vector.drain()
            b.mark(vector, b.tagv, "ve", 1)


def emit_l0_weight_quant(b, K, w_gate, w_up, w_down, qgT, quT, qdT,
                         sg_sc, su_sc, sd_sc):
    nc = b.nc
    jobs = [(w_gate, qgT, sg_sc, NI0, D), (w_up, quT, su_sc, NI0, D),
            (w_down, qdT, sd_sc, NCH, I0)]
    with contextlib.ExitStack() as sc:
        wt = [b.sb(sc, "wq_w", [128, I0], F32) for _ in range(2)]
        a_t = b.sb(sc, "wq_a", [128, I0], F32)
        i8t = b.sb(sc, "wq_i8", [128, I0], I8)
        odd8 = b.sb(sc, "wq_o8", [128, I0], I8)
        c1f = b.sb(sc, "wq_c1f", [128, I0], F32)
        oddf = b.sb(sc, "wq_oddf", [128, I0], F32)
        dl = b.sb(sc, "wq_dl", [128, I0], F32)
        gt = b.sb(sc, "wq_gt", [128, I0], F32)
        sgn = b.sb(sc, "wq_sgn", [128, I0], F32)
        qt = b.sb(sc, "wq_q", [128, I0], BF16)
        rmx = b.sb(sc, "wq_rmx", [128, 1], F32)
        inw = b.sb(sc, "wq_inw", [128, 1], F32)
        tpe = [b.sb(sc, "wq_tpe", [128, 128], BF16) for _ in range(2)]
        tp_ps = [b.ps(sc, "wq_tp", [128, 128], BF16) for _ in range(2)]

        for (src, dstT, scs, nrc, fd) in jobs:
            nsub = fd // 128
            for rc in range(nrc):
                wbuf = wt[rc % 2]
                fsl = slice(0, fd)
                with nc.Block() as block:
                    @block.sync
                    def _(sync):
                        b.dma(sync, wbuf[:, fsl], src[rc * 128:(rc + 1) * 128, :], "dma")

                    @block.vector
                    def _(vector):
                        b.wait(vector, "dma")
                        vector.tensor_reduce(out=rmx[:], in_=wbuf[:, fsl], axis=AX.X,
                                             op=ALU.max, apply_absolute_value=True)
                        vector.drain()
                        vector.tensor_scalar_mul(out=scs[:, rc:rc + 1], in0=rmx[:],
                                                 scalar1=1.0 / QMAX)
                        vector.reciprocal(out=inw[:], in_=rmx[:])
                        vector.drain()
                        vector.tensor_scalar_mul(out=inw[:], in0=inw[:], scalar1=QMAX)
                        vector.drain()
                        b.mark(vector, b.tagv, "ve", 1)

                    @block.scalar
                    def _(scalar):
                        b.wait(scalar, "ve")
                        scalar.activation(out=a_t[:, fsl], in_=wbuf[:, fsl],
                                          func=AF.Abs, scale=inw[:])
                        scalar.activation(out=sgn[:, fsl], in_=wbuf[:, fsl], func=AF.Sign)
                        b.mark(scalar, b.tags, "sc", 1)

                with nc.Block() as block:
                    @block.vector
                    def _(vector):
                        vector.tensor_scalar(out=i8t[:, fsl], in0=a_t[:, fsl],
                                             scalar1=127.49, scalar2=None, op0=ALU.min)
                        vector.drain()
                        vector.tensor_single_scalar(out=odd8[:, fsl], in_=i8t[:, fsl],
                                                    scalar=1, op=ALU.bitwise_and)
                        vector.drain()
                        b.mark(vector, b.tagv, "ve", 1)

                    @block.scalar
                    def _(scalar):
                        b.wait(scalar, "ve")
                        scalar.copy(out=c1f[:, fsl], in_=i8t[:, fsl])
                        scalar.copy(out=oddf[:, fsl], in_=odd8[:, fsl])
                        b.mark(scalar, b.tags, "sc", 1)

                with nc.Block() as block:
                    @block.vector
                    def _(vector):
                        vector.scalar_tensor_tensor(out=dl[:, fsl], in0=a_t[:, fsl],
                                                    scalar=1.0, in1=c1f[:, fsl],
                                                    op0=ALU.mult, op1=ALU.subtract)
                        vector.drain()
                        vector.tensor_single_scalar(out=gt[:, fsl], in_=dl[:, fsl],
                                                    scalar=0.5, op=ALU.is_gt)
                        vector.drain()
                        vector.tensor_single_scalar(out=dl[:, fsl], in_=dl[:, fsl],
                                                    scalar=0.5, op=ALU.is_equal)
                        vector.drain()
                        vector.tensor_mul(out=dl[:, fsl], in0=dl[:, fsl], in1=oddf[:, fsl])
                        vector.drain()
                        vector.tensor_add(out=gt[:, fsl], in0=gt[:, fsl], in1=dl[:, fsl])
                        vector.drain()
                        vector.tensor_add(out=gt[:, fsl], in0=gt[:, fsl], in1=c1f[:, fsl])
                        vector.drain()
                        vector.tensor_mul(out=qt[:, fsl], in0=gt[:, fsl], in1=sgn[:, fsl])
                        vector.drain()
                        b.mark(vector, b.tagv, "ve", 1)

                (vt0, tt0) = b.base("vt", "tt")
                sd_par0 = [b.base("sdma0")[0], b.base("sdma1")[0]]
                with nc.Block() as block:
                    @block.tensor
                    def _(tensor):
                        for s in range(nsub):
                            if s >= 2:
                                tensor.wait_ge(b.sem("vt"), vt0 + s - 1)
                            ins = tensor.transpose(tp_ps[s % 2][:, :],
                                             qt[:, s * 128:(s + 1) * 128], K["ident"][:])
                            b.inc(ins, "tt", 1)

                    @block.vector
                    def _(vector):
                        for s in range(nsub):
                            vector.wait_ge(b.sem("tt"), tt0 + s + 1)
                            if s >= 2:
                                vector.wait_ge(b.sem(f"sdma{s % 2}"),
                                               sd_par0[s % 2] + 16 * ((s - 2) // 2 + 1))
                            ins = vector.tensor_copy(out=tpe[s % 2][:], in_=tp_ps[s % 2][:])
                            b.inc(ins, "vt", 1)
                            vector.drain()

                    @block.scalar
                    def _(scalar):
                        for s in range(nsub):
                            scalar.wait_ge(b.sem("vt"), vt0 + s + 1)
                            b.dma(scalar,
                                  dstT[s * 128:(s + 1) * 128, rc * 128:(rc + 1) * 128],
                                  tpe[s % 2][:], f"sdma{s % 2}")
                        b.wait(scalar, "sdma0")
                        b.wait(scalar, "sdma1")


def emit_l0_ffn(b, K, X, s_clip, qgT, quT, qdT, sg_sc, su_sc, sd_sc,
                h_spill, CC, n_cores):
    nc = b.nc
    with contextlib.ExitStack() as osc:
        gmax = b.sb(osc, "ff_gmax", [1, 1], F32)
        inv_col = b.sb(osc, "ff_inv", [128, 1], F32)
        sx_col = b.sb(osc, "ff_sxc", [128, 1], F32)
        hgmax = b.sb(osc, "ff_hgm", [1, 1], F32)
        hinv_col = b.sb(osc, "ff_hinv", [128, 1], F32)
        sh_col = b.sb(osc, "ff_shc", [128, 1], F32)
        sgo = b.sb(osc, "ff_sgo", [128, NI0], F32)
        suo = b.sb(osc, "ff_suo", [128, NI0], F32)
        sdo = b.sb(osc, "ff_sdo", [128, NCH], F32)
        hmax = b.sb(osc, "ff_hmax", [128, 1], F32)
        xq = [b.sb(osc, "ff_xq", [128, T], BF16) for _ in range(NCH)]

        emit_stats(b, X, K, "ln")
        with contextlib.ExitStack() as sc1:
            rbs_full = b.sb(sc1, "ff_rbs", [128, T], F32)
            mbs_full = b.sb(sc1, "ff_mbs", [128, T], F32)
            lnc = b.sb(sc1, "ff_lnc", [128, T], F32)
            amax = b.sb(sc1, "ff_amax", [128, 1], F32)
            emit_bcast_full(b, K, K["R"]["r"], rbs_full)
            emit_bcast_full(b, K, K["R"]["mur"], mbs_full)

            with nc.Block() as block:
                @block.vector
                def _(vector):
                    for c in range(NCH):
                        vector.tensor_mul(out=lnc[:], in0=X[c][:], in1=rbs_full[:])
                        vector.drain()
                        vector.tensor_sub(out=lnc[:], in0=lnc[:], in1=mbs_full[:])
                        vector.drain()
                        dst = amax if c == 0 else K["col1"]
                        vector.tensor_reduce(out=dst[:], in_=lnc[:], axis=AX.X,
                                             op=ALU.max, apply_absolute_value=True)
                        vector.drain()
                        if c > 0:
                            vector.tensor_max(out=amax[:], in0=amax[:], in1=K["col1"][:])
                            vector.drain()
                    b.mark(vector, b.tagv, "ve", 1)
            emit_global_max(b, K, amax, gmax, CC["in"][CC["i"]], CC["out"][CC["i"]],
                            n_cores, sc1)
            CC["i"] += 1
            emit_scale_rows(b, K, s_clip[:, 0:1], gmax, inv_col, sx_col, sc1)

            Q = {"a": b.sb(sc1, "ffq_a", [128, T], F32),
                 "i8": b.sb(sc1, "ffq_i8", [128, T], I8),
                 "c1f": b.sb(sc1, "ffq_c1f", [128, T], F32),
                 "sgn": b.sb(sc1, "ffq_sgn", [128, T], F32),
                 "t1": b.sb(sc1, "ffq_t1", [128, T], F32)}
            for c in range(NCH):
                with nc.Block() as block:
                    @block.vector
                    def _(vector):
                        vector.tensor_mul(out=lnc[:], in0=X[c][:], in1=rbs_full[:])
                        vector.drain()
                        vector.tensor_sub(out=lnc[:], in0=lnc[:], in1=mbs_full[:])
                        vector.drain()
                        b.mark(vector, b.tagv, "ve", 1)
                emit_quant_away(b, lnc[:], xq[c][:], T, Q, inv_col)

        with nc.Block() as block:
            @block.vector
            def _(vector):
                for i in range(NI0):
                    vector.tensor_mul(out=sgo[:, i:i + 1], in0=sg_sc[:, i:i + 1],
                                      in1=sx_col[:])
                    vector.tensor_mul(out=suo[:, i:i + 1], in0=su_sc[:, i:i + 1],
                                      in1=sx_col[:])
                vector.drain()
                b.mark(vector, b.tagv, "ve", 1)

        with contextlib.ExitStack() as sc2:
            lg = [b.sb(sc2, "ff_lg", [128, D], BF16) for _ in range(2)]
            lu = [b.sb(sc2, "ff_lu", [128, D], BF16) for _ in range(2)]
            gs = b.sb(sc2, "ff_gs", [128, TB], F32)
            us = b.sb(sc2, "ff_us", [128, TB], F32)
            ht = [b.sb(sc2, "ff_ht", [128, TB], F32) for _ in range(2)]
            pg = [b.ps(sc2, "ff_pg", [128, TB]) for _ in range(2)]
            pu = [b.ps(sc2, "ff_pu", [128, TB]) for _ in range(2)]

            niter = NTB * NI0
            (dm0, te0, sc0, ve0) = b.base("dma", "te", "sc", "ve")
            hs_par0 = [b.base("hsp0")[0], b.base("hsp1")[0]]
            wld0 = [b.base("wld0")[0], b.base("wld1")[0]]
            with nc.Block() as block:
                @block.sync
                def _(sync):
                    it = 0
                    for tb in range(NTB):
                        for ic in range(NI0):
                            sync.wait_ge(b.sem("te"), te0 + max(0, it - 1))
                            csl = slice(ic * 128, (ic + 1) * 128)
                            b.dma(sync,
                                  lg[it % 2][:].rearrange("p (s c) -> p s c", c=128),
                                  qgT[:, csl].rearrange("(s p) c -> p s c", p=128),
                                  f"wld{it % 2}")
                            b.dma(sync,
                                  lu[it % 2][:].rearrange("p (s c) -> p s c", c=128),
                                  quT[:, csl].rearrange("(s p) c -> p s c", p=128),
                                  f"wld{it % 2}")
                            it += 1

                @block.tensor
                def _(tensor):
                    it = 0
                    for tb in range(NTB):
                        sl = slice(tb * TB, (tb + 1) * TB)
                        for ic in range(NI0):
                            tensor.wait_ge(b.sem(f"wld{it % 2}"), wld0[it % 2] + 32 * (it // 2 + 1))
                            if it >= 2:
                                tensor.wait_ge(b.sem("sc"), sc0 + 2 * (it - 1))
                            for s in range(NCH):
                                tensor.matmul(pg[it % 2][:, :],
                                              lg[it % 2][:, s * 128:(s + 1) * 128],
                                              xq[s][:, sl],
                                              start=(s == 0), stop=(s == NCH - 1))
                            for s in range(NCH):
                                ins = tensor.matmul(pu[it % 2][:, :],
                                                    lu[it % 2][:, s * 128:(s + 1) * 128],
                                                    xq[s][:, sl],
                                                    start=(s == 0), stop=(s == NCH - 1))
                            b.inc(ins, "te", 1)
                            it += 1

                @block.scalar
                def _(scalar):
                    it = 0
                    spills = []
                    for tb in range(NTB):
                        for ic in range(NI0):
                            spills.append((ic, tb))
                            scalar.wait_ge(b.sem("te"), te0 + it + 1)
                            if it >= 1:
                                scalar.wait_ge(b.sem("ve"), ve0 + it)
                            scalar.activation(out=gs[:], in_=pg[it % 2][:, :],
                                              func=AF.Silu, scale=sgo[:, ic:ic + 1])
                            scalar.activation(out=us[:], in_=pu[it % 2][:, :],
                                              func=AF.Copy, scale=suo[:, ic:ic + 1])
                            b.mark(scalar, b.tags, "sc", 2)
                            if it >= 1:
                                pic, ptb = spills[it - 1]
                                b.dma(scalar, h_spill[pic * 128:(pic + 1) * 128,
                                                      ptb * TB:(ptb + 1) * TB],
                                      ht[(it - 1) % 2][:], f"hsp{(it - 1) % 2}")
                            it += 1
                    scalar.wait_ge(b.sem("ve"), ve0 + it)
                    pic, ptb = spills[it - 1]
                    b.dma(scalar, h_spill[pic * 128:(pic + 1) * 128,
                                          ptb * TB:(ptb + 1) * TB],
                          ht[(it - 1) % 2][:], f"hsp{(it - 1) % 2}")
                    b.wait(scalar, "hsp0")
                    b.wait(scalar, "hsp1")

                @block.vector
                def _(vector):
                    it = 0
                    for tb in range(NTB):
                        for ic in range(NI0):
                            vector.wait_ge(b.sem("sc"), sc0 + 2 * (it + 1))
                            if it >= 2:
                                vector.wait_ge(b.sem(f"hsp{it % 2}"),
                                               hs_par0[it % 2] + 16 * ((it - 2) // 2 + 1))
                            vector.tensor_mul(out=ht[it % 2][:], in0=gs[:], in1=us[:])
                            vector.drain()
                            vector.tensor_reduce(out=K["col1"][:], in_=ht[it % 2][:],
                                                 axis=AX.X, op=ALU.max,
                                                 apply_absolute_value=True)
                            vector.drain()
                            if it == 0:
                                vector.tensor_copy(out=hmax[:], in_=K["col1"][:])
                            else:
                                vector.tensor_max(out=hmax[:], in0=hmax[:],
                                                  in1=K["col1"][:])
                            vector.drain()
                            b.mark(vector, b.tagv, "ve", 1)
                            it += 1

        with contextlib.ExitStack() as sc3:
            emit_global_max(b, K, hmax, hgmax, CC["in"][CC["i"]], CC["out"][CC["i"]],
                            n_cores, sc3)
            CC["i"] += 1
            emit_scale_rows(b, K, s_clip[:, 1:2], hgmax, hinv_col, sh_col, sc3)
        with nc.Block() as block:
            @block.vector
            def _(vector):
                for hc in range(NCH):
                    vector.tensor_mul(out=sdo[:, hc:hc + 1], in0=sd_sc[:, hc:hc + 1],
                                      in1=sh_col[:])
                vector.drain()
                vector.tensor_scalar_mul(out=sdo[:], in0=sdo[:], scalar1=0.5)
                vector.drain()
                b.mark(vector, b.tagv, "ve", 1)

        with contextlib.ExitStack() as sc4:
            hb = [b.sb(sc4, "ff_hb", [128, TB], F32) for _ in range(2)]
            qh = [b.sb(sc4, "ff_qh", [128, TB], BF16) for _ in range(2)]
            ld = [b.sb(sc4, "ff_ld", [128, D], BF16) for _ in range(2)]
            pd = [b.ps(sc4, "ff_pd", [128, TB]) for _ in range(NCH)]
            Q2 = {"a": b.sb(sc4, "ffq2_a", [128, TB], F32),
                  "i8": b.sb(sc4, "ffq2_i8", [128, TB], I8),
                  "c1f": b.sb(sc4, "ffq2_c1f", [128, TB], F32),
                  "sgn": b.sb(sc4, "ffq2_sgn", [128, TB], F32),
                  "t1": b.sb(sc4, "ffq2_t1", [128, TB], F32)}

            for tb in range(NTB):
                sl = slice(tb * TB, (tb + 1) * TB)
                for ic in range(NI0):
                    it = ic
                    with nc.Block() as block:
                        @block.sync
                        def _(sync):
                            b.dma(sync, hb[it % 2][:],
                                  h_spill[ic * 128:(ic + 1) * 128, sl], "dma")
                            b.dma(sync, ld[it % 2][:],
                                  qdT[ic * 128:(ic + 1) * 128, :], "dma")

                        @block.vector
                        def _(vector):
                            b.wait(vector, "dma")
                            b.mark(vector, b.tagv, "ve", 1)
                    emit_quant_away(b, hb[it % 2][:], qh[it % 2][:], TB, Q2, hinv_col)
                    with nc.Block() as block:
                        @block.tensor
                        def _(tensor):
                            for hc in range(NCH):
                                ins = tensor.matmul(pd[hc][:, :],
                                                    ld[it % 2][:, hc * 128:(hc + 1) * 128],
                                                    qh[it % 2][:],
                                                    start=(ic == 0), stop=(ic == NI0 - 1))
                            b.inc(ins, "te", 1)
                with nc.Block() as block:
                    @block.vector
                    def _(vector):
                        for hc in range(NCH):
                            vector.scalar_tensor_tensor(out=X[hc][:, sl], in0=pd[hc][:, :],
                                                        scalar=sdo[:, hc:hc + 1],
                                                        in1=X[hc][:, sl],
                                                        op0=ALU.mult, op1=ALU.add)
                            vector.drain()
                        b.mark(vector, b.tagv, "ve", 1)



def emit_l0_attn(b, K, X, WrT, WfT, att_r_col_h, gamma_col_h, beta_col_h, CC):
    nc = b.nc
    emit_stats(b, X, K, "ln")
    with contextlib.ExitStack() as osc:
        rbs = b.sb(osc, "at_rbs", [128, T], F32)
        mbs = b.sb(osc, "at_mbs", [128, T], F32)
        emit_bcast_full(b, K, K["R"]["r"], rbs)
        emit_bcast_full(b, K, K["R"]["mur"], mbs)
        fr = [b.sb(osc, "at_fr", [128, T], BF16) for _ in range(NCH)]
        wbuf = [b.sb(osc, "at_w", [128, D], BF16) for _ in range(NCH)]
        attr = b.sb(osc, "at_attr", [128, 1], F32)
        gcol = b.sb(osc, "at_g", [128, NCH], F32)
        bcol = b.sb(osc, "at_b", [128, NCH], F32)
        srows = b.sb(osc, "at_sr", [8, T], F32)
        prows = b.sb(osc, "at_p", [8, T], BF16)
        ctx_sb = b.sb(osc, "at_ctx", [128, NCH], BF16)
        fh_sb = b.sb(osc, "at_fh", [128, NCH], F32)
        muy, vy, ry = K["R"]["mu"], K["R"]["msq"], K["R"]["r"]
        sc11 = b.sb(osc, "at_s11", [1, 4], F32)

        with nc.Block() as block:
            @block.gpsimd
            def _(gpsimd):
                for s in range(NCH):
                    b.dma(gpsimd, wbuf[s][:], WrT[s * 128:(s + 1) * 128, :], f"gld{s % 2}")

            @block.sync
            def _(sync):
                b.dma(sync, attr[:], att_r_col_h, "dma")
                b.dma(sync, gcol[:], gamma_col_h, "dma")
                b.dma(sync, bcol[:], beta_col_h, "dma")
                b.wait(sync, "dma")

        with contextlib.ExitStack() as sc2:
            hbf = [b.sb(sc2, "at_hbf", [128, TB], BF16) for _ in range(NCH)]
            frp = [b.ps(sc2, "at_frp", [128, TB]) for _ in range(2)]
            (ve0, te0, sc0) = b.base("ve", "te", "sc")
            gl = [b.cnt["gld0"], b.cnt["gld1"]]
            with nc.Block() as block:
                @block.vector
                def _(vector):
                    for tb in range(NTB):
                        sl = slice(tb * TB, (tb + 1) * TB)
                        if tb >= 1:
                            vector.wait_ge(b.sem("te"), te0 + tb * NCH)
                        for c in range(NCH):
                            vector.tensor_mul(out=K["ftmp"][:, 0:TB], in0=X[c][:, sl],
                                              in1=rbs[:, sl])
                            vector.drain()
                            vector.tensor_sub(out=hbf[c][:], in0=K["ftmp"][:, 0:TB],
                                              in1=mbs[:, sl])
                            vector.drain()
                        b.mark(vector, b.tagv, "ve", 1)

                @block.tensor
                def _(tensor):
                    tensor.wait_ge(b.sem("gld0"), gl[0])
                    tensor.wait_ge(b.sem("gld1"), gl[1])
                    for tb in range(NTB):
                        tensor.wait_ge(b.sem("ve"), ve0 + tb + 1)
                        for oc in range(NCH):
                            it = tb * NCH + oc
                            if it >= 2:
                                tensor.wait_ge(b.sem("sc"), sc0 + it - 1)
                            for s in range(NCH):
                                ins = tensor.matmul(frp[it % 2][:, :],
                                                    wbuf[s][:, oc * 128:(oc + 1) * 128],
                                                    hbf[s][:],
                                                    start=(s == 0), stop=(s == NCH - 1))
                            b.inc(ins, "te", 1)

                @block.scalar
                def _(scalar):
                    for tb in range(NTB):
                        for oc in range(NCH):
                            it = tb * NCH + oc
                            scalar.wait_ge(b.sem("te"), te0 + it + 1)
                            scalar.copy(out=fr[oc][:, tb * TB:(tb + 1) * TB],
                                        in_=frp[it % 2][:, :])
                            b.mark(scalar, b.tags, "sc", 1)

        with contextlib.ExitStack() as sc3:
            lk = b.sb(sc3, "at_lk", [128, TB], F32)
            lk2 = b.sb(sc3, "at_lk2", [128, TB], F32)
            srstage = [b.sb(sc3, "at_srst", [1, TB], F32) for _ in range(2)]
            srp = b.ps(sc3, "at_srp", [1, TB])
            (ve0, te0) = b.base("ve", "te")
            for h in range(NCH):
                for tb in range(NTB):
                    with nc.Block() as block:
                        @block.vector
                        def _(vector):
                            sl = slice(tb * TB, (tb + 1) * TB)
                            vector.tensor_scalar_mul(out=lk2[:], in0=fr[h][:, sl],
                                                     scalar1=0.01)
                            vector.drain()
                            vector.tensor_max(out=lk[:], in0=fr[h][:, sl], in1=lk2[:])
                            vector.drain()
                            b.mark(vector, b.tagv, "ve", 1)

                    with nc.Block() as block:
                        @block.tensor
                        def _(tensor):
                            ins = tensor.matmul(srp[0:1, :], attr[:], lk[:],
                                                start=True, stop=True)
                            b.inc(ins, "te", 1)

                        @block.vector
                        def _(vector):
                            b.wait(vector, "te")
                            it_sr = h * NTB + tb
                            if it_sr >= 2:
                                b.wait(vector, f"srd{it_sr % 2}")
                            vector.tensor_copy(out=srstage[(h * NTB + tb) % 2][0:1, :], in_=srp[0:1, :])
                            vector.drain()
                            b.mark(vector, b.tagv, "ve", 1)

                        @block.scalar
                        def _(scalar):
                            b.wait(scalar, "ve")
                            b.dma(scalar, srows[h:h + 1, tb * TB:(tb + 1) * TB],
                                  srstage[(h * NTB + tb) % 2][0:1, :],
                                  f"srd{(h * NTB + tb) % 2}")

        (ve0s, sc0s) = b.base("ve", "sc")
        with nc.Block() as block:
            @block.vector
            def _(vector):
                b.wait(vector, "srd0")
                b.wait(vector, "srd1")
                vector.tensor_reduce(out=K["col8a"][0:8, :], in_=srows[:], axis=AX.X,
                                     op=ALU.max)
                vector.drain()
                vector.tensor_scalar_mul(out=K["col8a"][0:8, :], in0=K["col8a"][0:8, :],
                                         scalar1=-1.0)
                vector.drain()
                b.mark(vector, b.tagv, "ve", 1)
                vector.wait_ge(b.sem("sc"), sc0s + 1)
                vector.reciprocal(out=K["col8a"][0:8, :], in_=K["col8b"][0:8, :])
                vector.drain()
                vector.tensor_scalar_mul(out=prows[:], in0=srows[:],
                                         scalar1=K["col8a"][0:8, :])
                vector.drain()
                b.mark(vector, b.tagv, "ve", 1)

            @block.scalar
            def _(scalar):
                scalar.wait_ge(b.sem("ve"), ve0s + 1)
                scalar.activation(out=srows[:], in_=srows[:], func=AF.Exp,
                                  bias=K["col8a"][0:8, :], scale=1.0,
                                  accum_out=K["col8b"][0:8, :])
                scalar.drain()
                b.mark(scalar, b.tags, "sc", 1)

        with contextlib.ExitStack() as sc4:
            ptp = b.ps(sc4, "at_ptp", [128, 8], BF16)
            pT = b.sb(sc4, "at_pT", [128, 128], BF16)
            ftp = [b.ps(sc4, "at_ftp", [128, 128], BF16) for _ in range(2)]
            fts = [b.sb(sc4, "at_fts", [128, 128], BF16) for _ in range(2)]
            cxp = b.ps(sc4, "at_cxp", [128, 8])
            (te0, ve0) = b.base("te", "ve")
            with nc.Block() as block:
                @block.tensor
                def _(tensor):
                    for m in range(16):
                        if m >= 1:
                            tensor.wait_ge(b.sem("ve"), ve0 + m)
                        ins = tensor.transpose(ptp[:, :],
                                               prows[:, m * 128:(m + 1) * 128],
                                               K["ident"][0:8, 0:8])
                        b.inc(ins, "te", 1)

                @block.vector
                def _(vector):
                    for m in range(16):
                        vector.wait_ge(b.sem("te"), te0 + m + 1)
                        vector.tensor_copy(out=pT[:, m * 8:(m + 1) * 8], in_=ptp[:, :])
                        vector.drain()
                        b.mark(vector, b.tagv, "ve", 1)

            (te0, ve0, tt0, vt0) = b.base("te", "ve", "tt", "vt")
            with nc.Block() as block:
                @block.tensor
                def _(tensor):
                    it = 0
                    for h in range(NCH):
                        for m in range(16):
                            if it >= 2:
                                tensor.wait_ge(b.sem("vt"), vt0 + it - 1)
                            ins = tensor.transpose(ftp[it % 2][:, :],
                                                   fr[h][:, m * 128:(m + 1) * 128],
                                                   K["ident"][:])
                            b.inc(ins, "tt", 1)
                            tensor.wait_ge(b.sem("vt"), vt0 + it + 1)
                            ins = tensor.matmul(cxp[:, h:h + 1], fts[it % 2][:],
                                                pT[:, m * 8 + h:m * 8 + h + 1],
                                                start=(m == 0), stop=(m == 15))
                            b.inc(ins, "te", 1)
                            it += 1

                @block.vector
                def _(vector):
                    it = 0
                    for h in range(NCH):
                        for m in range(16):
                            vector.wait_ge(b.sem("tt"), tt0 + it + 1)
                            if it >= 2:
                                vector.wait_ge(b.sem("te"), te0 + it - 1)
                            ins = vector.tensor_copy(out=fts[it % 2][:], in_=ftp[it % 2][:])
                            b.inc(ins, "vt", 1)
                            vector.drain()
                            it += 1
                    vector.wait_ge(b.sem("te"), te0 + it)
                    vector.tensor_copy(out=ctx_sb[:], in_=cxp[:, :])
                    vector.drain()
                    b.mark(vector, b.tagv, "ve", 1)


        with contextlib.ExitStack() as sc5:
            fhp = b.ps(sc5, "at_fhp", [128, 8])
            gl = None
            with nc.Block() as block:
                @block.gpsimd
                def _(gpsimd):
                    for s in range(NCH):
                        b.dma(gpsimd, wbuf[s][:], WfT[s * 128:(s + 1) * 128, :],
                              f"gld{s % 2}")

            with nc.Block() as block:
                @block.tensor
                def _(tensor):
                    b.wait(tensor, "gld0")
                    b.wait(tensor, "gld1")
                    for oc in range(NCH):
                        for ic in range(NCH):
                            ins = tensor.matmul(fhp[:, oc:oc + 1],
                                                wbuf[ic][:, oc * 128:(oc + 1) * 128],
                                                ctx_sb[:, ic:ic + 1],
                                                start=(ic == 0), stop=(ic == NCH - 1))
                    b.inc(ins, "te", 1)

                @block.vector
                def _(vector):
                    b.wait(vector, "te")
                    vector.tensor_copy(out=fh_sb[:], in_=fhp[:, :])
                    vector.drain()
                    b.mark(vector, b.tagv, "ve", 1)

        with contextlib.ExitStack() as sc6:
            hsq = b.sb(sc6, "at_hsq", [128, TB], F32)
            htmp = b.sb(sc6, "at_htmp", [128, TB], F32)
            fhbits = b.sb(sc6, "at_fhb", [128, 2], F32)
            fb2 = b.sb(sc6, "at_fb2", [2, 128], F32)
            fb1 = b.sb(sc6, "at_fb1", [2, 1], F32)
            p_mu = [b.ps(sc6, "at_pmu", [1, TB])]
            p_sq = [b.ps(sc6, "at_psq", [1, TB])]
            p_hf = [b.ps(sc6, "at_phf", [1, TB])]
            tp2 = b.ps(sc6, "at_tp2", [128, 128])
            (ve0, te0, sc0) = b.base("ve", "te", "sc")
            with nc.Block() as block:
                @block.vector
                def _(vector):
                    vector.tensor_reduce(out=fhbits[:, 0:1], in_=fh_sb[:], axis=AX.X,
                                         op=ALU.add)
                    vector.drain()
                    vector.tensor_mul(out=K["ftmp"][:, 0:NCH], in0=fh_sb[:], in1=fh_sb[:])
                    vector.drain()
                    vector.tensor_reduce(out=fhbits[:, 1:2], in_=K["ftmp"][:, 0:NCH],
                                         axis=AX.X, op=ALU.add)
                    vector.drain()
                    b.mark(vector, b.tagv, "ve", 1)

                @block.tensor
                def _(tensor):
                    b.wait(tensor, "ve")
                    ins = tensor.transpose(tp2[0:2, 0:128], fhbits[:, 0:2],
                                           K["ident32"][:])
                    b.inc(ins, "te", 1)

                @block.vector
                def _(vector):
                    b.wait(vector, "te")
                    vector.tensor_copy(out=fb2[:], in_=tp2[0:2, 0:128])
                    vector.drain()
                    vector.tensor_reduce(out=fb1[:], in_=fb2[:], axis=AX.X, op=ALU.add)
                    vector.drain()
                    b.mark(vector, b.tagv, "ve", 1)

            with nc.Block() as block:
                @block.tensor
                def _(tensor):
                    ins = tensor.transpose(tp2[0:1, 0:2], fb1[:], K["ident32"][0:2, 0:2])
                    b.inc(ins, "te", 1)

                @block.vector
                def _(vector):
                    b.wait(vector, "te")
                    vector.tensor_copy(out=sc11[0:1, 0:2], in_=tp2[0:1, 0:2])
                    vector.drain()
                    vector.tensor_scalar_mul(out=sc11[0:1, 0:2], in0=sc11[0:1, 0:2],
                                             scalar1=1.0 / 1024.0)
                    vector.drain()
                    b.mark(vector, b.tagv, "ve", 1)

            for tb in range(NTB):
                sl = slice(tb * TB, (tb + 1) * TB)
                for c in range(NCH):
                    with nc.Block() as block:
                        @block.vector
                        def _(vector):
                            vector.tensor_mul(out=htmp[:, 0:TB], in0=X[c][:, sl],
                                              in1=rbs[:, sl])
                            vector.drain()
                            vector.tensor_sub(out=htmp[:, 0:TB], in0=htmp[:, 0:TB],
                                              in1=mbs[:, sl])
                            vector.drain()
                            b.mark(vector, b.tagv, "ve", 1)

                        @block.scalar
                        def _(scalar):
                            b.wait(scalar, "ve")
                            scalar.activation(out=hsq[:, 0:TB], in_=htmp[:, 0:TB],
                                              func=AF.Square)
                            scalar.drain()
                            b.mark(scalar, b.tags, "sc", 1)

                    with nc.Block() as block:
                        @block.tensor
                        def _(tensor):
                            tensor.matmul(p_mu[0][0:1, :], K["onesm32"][:],
                                          htmp[:, 0:TB],
                                          start=(c == 0), stop=(c == NCH - 1))
                            tensor.matmul(p_sq[0][0:1, :], K["onesm32"][:],
                                          hsq[:, 0:TB],
                                          start=(c == 0), stop=(c == NCH - 1))
                            ins = tensor.matmul(p_hf[0][0:1, :], fh_sb[:, c:c + 1],
                                                htmp[:, 0:TB],
                                                start=(c == 0), stop=(c == NCH - 1))
                            b.inc(ins, "te", 1)
                with nc.Block() as block:
                    @block.vector
                    def _(vector):
                        vector.tensor_copy(out=muy[0:1, sl], in_=p_mu[0][0:1, :])
                        vector.tensor_copy(out=vy[0:1, sl], in_=p_sq[0][0:1, :])
                        vector.tensor_copy(out=ry[0:1, sl], in_=p_hf[0][0:1, :])
                        vector.drain()
                        b.mark(vector, b.tagv, "ve", 1)

            with nc.Block() as block:
                @block.vector
                def _(vector):
                    vector.tensor_single_scalar(out=muy[0:1, :], in_=muy[0:1, :],
                                                scalar=sc11[0:1, 0:1], op=ALU.add)
                    vector.drain()
                    vector.tensor_scalar_mul(out=ry[0:1, :], in0=ry[0:1, :],
                                             scalar1=2.0 / 1024.0)
                    vector.drain()
                    vector.tensor_add(out=vy[0:1, :], in0=vy[0:1, :], in1=ry[0:1, :])
                    vector.drain()
                    vector.tensor_single_scalar(out=vy[0:1, :], in_=vy[0:1, :],
                                                scalar=sc11[0:1, 1:2], op=ALU.add)
                    vector.drain()
                    vector.tensor_mul(out=ry[0:1, :], in0=muy[0:1, :], in1=muy[0:1, :])
                    vector.drain()
                    vector.tensor_sub(out=vy[0:1, :], in0=vy[0:1, :], in1=ry[0:1, :])
                    vector.drain()
                    b.mark(vector, b.tagv, "ve", 1)

                @block.scalar
                def _(scalar):
                    b.wait(scalar, "ve")
                    scalar.activation(out=vy[0:1, :], in_=vy[0:1, :], func=AF.Sqrt,
                                      bias=K["eps11"][:], scale=1.0)
                    scalar.drain()
                    b.mark(scalar, b.tags, "sc", 1)

                @block.vector
                def _(vector):
                    b.wait(vector, "sc")
                    vector.reciprocal(out=ry[0:1, :], in_=vy[0:1, :])
                    vector.drain()
                    b.mark(vector, b.tagv, "ve", 1)

        with contextlib.ExitStack() as sc7:
            pmy = [b.ps(sc7, "at_pmy", [128, TB]) for _ in range(2)]
            pry = [b.ps(sc7, "at_pry", [128, TB]) for _ in range(2)]
            rysb = [b.sb(sc7, "at_rysb", [128, TB], F32) for _ in range(2)]
            v1 = b.sb(sc7, "at_v1", [128, TB], F32)
            v2 = b.sb(sc7, "at_v2", [128, TB], F32)
            (te0, sc0, ve0) = b.base("te", "sc", "ve")
            with nc.Block() as block:
                @block.tensor
                def _(tensor):
                    for tb in range(NTB):
                        if tb >= 2:
                            tensor.wait_ge(b.sem("ve"), ve0 + tb - 1)
                        tensor.matmul(pmy[tb % 2][:, :], K["ones_row32"][:],
                                      muy[0:1, tb * TB:(tb + 1) * TB],
                                      start=True, stop=True)
                        ins = tensor.matmul(pry[tb % 2][:, :], K["ones_row32"][:],
                                            ry[0:1, tb * TB:(tb + 1) * TB],
                                            start=True, stop=True)
                        b.inc(ins, "te", 1)

                @block.scalar
                def _(scalar):
                    for tb in range(NTB):
                        scalar.wait_ge(b.sem("te"), te0 + tb + 1)
                        if tb >= 2:
                            scalar.wait_ge(b.sem("ve"), ve0 + tb - 1)
                        scalar.copy(out=rysb[tb % 2][:], in_=pry[tb % 2][:, :])
                        b.mark(scalar, b.tags, "sc", 1)

                @block.vector
                def _(vector):
                    for tb in range(NTB):
                        sl = slice(tb * TB, (tb + 1) * TB)
                        vector.wait_ge(b.sem("sc"), sc0 + tb + 1)
                        for c in range(NCH):
                            vector.tensor_mul(out=v1[:], in0=X[c][:, sl], in1=rbs[:, sl])
                            vector.drain()
                            vector.tensor_sub(out=v1[:], in0=v1[:], in1=mbs[:, sl])
                            vector.drain()
                            vector.scalar_tensor_tensor(out=v2[:], in0=v1[:],
                                                        scalar=fh_sb[:, c:c + 1],
                                                        in1=pmy[tb % 2][:, :],
                                                        op0=ALU.add, op1=ALU.subtract)
                            vector.drain()
                            vector.tensor_mul(out=v2[:], in0=v2[:], in1=rysb[tb % 2][:])
                            vector.drain()
                            vector.scalar_tensor_tensor(out=v2[:], in0=v2[:],
                                                        scalar=gcol[:, c:c + 1],
                                                        in1=X[c][:, sl],
                                                        op0=ALU.mult, op1=ALU.add)
                            vector.drain()
                            vector.tensor_single_scalar(out=X[c][:, sl], in_=v2[:],
                                                        scalar=bcol[:, c:c + 1],
                                                        op=ALU.add)
                            vector.drain()
                        b.mark(vector, b.tagv, "ve", 1)


def build_program(n_cores=8, stages=1):
    nc = bass.Bass("TRN2", target_bir_lowering=False, debug=False)

    def hin(name, shape, dt=F32):
        return nc.dram_tensor(name, shape, dt, kind="ExternalInput").ap()

    def hscr(name, shape, dt):
        return nc.dram_tensor(name, shape, dt, kind="Internal").ap()

    x_in = hin("x_fm", [D, T])
    out_fm = nc.dram_tensor("out_fm", [D, T], F32, kind="ExternalOutput").ap()
    ident = hin("ident", [128, 128], BF16)
    ident32 = hin("ident32", [128, 128], F32)
    onesm = hin("onesm", [128, 1], BF16)
    ones_row32 = hin("ones_row32", [1, 128])
    w_gate = hin("w_gate", [I0, D])
    w_up = hin("w_up", [I0, D])
    w_down = hin("w_down", [D, I0])
    clip2 = hin("clip2", [1, 2])
    WrT = hin("WrT", [D, D])
    WfT = hin("WfT", [D, D])
    att_r_col = hin("att_r_col", [128, 1])
    gamma_col = hin("gamma_col", [128, NCH])
    beta_col = hin("beta_col", [128, NCH])
    onesm32 = hin("onesm32", [128, 1])
    W1 = {
        "W_kvd": hin("W_kvd", [D, 256]),
        "W_qd": hin("W_qd", [D, 256]),
        "W_qcomb": hin("W_qcomb", [256, D]),
        "W_ku": hin("W_ku", [256, D // 2]),
        "W_ropek": hin("W_ropek", [D, D // 2]),
        "W_vu": hin("W_vu", [256, D]),
        "W_o": hin("W_o", [D, D]),
        "Wg_s": hin("Wg_s", [D, D]),
        "Wu_s": hin("Wu_s", [D, D]),
        "Wd_s": hin("Wd_s", [D, D]),
        "W_router": hin("W_router", [D, 7]),
        "rbias": hin("rbias", [7, 1]),
        "cos_t": hin("cos_t", [128, T]),
        "sin_t": hin("sin_t", [128, T]),
        "perm": hin("perm", [128, 128], BF16),
        "w1col": hin("w1col", [128, NCH]),
        "w2col": hin("w2col", [128, NCH]),
    }
    _wgr = hin("Wg_r", [7 * D, D])
    _wur = hin("Wu_r", [7 * D, D])
    _wdr = hin("Wd_r", [7 * D, D])
    W1["Wg_r"] = [_wgr[e * D:(e + 1) * D, :] for e in range(7)]
    W1["Wu_r"] = [_wur[e * D:(e + 1) * D, :] for e in range(7)]
    W1["Wd_r"] = [_wdr[e * D:(e + 1) * D, :] for e in range(7)]
    W1["cmask"] = [hin(f"cmask{i}", [128, 2 * TB], BF16) for i in range(4)]
    W2 = {
        "wqT": hin("wqT", [D, D]),
        "wkT": hin("wkT", [D, D]),
        "wvT": hin("wvT", [D, D]),
        "woT": hin("woT", [D, D]),
        "bq": hin("bq_c", [128, NCH]),
        "bk": hin("bk_c", [128, NCH]),
        "bo": hin("bo_c", [128, NCH]),
        "bv_row": hin("bv_row", [1, D]),
    }

    h_spill = hscr("h_spill", [I0, T], F32)
    qgT = hscr("qgT", [D, I0], BF16)
    quT = hscr("quT", [D, I0], BF16)
    qdT = hscr("qdT", [I0, D], BF16)
    CC = {"in": [hscr(f"cc_in{i}", [1, 1], F32) for i in range(4)],
          "out": [hscr(f"cc_out{i}", [1, 1], F32) for i in range(4)],
          "i": 0}
    HS = {"zbf": hscr("zbf_spill", [D, T], BF16),
          "ctx": hscr("ctx_spill", [D, T], BF16),
          "wts": hscr("wts_spill", [7, T], F32)}

    es = contextlib.ExitStack()
    with es:
        b = B(nc, es)
        b.tagv = b.sb(es, "tagv", [1, 1], F32)
        b.tags = b.sb(es, "tags", [1, 1], F32)
        b.tagg = b.sb(es, "tagg", [1, 1], F32)
        X = [b.sb(es, "x", [128, T], F32) for _ in range(NCH)]
        K = {"ident": b.sb(es, "ident", [128, 128], BF16),
             "ident32": b.sb(es, "ident32", [128, 128], F32),
             "onesm": b.sb(es, "onesm", [128, 1], BF16),
             "ones_row32": b.sb(es, "onesr32", [1, 128], F32),
             "R": {nm: b.sb(es, "row_" + nm, [1, T], F32)
                   for nm in ["mu", "msq", "r", "mur"]},
             "col1": b.sb(es, "col1", [128, 1], F32),
             "col8a": b.sb(es, "col8a", [8, 1], F32),
             "col8b": b.sb(es, "col8b", [8, 1], F32),
             "ftmp": b.sb(es, "ftmp", [128, TB], F32),
             "onesm32": b.sb(es, "onesm32_t", [128, 1], F32),
             "eps11": b.sb(es, "eps11", [1, 1], F32)}
        K["ftmp32"] = K["ftmp"]
        s_clip = b.sb(es, "clip", [1, 2], F32)
        sg_sc = b.sb(es, "sg_sc", [128, NI0], F32)
        su_sc = b.sb(es, "su_sc", [128, NI0], F32)
        sd_sc = b.sb(es, "sd_sc", [128, NCH], F32)

        with nc.Block() as block:
            @block.sync
            def _(sync):
                for c in range(NCH):
                    b.dma(sync, X[c][:], x_in[c * 128:(c + 1) * 128, :], "dma")
                b.dma(sync, K["ident"][:], ident, "dma")
                b.dma(sync, K["ident32"][:], ident32, "dma")
                b.dma(sync, K["onesm"][:], onesm, "dma")
                b.dma(sync, K["ones_row32"][:], ones_row32, "dma")
                b.dma(sync, K["onesm32"][:], onesm32, "dma")
                b.dma(sync, s_clip[:], clip2, "dma")
                b.wait(sync, "dma")

            @block.vector
            def _(vector):
                vector.memset(K["eps11"][:], EPS)
                b.mark(vector, b.tagv, "ve", 1)

        emit_l0_weight_quant(b, K, w_gate, w_up, w_down, qgT, quT, qdT,
                             sg_sc, su_sc, sd_sc)
        emit_l0_ffn(b, K, X, s_clip, qgT, quT, qdT, sg_sc, su_sc, sd_sc,
                    h_spill, CC, n_cores)
        if stages >= 2:
            emit_l0_attn(b, K, X, WrT, WfT, att_r_col, gamma_col, beta_col, CC)
        if stages >= 3:
            emit_l0_ffn(b, K, X, s_clip, qgT, quT, qdT, sg_sc, su_sc, sd_sc,
                        h_spill, CC, n_cores)
        if stages >= 4:
            w1sb = b.sb(es, "w1sb", [128, NCH], F32)
            w2sb = b.sb(es, "w2sb", [128, NCH], F32)
            with nc.Block() as block:
                @block.sync
                def _(sync):
                    b.dma(sync, w1sb[:], W1["w1col"], "dma")
                    b.dma(sync, w2sb[:], W1["w2col"], "dma")
                    b.wait(sync, "dma")
            W1["w1col_sb"] = w1sb
            W1["w2col_sb"] = w2sb
            emit_l1_ffn(b, K, X, W1, CC, HS, n_cores)
        if stages >= 5:
            with contextlib.ExitStack() as s5:
                bv_sb = b.sb(s5, "bv_sb", [1, D], F32)
                with nc.Block() as block:
                    @block.sync
                    def _(sync):
                        b.dma(sync, bv_sb[:], W2["bv_row"], "dma")
                        b.wait(sync, "dma")
                W2["bv_sb"] = bv_sb
                emit_l1_attn(b, K, X, W2, HS)
        if stages >= 6:
            emit_l1_ffn(b, K, X, W1, CC, HS, n_cores)

        with nc.Block() as block:
            @block.sync
            def _(sync):
                for c in range(NCH):
                    b.dma(sync, out_fm[c * 128:(c + 1) * 128, :], X[c][:], "sdma")
                b.wait(sync, "sdma")
    return nc


def make_inputs(d, core):
    x = np.ascontiguousarray(np.asarray(d["L0_attn_h"])[core].T)
    return {
        "x_fm": x,
        "ident": _bf16(np.eye(128, dtype=np.float32)),
        "ident32": np.eye(128, dtype=np.float32),
        "onesm": _bf16(np.full((128, 1), 1.0 / 1024.0, np.float32)),
        "ones_row32": np.ones((1, 128), np.float32),
        "w_gate": np.asarray(d["L0_ffn_w_gate"]),
        "w_up": np.asarray(d["L0_ffn_w_up"]),
        "w_down": np.asarray(d["L0_ffn_w_down"]),
        "clip2": np.stack([np.asarray(d["L0_ffn_clip_x"]),
                           np.asarray(d["L0_ffn_clip_down"])], 1).astype(np.float32),
        "WrT": np.ascontiguousarray(np.asarray(d["L0_attn_Wr"]).T),
        "WfT": np.ascontiguousarray(np.asarray(d["L0_attn_Wf"]).T),
        "att_r_col": np.ascontiguousarray(np.asarray(d["L0_attn_att_r"]).reshape(128, 1)),
        "gamma_col": np.ascontiguousarray(np.asarray(d["L0_attn_ln_gamma"]).reshape(NCH, 128).T),
        "beta_col": np.ascontiguousarray(np.asarray(d["L0_attn_ln_beta"]).reshape(NCH, 128).T),
        "onesm32": np.full((128, 1), 1.0 / 1024.0, np.float32),
        **_l1_inputs(d),
        "wqT": np.ascontiguousarray(np.asarray(d["L1_attn_wq_w"]).T),
        "wkT": np.ascontiguousarray(np.asarray(d["L1_attn_wk_w"]).T),
        "wvT": np.ascontiguousarray(np.asarray(d["L1_attn_wv_w"]).T),
        "woT": np.ascontiguousarray(np.asarray(d["L1_attn_wo_w"]).T),
        "bq_c": np.ascontiguousarray(np.asarray(d["L1_attn_wq_b"]).reshape(NCH, 128).T),
        "bk_c": np.ascontiguousarray(np.asarray(d["L1_attn_wk_b"]).reshape(NCH, 128).T),
        "bo_c": np.ascontiguousarray(np.asarray(d["L1_attn_wo_b"]).reshape(NCH, 128).T),
        "bv_row": np.ascontiguousarray(np.asarray(d["L1_attn_wv_b"]).reshape(1, D)),
    }


def _bf16(a):
    import ml_dtypes
    return a.astype(ml_dtypes.bfloat16)


def _l1_inputs(d):
    g = lambda n: np.asarray(d[n])
    wqu, wrq = g("L1_ffn_W_qu"), g("L1_ffn_W_ropeq")
    qcomb = np.empty((256, D), np.float32)
    for h in range(16):
        qcomb[:, 64 * h:64 * h + 32] = wqu[:, 32 * h:32 * h + 32]
        qcomb[:, 64 * h + 32:64 * h + 64] = wrq[:, 32 * h:32 * h + 32]
    cos_t = np.ones((128, T), np.float32)
    sin_t = np.zeros((128, T), np.float32)
    t = np.arange(T, dtype=np.float64)
    for r in range(128):
        r64 = r % 64
        if r64 >= 32:
            dd = r64 - 32
            invf = 1.0 / 10000.0 ** (2 * (dd % 16) / 32.0)
            cos_t[r] = np.cos(t * invf).astype(np.float32)
            sin_t[r] = np.sin(t * invf).astype(np.float32)
    perm = np.zeros((128, 128), np.float32)
    for base in (0, 64):
        for dd in range(32):
            m = base + 32 + dd
            if dd < 16:
                perm[base + 32 + dd + 16, m] = -1.0
            else:
                perm[base + 32 + dd - 16, m] = 1.0
    cms = []
    for i in range(4):
        k = np.arange(128)[:, None]
        q = np.arange(TB)[None, :]
        m = (k + 128 * i <= q).astype(np.float32)
        cms.append(_bf16(np.concatenate([m, m], axis=1)))
    return {
        "W_kvd": g("L1_ffn_W_kvd"), "W_qd": g("L1_ffn_W_qd"),
        "W_qcomb": qcomb, "W_ku": g("L1_ffn_W_ku"),
        "W_ropek": g("L1_ffn_W_ropek"), "W_vu": g("L1_ffn_W_vu"),
        "W_o": g("L1_ffn_W_o"),
        "Wg_s": g("L1_ffn_Wg_s"), "Wu_s": g("L1_ffn_Wu_s"), "Wd_s": g("L1_ffn_Wd_s"),
        "Wg_r": g("L1_ffn_Wg_r").reshape(7 * D, D),
        "Wu_r": g("L1_ffn_Wu_r").reshape(7 * D, D),
        "Wd_r": g("L1_ffn_Wd_r").reshape(7 * D, D),
        "W_router": g("L1_ffn_W_router"),
        "rbias": g("L1_ffn_routing_bias").reshape(7, 1).astype(np.float32),
        "cos_t": cos_t, "sin_t": sin_t, "perm": _bf16(perm),
        "w1col": np.ascontiguousarray(g("L1_ffn_w_ln1").reshape(NCH, 128).T).astype(np.float32),
        "w2col": np.ascontiguousarray(g("L1_ffn_w_ln2").reshape(NCH, 128).T).astype(np.float32),
        **{f"cmask{i}": cms[i] for i in range(4)},
    }



NT128 = T // 128


def _ln_tile(b, vector, X_c_slice, rbs_slice, mbs_slice, out_ap):
    vector.tensor_mul(out=out_ap, in0=X_c_slice, in1=rbs_slice)
    vector.drain()
    vector.tensor_sub(out=out_ap, in0=out_ap, in1=mbs_slice)
    vector.drain()


def _rms_rows(b, K, msq_src_row, out_r_row):
    nc = b.nc
    with nc.Block() as block:
        @block.scalar
        def _(scalar):
            scalar.activation(out=K["R"]["mur"][0:1, :], in_=msq_src_row,
                              func=AF.Sqrt)
            scalar.drain()
            b.mark(scalar, b.tags, "sc", 1)

        @block.vector
        def _(vector):
            b.wait(vector, "sc")
            vector.tensor_scalar_add(out=K["R"]["mur"][0:1, :],
                                     in0=K["R"]["mur"][0:1, :], scalar1=1e-5)
            vector.drain()
            vector.reciprocal(out=out_r_row, in_=K["R"]["mur"][0:1, :])
            vector.drain()
            b.mark(vector, b.tagv, "ve", 1)


def _bcast_col11(b, K, scope, src11, dst_col):
    nc = b.nc
    pbc = b.ps(scope, "bc11", [128, 1])
    with nc.Block() as block:
        @block.tensor
        def _(tensor):
            ins = tensor.matmul(pbc[:, 0:1], K["ones_row32"][:], src11, start=True,
                                stop=True)
            b.inc(ins, "te", 1)

        @block.vector
        def _(vector):
            b.wait(vector, "te")
            vector.tensor_copy(out=dst_col, in_=pbc[:, 0:1])
            vector.drain()
            b.mark(vector, b.tagv, "ve", 1)


def emit_l1_ffn(b, K, X, W, CC, HS, n_cores):
    nc = b.nc
    emit_stats(b, X, K, "ln")
    with contextlib.ExitStack() as osc:
        with contextlib.ExitStack() as asc:
            xn = [b.sb(asc, "l1_xn", [128, T], BF16) for _ in range(NCH)]
            kv = [b.sb(asc, "l1_kv", [128, T], BF16) for _ in range(2)]
            ql = [b.sb(asc, "l1_ql", [128, T], BF16) for _ in range(2)]
            with contextlib.ExitStack() as zsc:
                rbs = b.sb(zsc, "l1_rbs", [128, T], F32)
                mbs = b.sb(zsc, "l1_mbs", [128, T], F32)
                emit_bcast_full(b, K, K["R"]["r"], rbs)
                emit_bcast_full(b, K, K["R"]["mur"], mbs)
                r2row = K["R"]["mu"]
                _emit_z_passes(b, K, X, W, HS, rbs, mbs, r2row, xn, zsc)
            _emit_latents(b, K, W, xn, kv, ql, asc)
            _emit_mla_attention(b, K, W, HS, xn, kv, ql, asc)

        xn2 = [b.sb(osc, "l1_xn2", [128, T], BF16) for _ in range(NCH)]
        _emit_l1_after_attn(b, K, W, HS, X, xn2)
        _emit_moe(b, K, W, X, xn2, HS, osc)


def _emit_z_passes(b, K, X, W, HS, rbs, mbs, r2row, xn, asc):
    nc = b.nc
    with contextlib.ExitStack() as pa:
        zt = b.sb(pa, "z_zt", [128, T], F32)
        zsq32 = b.sb(pa, "z_zs32", [128, T], F32)
        zsq = [b.sb(pa, "z_zsq", [128, T], BF16)]
        ze2 = [b.sb(pa, "z_ze2", [128, T], BF16)]
        pzs = [b.ps(pa, "z_pzs", [1, TB]) for _ in range(NTB)]
        for c in range(NCH):
            (te0, sc0, ve0) = b.base("te", "sc", "ve")
            with nc.Block() as block:
                @block.vector
                def _(vector):
                    _ln_tile(b, vector, X[c][:], rbs[:], mbs[:], zt[:])
                    b.mark(vector, b.tagv, "ve", 1)

                @block.scalar
                def _(scalar):
                    scalar.wait_ge(b.sem("ve"), ve0 + 1)
                    scalar.activation(out=zsq32[:], in_=zt[:], func=AF.Square)
                    scalar.drain()
                    scalar.copy(out=zsq[0][:], in_=zsq32[:])
                    b.mark(scalar, b.tags, "sc", 1)
            with nc.Block() as block:
                @block.vector
                def _(vector):
                    vector.tensor_sub(out=ze2[0][:], in0=zsq32[:], in1=zsq[0][:])
                    vector.drain()
                    b.mark(vector, b.tagv, "ve", 1)
            with nc.Block() as block:
                @block.tensor
                def _(tensor):
                    for tb in range(NTB):
                        sl = slice(tb * TB, (tb + 1) * TB)
                        tensor.matmul(pzs[tb][0:1, :], K["onesm"][:], zsq[0][:, sl],
                                      start=(c == 0), stop=False)
                        ins = tensor.matmul(pzs[tb][0:1, :], K["onesm"][:],
                                            ze2[0][:, sl],
                                            start=False, stop=(c == NCH - 1))
                    b.inc(ins, "te", 1)
        with nc.Block() as block:
            @block.vector
            def _(vector):
                for tb in range(NTB):
                    vector.tensor_copy(out=K["R"]["msq"][0:1, tb * TB:(tb + 1) * TB],
                                       in_=pzs[tb][0:1, :])
                vector.drain()
                b.mark(vector, b.tagv, "ve", 1)
        _rms_rows(b, K, K["R"]["msq"][0:1, :], r2row[0:1, :])

    with contextlib.ExitStack() as pb:
        r2b = b.sb(pb, "z_r2b", [128, T], F32)
        emit_bcast_full(b, K, r2row, r2b)
        zt = b.sb(pb, "z_ztB", [128, T], F32)
        zbf = [b.sb(pb, "z_zbf", [128, T], BF16) for _ in range(2)]
        zs0 = [b.base("zsp0")[0], b.base("zsp1")[0]]
        (ve0,) = b.base("ve")
        with nc.Block() as block:
            @block.vector
            def _(vector):
                for c in range(NCH):
                    _ln_tile(b, vector, X[c][:], rbs[:], mbs[:], zt[:])
                    if c >= 2:
                        b.wait(vector, f"zsp{c % 2}", zs0[c % 2] + 16 * ((c - 2) // 2 + 1))
                    vector.tensor_copy(out=zbf[c % 2][:], in_=zt[:])
                    vector.drain()
                    vector.scalar_tensor_tensor(out=xn[c][:], in0=zt[:],
                                                scalar=W["w1col_sb"][:, c:c + 1],
                                                in1=r2b[:], op0=ALU.mult, op1=ALU.mult)
                    vector.drain()
                    b.mark(vector, b.tagv, "ve", 1)

            @block.scalar
            def _(scalar):
                for c in range(NCH):
                    scalar.wait_ge(b.sem("ve"), ve0 + c + 1)
                    b.dma(scalar, HS["zbf"][c * 128:(c + 1) * 128, :], zbf[c % 2][:],
                          f"zsp{c % 2}")
                b.wait(scalar, "zsp0")
                b.wait(scalar, "zsp1")


def _emit_latents(b, K, W, xn, kv, ql, asc):
    nc = b.nc
    with contextlib.ExitStack() as pj:
        wkd = [b.sb(pj, "lt_wkd", [128, 256], BF16) for _ in range(NCH)]
        wqd = [b.sb(pj, "lt_wqd", [128, 256], BF16) for _ in range(NCH)]
        pp = [b.ps(pj, "lt_pp", [128, TB]) for _ in range(2)]
        with nc.Block() as block:
            @block.gpsimd
            def _(gpsimd):
                for s in range(NCH):
                    b.dma(gpsimd, wkd[s][:], W["W_kvd"][s * 128:(s + 1) * 128, :],
                          f"gld{s % 2}")
                    b.dma(gpsimd, wqd[s][:], W["W_qd"][s * 128:(s + 1) * 128, :],
                          f"gld{s % 2}")
        gl = [b.cnt["gld0"], b.cnt["gld1"]]
        (te0, sc0) = b.base("te", "sc")
        with nc.Block() as block:
            @block.tensor
            def _(tensor):
                tensor.wait_ge(b.sem("gld0"), gl[0])
                tensor.wait_ge(b.sem("gld1"), gl[1])
                it = 0
                for (dst, wsrc) in [(kv, wkd), (ql, wqd)]:
                    for oc in range(2):
                        for tb in range(NTB):
                            if it >= 2:
                                tensor.wait_ge(b.sem("sc"), sc0 + it - 1)
                            for s in range(NCH):
                                ins = tensor.matmul(
                                    pp[it % 2][:, :],
                                    wsrc[s][:, oc * 128:(oc + 1) * 128],
                                    xn[s][:, tb * TB:(tb + 1) * TB],
                                    start=(s == 0), stop=(s == NCH - 1))
                            b.inc(ins, "te", 1)
                            it += 1

            @block.scalar
            def _(scalar):
                it = 0
                for (dst, wsrc) in [(kv, wkd), (ql, wqd)]:
                    for oc in range(2):
                        for tb in range(NTB):
                            scalar.wait_ge(b.sem("te"), te0 + it + 1)
                            scalar.copy(out=dst[oc][:, tb * TB:(tb + 1) * TB],
                                        in_=pp[it % 2][:, :])
                            b.mark(scalar, b.tags, "sc", 1)
                            it += 1


def _emit_mla_attention(b, K, W, HS, xn, kv, ql, asc):
    nc = b.nc
    with contextlib.ExitStack() as pl:
        cosb = b.sb(pl, "ml_cos", [128, T], F32)
        sinb = b.sb(pl, "ml_sin", [128, T], F32)
        perm = b.sb(pl, "ml_perm", [128, 128], BF16)
        cmask = [b.sb(pl, "ml_cm", [128, 2 * TB], BF16) for _ in range(4)]
        qq = b.sb(pl, "ml_qq", [128, T], BF16)
        kk = b.sb(pl, "ml_kk", [128, T], BF16)
        kkraw = b.sb(pl, "ml_kkr", [128, T], BF16)
        vpair = [b.sb(pl, "ml_vp", [128, 130], BF16) for _ in range(NT128)]
        PT = [b.sb(pl, "ml_pt", [128, 2 * TB], BF16) for _ in range(2)]
        wq_p = b.sb(pl, "ml_wqp", [128, 2 * 128], BF16)
        wku_p = b.sb(pl, "ml_wkup", [128, 2 * 64], BF16)
        wrk_p = b.sb(pl, "ml_wrkp", [128, NCH * 64], BF16)
        wvu_p = b.sb(pl, "ml_wvup", [128, 2 * 128], BF16)
        qraw = b.sb(pl, "ml_qraw", [128, TB], BF16)
        sqb = b.sb(pl, "ml_sqb", [128, TB], BF16)
        n11 = b.sb(pl, "ml_n11", [1, 4], F32)
        bcol = b.sb(pl, "ml_bcol", [128, 1], F32)
        rr = b.sb(pl, "ml_rr", [1, TB], F32)
        rbx = b.sb(pl, "ml_rbx", [64, TB], F32)
        cxst = b.sb(pl, "ml_cxst", [128, TB], BF16)

        with nc.Block() as block:
            @block.sync
            def _(sync):
                b.dma(sync, cosb[:], W["cos_t"], "dma")
                b.dma(sync, sinb[:], W["sin_t"], "dma")
                for i in range(4):
                    b.dma(sync, cmask[i][:], W["cmask"][i], "dma")
                b.wait(sync, "dma")

            @block.gpsimd
            def _(gpsimd):
                b.dma(gpsimd, perm[:], W["perm"], "gld0")
                b.wait(gpsimd, "gld0")

        for p in range(NCH):
            _emit_mla_pair(b, K, W, HS, p, xn, kv, ql, cosb, sinb, perm, cmask,
                           qq, kk, kkraw, vpair, PT, wq_p, wku_p, wrk_p, wvu_p,
                           qraw, sqb, n11, bcol, rr, rbx, cxst)


def _emit_mla_pair(b, K, W, HS, p, xn, kv, ql, cosb, sinb, perm, cmask,
                   qq, kk, kkraw, vpair, PT, wq_p, wku_p, wrk_p, wvu_p,
                   qraw, sqb, n11, bcol, rr, rbx, cxst):
    nc = b.nc
    with contextlib.ExitStack() as sc_:
      with contextlib.ExitStack() as proj_sc:
        pqq = b.ps(proj_sc, "mp_pqq", [128, TB])
        prot = b.ps(proj_sc, "mp_rot", [128, TB])
        pk = b.ps(proj_sc, "mp_pk", [64, TB])
        pv = [b.ps(proj_sc, "mp_pv", [128, 128]) for _ in range(2)]
        pn = b.ps(proj_sc, "mp_pn", [1, TB])
        pb1 = b.ps(proj_sc, "mp_pb1", [128, 1])
        stg96 = b.sb(proj_sc, "mp_s96", [32, TB], BF16)

        with nc.Block() as block:
            @block.gpsimd
            def _(gpsimd):
                b.dma(gpsimd, wq_p[:].rearrange("p (s c) -> p s c", c=128),
                      W["W_qcomb"][:, p * 128:(p + 1) * 128]
                      .rearrange("(s p) c -> p s c", p=128), "gld0")
                b.dma(gpsimd, wku_p[:].rearrange("p (s c) -> p s c", c=64),
                      W["W_ku"][:, p * 64:(p + 1) * 64]
                      .rearrange("(s p) c -> p s c", p=128), "gld1")
                b.dma(gpsimd, wrk_p[:].rearrange("p (s c) -> p s c", c=64),
                      W["W_ropek"][:, p * 64:(p + 1) * 64]
                      .rearrange("(s p) c -> p s c", p=128), "gld0")
                b.dma(gpsimd, wvu_p[:].rearrange("p (s c) -> p s c", c=128),
                      W["W_vu"][:, p * 128:(p + 1) * 128]
                      .rearrange("(s p) c -> p s c", p=128), "gld1")
                b.wait(gpsimd, "gld0")
                b.wait(gpsimd, "gld1")

        for tb in range(NTB):
            sl = slice(tb * TB, (tb + 1) * TB)
            with nc.Block() as block:
                @block.tensor
                def _(tensor):
                    for s in range(2):
                        ins = tensor.matmul(pqq[:, :], wq_p[:, s * 128:(s + 1) * 128],
                                            ql[s][:, sl], start=(s == 0), stop=(s == 1))
                    b.inc(ins, "te", 1)

                @block.scalar
                def _(scalar):
                    b.wait(scalar, "te")
                    scalar.copy(out=qraw[:], in_=pqq[:, :])
                    b.mark(scalar, b.tags, "sc", 1)

            with nc.Block() as block:
                @block.tensor
                def _(tensor):
                    ins = tensor.matmul(prot[:, :], perm[:], qraw[:], start=True,
                                        stop=True)
                    b.inc(ins, "te", 1)

                @block.vector
                def _(vector):
                    b.wait(vector, "te")
                    vector.tensor_mul(out=K["ftmp"][:, 0:TB], in0=qraw[:], in1=cosb[:, sl])
                    vector.drain()
                    vector.scalar_tensor_tensor(out=qq[:, sl], in0=prot[:, :],
                                                scalar=1.0, in1=sinb[:, sl],
                                                op0=ALU.mult, op1=ALU.mult)
                    vector.drain()
                    vector.tensor_add(out=qq[:, sl], in0=qq[:, sl],
                                      in1=K["ftmp"][:, 0:TB])
                    vector.drain()
                    b.mark(vector, b.tagv, "ve", 1)

        for tb in range(NTB):
            sl = slice(tb * TB, (tb + 1) * TB)
            with nc.Block() as block:
                @block.tensor
                def _(tensor):
                    for s in range(2):
                        ins = tensor.matmul(pk[:, :], wku_p[:, s * 64:(s + 1) * 64],
                                            kv[s][:, sl], start=(s == 0), stop=(s == 1))
                    b.inc(ins, "te", 1)

                @block.scalar
                def _(scalar):
                    b.wait(scalar, "te")
                    scalar.copy(out=kkraw[0:32, sl], in_=pk[0:32, :])
                    scalar.copy(out=kkraw[64:96, sl], in_=pk[32:64, :])
                    b.mark(scalar, b.tags, "sc", 1)

            with nc.Block() as block:
                @block.tensor
                def _(tensor):
                    for s in range(NCH):
                        ins = tensor.matmul(pk[:, :], wrk_p[:, s * 64:(s + 1) * 64],
                                            xn[s][:, sl],
                                            start=(s == 0), stop=(s == NCH - 1))
                    b.inc(ins, "te", 1)

                @block.scalar
                def _(scalar):
                    b.wait(scalar, "te")
                    scalar.copy(out=kkraw[32:64, sl], in_=pk[0:32, :])
                    scalar.copy(out=stg96[:], in_=pk[32:64, :])
                    scalar.drain()
                    b.dma(scalar, kkraw[96:128, sl], stg96[:], f"kkd{tb % 2}")
                    b.mark(scalar, b.tags, "sc", 1)

            with nc.Block() as block:
                @block.tensor
                def _(tensor):
                    b.wait(tensor, "kkd0")
                    b.wait(tensor, "kkd1")
                    ins = tensor.matmul(prot[:, :], perm[:], kkraw[:, sl], start=True,
                                        stop=True)
                    b.inc(ins, "te", 1)

                @block.vector
                def _(vector):
                    b.wait(vector, "te")
                    vector.tensor_mul(out=K["ftmp"][:, 0:TB], in0=kkraw[:, sl],
                                      in1=cosb[:, sl])
                    vector.drain()
                    vector.scalar_tensor_tensor(out=kk[:, sl], in0=prot[:, :],
                                                scalar=1.0, in1=sinb[:, sl],
                                                op0=ALU.mult, op1=ALU.mult)
                    vector.drain()
                    vector.tensor_add(out=kk[:, sl], in0=kk[:, sl],
                                      in1=K["ftmp"][:, 0:TB])
                    vector.drain()
                    b.mark(vector, b.tagv, "ve", 1)

        (te0, ve0) = b.base("te", "ve")
        with nc.Block() as block:
            @block.tensor
            def _(tensor):
                for t in range(NT128):
                    if t >= 2:
                        tensor.wait_ge(b.sem("ve"), ve0 + t - 1)
                    for s in range(2):
                        ins = tensor.matmul(pv[t % 2][:, :],
                                            kv[s][:, t * 128:(t + 1) * 128],
                                            wvu_p[:, s * 128:(s + 1) * 128],
                                            start=(s == 0), stop=(s == 1))
                    b.inc(ins, "te", 1)

            @block.vector
            def _(vector):
                for t in range(NT128):
                    vector.wait_ge(b.sem("te"), te0 + t + 1)
                    vector.tensor_copy(out=vpair[t][:, 0:64], in_=pv[t % 2][:, 0:64])
                    vector.tensor_copy(out=vpair[t][:, 65:129], in_=pv[t % 2][:, 64:128])
                    vector.memset(vpair[t][:, 64:65], 1.0)
                    vector.memset(vpair[t][:, 129:130], 1.0)
                    vector.drain()
                    b.mark(vector, b.tagv, "ve", 1)

        with nc.Block() as block:
            @block.vector
            def _(vector):
                vector.memset(n11[0:1, 0:2], 0.0)
                vector.drain()
                b.mark(vector, b.tagv, "ve", 1)
        for (i, srcb) in [(0, qq), (1, kk)]:
            for tb in range(NTB):
                sl = slice(tb * TB, (tb + 1) * TB)
                with nc.Block() as block:
                    @block.vector
                    def _(vector):
                        vector.tensor_mul(out=sqb[:], in0=srcb[:, sl], in1=srcb[:, sl])
                        vector.drain()
                        b.mark(vector, b.tagv, "ve", 1)
                with nc.Block() as block:
                    @block.tensor
                    def _(tensor):
                        ins = tensor.matmul(pn[0:1, :], K["onesm"][:], sqb[:],
                                            start=True, stop=True)
                        b.inc(ins, "te", 1)

                    @block.vector
                    def _(vector):
                        b.wait(vector, "te")
                        vector.tensor_reduce(out=n11[0:1, 2:3], in_=pn[0:1, :],
                                             axis=AX.X, op=ALU.max)
                        vector.drain()
                        vector.tensor_max(out=n11[0:1, i:i + 1], in0=n11[0:1, i:i + 1],
                                          in1=n11[0:1, 2:3])
                        vector.drain()
                        b.mark(vector, b.tagv, "ve", 1)
        with nc.Block() as block:
            @block.scalar
            def _(scalar):
                scalar.activation(out=n11[0:1, 0:2], in_=n11[0:1, 0:2], func=AF.Sqrt)
                scalar.drain()
                b.mark(scalar, b.tags, "sc", 1)

            @block.vector
            def _(vector):
                b.wait(vector, "sc")
                vector.tensor_mul(out=n11[0:1, 3:4], in0=n11[0:1, 0:1],
                                  in1=n11[0:1, 1:2])
                vector.drain()
                vector.tensor_scalar_mul(out=n11[0:1, 3:4], in0=n11[0:1, 3:4],
                                         scalar1=-128.0)
                vector.drain()
                b.mark(vector, b.tagv, "ve", 1)
        with nc.Block() as block:
            @block.tensor
            def _(tensor):
                ins = tensor.matmul(pb1[:, 0:1], K["ones_row32"][:], n11[0:1, 3:4],
                                    start=True, stop=True)
                b.inc(ins, "te", 1)

            @block.vector
            def _(vector):
                b.wait(vector, "te")
                vector.tensor_copy(out=bcol[:], in_=pb1[:, 0:1])
                vector.drain()
                b.mark(vector, b.tagv, "ve", 1)

      for qb in range(NTB):
          _emit_pair_scores(b, K, W, HS, p, qb, qq, kk, vpair, PT, cmask, bcol,
                            rr, rbx, cxst, sc_)


def _emit_pair_scores(b, K, W, HS, p, qb, qq, kk, vpair, PT, cmask, bcol,
                      rr, rbx, cxst, sc_, causal=True, bcols16=None):
    nc = b.nc
    nkt = (4 * qb + 4) if causal else 16
    qsl = slice(qb * TB, (qb + 1) * TB)
    with contextlib.ExitStack() as ss:
        psS = [b.ps(ss, "sq_S", [128, 2 * TB]) for _ in range(2)]
        psA = b.ps(ss, "sq_A", [65, TB])
        psB = b.ps(ss, "sq_B", [65, TB])
        psr = b.ps(ss, "sq_r", [64, TB])
        (te0, sc0, ve0, pv0) = b.base("te", "sc", "ve", "pv")
        with nc.Block() as block:
            @block.tensor
            def _(tensor):
                for kt in range(nkt):
                    ksl = slice(kt * 128, (kt + 1) * 128)
                    if kt >= 2:
                        tensor.wait_ge(b.sem("sc"), sc0 + kt - 1)
                    tensor.matmul(psS[kt % 2][:, 0:TB], kk[0:64, ksl], qq[0:64, qsl],
                                  start=True, stop=True, tile_position=(0, 0))
                    ins = tensor.matmul(psS[kt % 2][:, TB:2 * TB], kk[64:128, ksl],
                                        qq[64:128, qsl], start=True, stop=True,
                                        tile_position=(64, 0))
                    b.inc(ins, "te", 1)
                    tensor.wait_ge(b.sem("ve"), ve0 + kt + 1)
                    tensor.matmul(psA[:, :], vpair[kt][:, 0:65], PT[kt % 2][:, 0:TB],
                                  start=(kt == 0), stop=(kt == nkt - 1))
                    ins = tensor.matmul(psB[:, :], vpair[kt][:, 65:130],
                                        PT[kt % 2][:, TB:2 * TB],
                                        start=(kt == 0), stop=(kt == nkt - 1))
                    b.inc(ins, "pv", 1)

            @block.scalar
            def _(scalar):
                for kt in range(nkt):
                    scalar.wait_ge(b.sem("te"), te0 + kt + 1)
                    if kt >= 2:
                        scalar.wait_ge(b.sem("pv"), pv0 + kt - 1)
                    bias_ap = bcol[:] if bcols16 is None else bcols16[:, kt:kt + 1]
                    scalar.activation(out=PT[kt % 2][:], in_=psS[kt % 2][:, :],
                                      func=AF.Exp, bias=bias_ap, scale=0.125)
                    b.mark(scalar, b.tags, "sc", 1)

            @block.vector
            def _(vector):
                for kt in range(nkt):
                    vector.wait_ge(b.sem("sc"), sc0 + kt + 1)
                    di = (kt - 4 * qb) if causal else -1
                    if 0 <= di <= 3:
                        vector.tensor_mul(out=PT[kt % 2][:], in0=PT[kt % 2][:],
                                          in1=cmask[di][:])
                        vector.drain()
                    b.mark(vector, b.tagv, "ve", 1)

        (pv0b, ve0b, sc0b, te0b) = b.base("pv", "ve", "sc", "te")
        with nc.Block() as block:
            @block.scalar
            def _(scalar):
                scalar.copy(out=rr[0:1, :], in_=psA[64:65, :])
                b.mark(scalar, b.tags, "sc", 1)
                scalar.wait_ge(b.sem("te"), te0b + 1)
                scalar.copy(out=rbx[:, :], in_=psr[:, :])
                b.mark(scalar, b.tags, "sc", 1)

            @block.vector
            def _(vector):
                b.wait(vector, "sc", sc0b + 1)
                vector.reciprocal(out=rr[0:1, :], in_=rr[0:1, :])
                vector.drain()
                b.mark(vector, b.tagv, "ve", 1)

            @block.tensor
            def _(tensor):
                tensor.wait_ge(b.sem("ve"), ve0b + 1)
                ins = tensor.matmul(psr[:, :], K["ones_row32"][:, 0:64], rr[0:1, :],
                                    start=True, stop=True)
                b.inc(ins, "te", 1)

        with nc.Block() as block:
            @block.vector
            def _(vector):
                b.wait(vector, "cxd0")
                b.wait(vector, "cxd1")
                vector.tensor_mul(out=cxst[0:64, :], in0=psA[0:64, :], in1=rbx[:, :])
                vector.drain()
                b.mark(vector, b.tagv, "ve", 1)

            @block.scalar
            def _(scalar):
                b.wait(scalar, "ve")
                b.dma(scalar, HS["ctx"][p * 128:p * 128 + 64, qsl], cxst[0:64, :],
                      "cxd0")

        (sc0c, ve0c, te0c) = b.base("sc", "ve", "te")
        with nc.Block() as block:
            @block.scalar
            def _(scalar):
                scalar.copy(out=rr[0:1, :], in_=psB[64:65, :])
                b.mark(scalar, b.tags, "sc", 1)
                scalar.wait_ge(b.sem("te"), te0c + 1)
                scalar.copy(out=rbx[:, :], in_=psr[:, :])
                b.mark(scalar, b.tags, "sc", 1)

            @block.vector
            def _(vector):
                b.wait(vector, "sc", sc0c + 1)
                vector.reciprocal(out=rr[0:1, :], in_=rr[0:1, :])
                vector.drain()
                b.mark(vector, b.tagv, "ve", 1)

            @block.tensor
            def _(tensor):
                tensor.wait_ge(b.sem("ve"), ve0c + 1)
                ins = tensor.matmul(psr[:, :], K["ones_row32"][:, 0:64], rr[0:1, :],
                                    start=True, stop=True)
                b.inc(ins, "te", 1)

        with nc.Block() as block:
            @block.vector
            def _(vector):
                b.wait(vector, "cxd0")
                b.wait(vector, "cxd1")
                vector.tensor_mul(out=cxst[0:64, :], in0=psB[0:64, :], in1=rbx[:, :])
                vector.drain()
                b.mark(vector, b.tagv, "ve", 1)

            @block.scalar
            def _(scalar):
                b.wait(scalar, "ve")
                b.dma(scalar, HS["ctx"][p * 128 + 64:(p + 1) * 128, qsl], cxst[0:64, :],
                      "cxd1")


def _emit_l1_after_attn(b, K, W, HS, X, xn2):
    nc = b.nc
    with contextlib.ExitStack() as oa:
        wo = [b.sb(oa, "oa_wo", [128, D], BF16) for _ in range(NCH)]
        cx = [b.sb(oa, "oa_cx", [128, TB], BF16) for _ in range(NCH)]
        zb = [b.sb(oa, "oa_zb", [128, TB], BF16) for _ in range(2)]
        ub = [b.sb(oa, "oa_ub", [128, T], BF16) for _ in range(NCH)]
        po = [b.ps(oa, "oa_po", [128, TB]) for _ in range(2)]
        with nc.Block() as block:
            @block.gpsimd
            def _(gpsimd):
                for s in range(NCH):
                    b.dma(gpsimd, wo[s][:], W["W_o"][s * 128:(s + 1) * 128, :],
                          f"gld{s % 2}")
                b.wait(gpsimd, "gld0")
                b.wait(gpsimd, "gld1")
        for qb in range(NTB):
            qsl = slice(qb * TB, (qb + 1) * TB)
            with nc.Block() as block:
                @block.sync
                def _(sync):
                    if qb == 0:
                        b.wait(sync, "cxd0")
                        b.wait(sync, "cxd1")
                    for s in range(NCH):
                        b.dma(sync, cx[s][:], HS["ctx"][s * 128:(s + 1) * 128, qsl],
                              "dma")
                    b.wait(sync, "dma")
            (te0, sc0, ve0) = b.base("te", "sc", "ve")
            zl0 = [b.base("zld0")[0], b.base("zld1")[0]]
            with nc.Block() as block:
                @block.tensor
                def _(tensor):
                    for oc in range(NCH):
                        if oc >= 2:
                            tensor.wait_ge(b.sem("ve"), ve0 + oc - 1)
                        for s in range(NCH):
                            ins = tensor.matmul(po[oc % 2][:, :],
                                                wo[s][:, oc * 128:(oc + 1) * 128],
                                                cx[s][:], start=(s == 0),
                                                stop=(s == NCH - 1))
                        b.inc(ins, "te", 1)

                @block.scalar
                def _(scalar):
                    for oc in range(NCH):
                        scalar.wait_ge(b.sem("ve"), ve0 + max(0, oc - 1))
                        b.dma(scalar, zb[oc % 2][:],
                              HS["zbf"][oc * 128:(oc + 1) * 128, qsl],
                              f"zld{oc % 2}")

                @block.vector
                def _(vector):
                    for oc in range(NCH):
                        vector.wait_ge(b.sem("te"), te0 + oc + 1)
                        vector.wait_ge(b.sem(f"zld{oc % 2}"),
                                       zl0[oc % 2] + 16 * (oc // 2 + 1))
                        vector.tensor_add(out=ub[oc][:, qsl], in0=po[oc % 2][:, :],
                                          in1=zb[oc % 2][:])
                        vector.drain()
                        vector.scalar_tensor_tensor(out=X[oc][:, qsl],
                                                    in0=ub[oc][:, qsl], scalar=0.5,
                                                    in1=X[oc][:, qsl],
                                                    op0=ALU.mult, op1=ALU.add)
                        vector.drain()
                        b.mark(vector, b.tagv, "ve", 1)

        emit_stats_bf(b, ub, K)
        with contextlib.ExitStack() as px:
            r2b = b.sb(px, "oa_r2b", [128, T], F32)
            emit_bcast_full(b, K, K["R"]["r"], r2b)
            with nc.Block() as block:
                @block.vector
                def _(vector):
                    for c in range(NCH):
                        vector.scalar_tensor_tensor(out=xn2[c][:], in0=ub[c][:],
                                                    scalar=W["w2col_sb"][:, c:c + 1],
                                                    in1=r2b[:], op0=ALU.mult,
                                                    op1=ALU.mult)
                        vector.drain()
                    b.mark(vector, b.tagv, "ve", 1)


def emit_stats_bf(b, U, K):
    nc = b.nc
    with contextlib.ExitStack() as sc:
        usq32 = b.sb(sc, "sb_us32", [128, T], F32)
        usq = [b.sb(sc, "sb_usq", [128, T], BF16)]
        e2bf = [b.sb(sc, "sb_e2", [128, T], BF16)]
        pst = [b.ps(sc, "sb_ps", [1, TB]) for _ in range(NTB)]
        for c in range(NCH):
            (ve0, sc0) = b.base("ve", "sc")
            with nc.Block() as block:
                @block.scalar
                def _(scalar):
                    scalar.activation(out=usq32[:], in_=U[c][:], func=AF.Square)
                    scalar.drain()
                    scalar.copy(out=usq[0][:], in_=usq32[:])
                    b.mark(scalar, b.tags, "sc", 1)

                @block.vector
                def _(vector):
                    vector.wait_ge(b.sem("sc"), sc0 + 1)
                    vector.tensor_sub(out=e2bf[0][:], in0=usq32[:],
                                      in1=usq[0][:])
                    vector.drain()
                    b.mark(vector, b.tagv, "ve", 1)
            with nc.Block() as block:
                @block.tensor
                def _(tensor):
                    for tb in range(NTB):
                        sl = slice(tb * TB, (tb + 1) * TB)
                        tensor.matmul(pst[tb][0:1, :], K["onesm"][:], usq[0][:, sl],
                                      start=(c == 0), stop=False)
                        ins = tensor.matmul(pst[tb][0:1, :], K["onesm"][:],
                                            e2bf[0][:, sl],
                                            start=False, stop=(c == NCH - 1))
                    b.inc(ins, "te", 1)
        with nc.Block() as block:
            @block.vector
            def _(vector):
                for tb in range(NTB):
                    vector.tensor_copy(out=K["R"]["msq"][0:1, tb * TB:(tb + 1) * TB],
                                       in_=pst[tb][0:1, :])
                vector.drain()
                b.mark(vector, b.tagv, "ve", 1)
    _rms_rows(b, K, K["R"]["msq"][0:1, :], K["R"]["r"][0:1, :])


def _emit_moe(b, K, W, X, xn2, HS, osc):
    nc = b.nc
    with contextlib.ExitStack() as ms:
        wrow = b.sb(ms, "mo_wrow", [1, T], F32)
        _emit_router(b, K, W, xn2, HS, ms)
        wg = [b.sb(ms, "mo_wg", [128, D], BF16) for _ in range(NCH)]
        wu = [b.sb(ms, "mo_wu", [128, D], BF16) for _ in range(NCH)]
        wd = [b.sb(ms, "mo_wd", [128, D], BF16) for _ in range(NCH)]
        gsb = b.sb(ms, "mo_gs", [128, TB], F32)
        usb = b.sb(ms, "mo_us", [128, TB], F32)
        hmid = [b.sb(ms, "mo_hm", [128, TB], BF16) for _ in range(NCH)]
        wb = b.sb(ms, "mo_wb", [128, TB], F32)
        pg = [b.ps(ms, "mo_pg", [128, TB]) for _ in range(2)]
        pd = [b.ps(ms, "mo_pd", [128, TB]) for _ in range(2)]
        pwb = b.ps(ms, "mo_pwb", [128, TB])

        for e in range(8):
            srcs = (W["Wg_s"], W["Wu_s"], W["Wd_s"]) if e == 0 else \
                   (W["Wg_r"][e - 1], W["Wu_r"][e - 1], W["Wd_r"][e - 1])
            with nc.Block() as block:
                @block.sync
                def _(sync):
                    if e > 0:
                        b.wait(sync, "wtd0")
                        b.wait(sync, "wtd1")
                        b.dma(sync, wrow[0:1, :], HS["wts"][e - 1:e, :], "dma")
                        b.wait(sync, "dma")

                @block.gpsimd
                def _(gpsimd):
                    if e == 0:
                        b.wait(gpsimd, "wtd0")
                        b.wait(gpsimd, "wtd1")
                    for s in range(NCH):
                        b.dma(gpsimd, wg[s][:], srcs[0][s * 128:(s + 1) * 128, :],
                              f"gld{s % 2}")
                        b.dma(gpsimd, wu[s][:], srcs[1][s * 128:(s + 1) * 128, :],
                              f"gld{s % 2}")
                        b.dma(gpsimd, wd[s][:], srcs[2][s * 128:(s + 1) * 128, :],
                              f"gld{s % 2}")
                    b.wait(gpsimd, "gld0")
                    b.wait(gpsimd, "gld1")

            for tb in range(NTB):
                sl = slice(tb * TB, (tb + 1) * TB)
                if e > 0:
                    with nc.Block() as block:
                        @block.tensor
                        def _(tensor):
                            ins = tensor.matmul(pwb[:, :], K["ones_row32"][:],
                                                wrow[0:1, sl], start=True,
                                                stop=True)
                            b.inc(ins, "te", 1)

                        @block.scalar
                        def _(scalar):
                            b.wait(scalar, "te")
                            scalar.copy(out=wb[:], in_=pwb[:, :])
                            b.mark(scalar, b.tags, "sc", 1)

                (te0, sc0, ve0) = b.base("te", "sc", "ve")
                with nc.Block() as block:
                    @block.tensor
                    def _(tensor):
                        for ic in range(NCH):
                            it = ic
                            if it >= 1:
                                tensor.wait_ge(b.sem("ve"), ve0 + it)
                            for s in range(NCH):
                                tensor.matmul(pg[0][:, :],
                                              wg[s][:, ic * 128:(ic + 1) * 128],
                                              xn2[s][:, sl],
                                              start=(s == 0), stop=(s == NCH - 1))
                            for s in range(NCH):
                                ins = tensor.matmul(pg[1][:, :],
                                                    wu[s][:, ic * 128:(ic + 1) * 128],
                                                    xn2[s][:, sl],
                                                    start=(s == 0), stop=(s == NCH - 1))
                            b.inc(ins, "te", 1)

                    @block.scalar
                    def _(scalar):
                        for ic in range(NCH):
                            scalar.wait_ge(b.sem("te"), te0 + ic + 1)
                            if ic >= 1:
                                scalar.wait_ge(b.sem("ve"), ve0 + ic)
                            scalar.activation(out=gsb[:], in_=pg[0][:, :], func=AF.Silu)
                            scalar.activation(out=usb[:], in_=pg[1][:, :], func=AF.Copy,
                                              scale=0.5 if e == 0 else 1.0)
                            b.mark(scalar, b.tags, "sc", 1)

                    @block.vector
                    def _(vector):
                        for ic in range(NCH):
                            vector.wait_ge(b.sem("sc"), sc0 + ic + 1)
                            vector.tensor_mul(out=hmid[ic][:], in0=gsb[:], in1=usb[:])
                            vector.drain()
                            if e > 0:
                                vector.tensor_mul(out=hmid[ic][:], in0=hmid[ic][:],
                                                  in1=wb[:])
                                vector.drain()
                            b.mark(vector, b.tagv, "ve", 1)

                (te0, ve0) = b.base("te", "ve")
                with nc.Block() as block:
                    @block.tensor
                    def _(tensor):
                        for hc in range(NCH):
                            if hc >= 2:
                                tensor.wait_ge(b.sem("ve"), ve0 + hc - 1)
                            for ic in range(NCH):
                                ins = tensor.matmul(pd[hc % 2][:, :],
                                                    wd[ic][:, hc * 128:(hc + 1) * 128],
                                                    hmid[ic][:],
                                                    start=(ic == 0), stop=(ic == NCH - 1))
                            b.inc(ins, "te", 1)

                    @block.vector
                    def _(vector):
                        for hc in range(NCH):
                            vector.wait_ge(b.sem("te"), te0 + hc + 1)
                            vector.scalar_tensor_tensor(out=X[hc][:, sl],
                                                        in0=pd[hc % 2][:, :], scalar=1.0,
                                                        in1=X[hc][:, sl],
                                                        op0=ALU.mult, op1=ALU.add)
                            vector.drain()
                            b.mark(vector, b.tagv, "ve", 1)


def _emit_router(b, K, W, xn2, HS, ms):
    nc = b.nc
    with contextlib.ExitStack() as rs:
        wrt = b.sb(rs, "rt_w", [128, NCH * 7], BF16)
        rbias = b.sb(rs, "rt_b", [7, 1], F32)
        probsr = b.sb(rs, "rt_pr", [7, TB], F32)
        pT = b.sb(rs, "rt_pT", [128, 16 * 7], F32)
        m1 = b.sb(rs, "rt_m1", [128, 16], F32)
        m2 = b.sb(rs, "rt_m2", [128, 16], F32)
        eq = b.sb(rs, "rt_eq", [128, 16 * 7], F32)
        first = b.sb(rs, "rt_f1", [128, 16 * 7], F32)
        run = b.sb(rs, "rt_run", [128, 16], F32)
        nr = b.sb(rs, "rt_nr", [128, 16], F32)
        wtsT = b.sb(rs, "rt_wT", [128, 16 * 7], F32)
        masked = b.sb(rs, "rt_mk", [128, 16 * 7], F32)
        prt = b.ps(rs, "rt_ps", [7, TB])
        ptt = b.ps(rs, "rt_pt", [128, 7])
        ptb = b.ps(rs, "rt_ptb", [7, 128])

        with nc.Block() as block:
            @block.gpsimd
            def _(gpsimd):
                b.dma(gpsimd, wrt[:].rearrange("p (s c) -> p s c", c=7),
                      W["W_router"].rearrange("(s p) c -> p s c", p=128), "gld0")
                b.wait(gpsimd, "gld0")

            @block.sync
            def _(sync):
                b.dma(sync, rbias[:], W["rbias"], "dma")
                b.wait(sync, "dma")

        for tb in range(NTB):
            sl = slice(tb * TB, (tb + 1) * TB)
            (te0, sc0, ve0) = b.base("te", "sc", "ve")
            with nc.Block() as block:
                @block.tensor
                def _(tensor):
                    for s in range(NCH):
                        ins = tensor.matmul(prt[:, :], wrt[:, s * 7:(s + 1) * 7],
                                            xn2[s][:, sl],
                                            start=(s == 0), stop=(s == NCH - 1))
                    b.inc(ins, "te", 1)

                @block.scalar
                def _(scalar):
                    b.wait(scalar, "te")
                    scalar.activation(out=probsr[:], in_=prt[:, :], func=AF.Sigmoid,
                                      bias=rbias[:], scale=1.0)
                    b.mark(scalar, b.tags, "sc", 1)

            for j4 in range(4):
                t_idx = tb * 4 + j4
                with nc.Block() as block:
                    @block.tensor
                    def _(tensor):
                        ins = tensor.transpose(ptt[:, :],
                                               probsr[:, j4 * 128:(j4 + 1) * 128],
                                               K["ident32"][0:7, 0:7])
                        b.inc(ins, "te", 1)

                    @block.vector
                    def _(vector):
                        b.wait(vector, "te")
                        vector.tensor_copy(out=pT[:, t_idx * 7:(t_idx + 1) * 7],
                                           in_=ptt[:, :])
                        vector.drain()
                        b.mark(vector, b.tagv, "ve", 1)

        with nc.Block() as block:
            @block.vector
            def _(vector):
                p3 = pT[:].rearrange("p (t e) -> p t e", e=7)
                vector.tensor_reduce(out=m1[:].rearrange("p (t o) -> p t o", o=1),
                                     in_=p3, axis=AX.X, op=ALU.max)
                vector.drain()
                vector.memset(run[:], 0.0)
                vector.drain()
                for e in range(7):
                    sl_e = pT[:, e::7]
                    vector.tensor_tensor(out=eq[:, e::7], in0=sl_e, in1=m1[:],
                                         op=ALU.is_ge)
                    vector.drain()
                for e in range(7):
                    vector.tensor_scalar(out=nr[:], in0=run[:], scalar1=-1.0,
                                         scalar2=1.0, op0=ALU.mult, op1=ALU.add)
                    vector.drain()
                    vector.tensor_mul(out=first[:, e::7], in0=eq[:, e::7], in1=nr[:])
                    vector.drain()
                    vector.tensor_add(out=run[:], in0=run[:], in1=first[:, e::7])
                    vector.drain()
                vector.scalar_tensor_tensor(out=masked[:], in0=first[:], scalar=-1e30,
                                            in1=pT[:], op0=ALU.mult, op1=ALU.add)
                vector.drain()
                vector.tensor_reduce(out=m2[:].rearrange("p (t o) -> p t o", o=1),
                                     in_=masked[:].rearrange("p (t e) -> p t e", e=7),
                                     axis=AX.X, op=ALU.max)
                vector.drain()
                vector.memset(run[:], 0.0)
                vector.drain()
                for e in range(7):
                    vector.tensor_tensor(out=eq[:, e::7], in0=masked[:, e::7], in1=m2[:],
                                         op=ALU.is_ge)
                    vector.drain()
                for e in range(7):
                    vector.tensor_scalar(out=nr[:], in0=run[:], scalar1=-1.0,
                                         scalar2=1.0, op0=ALU.mult, op1=ALU.add)
                    vector.drain()
                    vector.tensor_mul(out=eq[:, e::7], in0=eq[:, e::7], in1=nr[:])
                    vector.drain()
                    vector.tensor_add(out=run[:], in0=run[:], in1=eq[:, e::7])
                    vector.drain()
                vector.tensor_add(out=nr[:], in0=m1[:], in1=m2[:])
                vector.drain()
                vector.reciprocal(out=nr[:], in_=nr[:])
                vector.drain()
                vector.tensor_scalar_mul(out=nr[:], in0=nr[:], scalar1=0.5)
                vector.drain()
                for e in range(7):
                    vector.tensor_mul(out=first[:, e::7], in0=first[:, e::7], in1=m1[:])
                    vector.drain()
                    vector.tensor_mul(out=eq[:, e::7], in0=eq[:, e::7], in1=m2[:])
                    vector.drain()
                vector.tensor_add(out=wtsT[:], in0=first[:], in1=eq[:])
                vector.drain()
                for e in range(7):
                    vector.tensor_mul(out=wtsT[:, e::7], in0=wtsT[:, e::7], in1=nr[:])
                    vector.drain()
                b.mark(vector, b.tagv, "ve", 1)

        wstage = b.sb(rs, "rt_wst", [7, 128], F32)
        for t_idx in range(NT128):
            with nc.Block() as block:
                @block.tensor
                def _(tensor):
                    ins = tensor.transpose(ptb[0:7, 0:128],
                                           wtsT[:, t_idx * 7:(t_idx + 1) * 7],
                                           K["ident32"][:])
                    b.inc(ins, "te", 1)

                @block.vector
                def _(vector):
                    b.wait(vector, "te")
                    b.wait(vector, "wtd0")
                    b.wait(vector, "wtd1")
                    vector.tensor_copy(out=wstage[:], in_=ptb[0:7, 0:128])
                    vector.drain()
                    b.mark(vector, b.tagv, "ve", 1)

                @block.scalar
                def _(scalar):
                    b.wait(scalar, "ve")
                    b.dma(scalar, HS["wts"][:, t_idx * 128:(t_idx + 1) * 128],
                          wstage[:], f"wtd{t_idx % 2}")




def emit_l1_attn(b, K, X, W2, HS):
    nc = b.nc
    emit_stats(b, X, K, "ln")
    with contextlib.ExitStack() as osc:
        xn = [b.sb(osc, "la_xn", [128, T], BF16) for _ in range(NCH)]
        maskcols = b.sb(osc, "la_mc", [128, 16], F32)
        with contextlib.ExitStack() as nsc:
            rbs = b.sb(nsc, "la_rbs", [128, T], F32)
            mbs = b.sb(nsc, "la_mbs", [128, T], F32)
            mrow = b.sb(nsc, "la_mrow", [1, T], F32)
            pmk = b.ps(nsc, "la_pmk", [128, 1])
            emit_bcast_full(b, K, K["R"]["r"], rbs)
            emit_bcast_full(b, K, K["R"]["mur"], mbs)
            lnft = b.sb(nsc, "la_ft", [128, T], F32)
            with nc.Block() as block:
                @block.vector
                def _(vector):
                    for c in range(NCH):
                        vector.tensor_mul(out=lnft[:], in0=X[c][:], in1=rbs[:])
                        vector.drain()
                        vector.tensor_sub(out=xn[c][:], in0=lnft[:], in1=mbs[:])
                        vector.drain()
                    b.mark(vector, b.tagv, "ve", 1)
            with nc.Block() as block:
                @block.vector
                def _(vector):
                    vector.tensor_mul(out=mrow[0:1, :], in0=X[0][0:1, :], in1=rbs[0:1, :])
                    vector.drain()
                    vector.tensor_sub(out=mrow[0:1, :], in0=mrow[0:1, :], in1=mbs[0:1, :])
                    vector.drain()
                    vector.tensor_single_scalar(out=mrow[0:1, :], in_=mrow[0:1, :],
                                                scalar=0.0, op=ALU.not_equal)
                    vector.drain()
                    vector.tensor_scalar(out=mrow[0:1, :], in0=mrow[0:1, :],
                                         scalar1=1.0, scalar2=1e30,
                                         op0=ALU.subtract, op1=ALU.mult)
                    vector.drain()
                    b.mark(vector, b.tagv, "ve", 1)
            for kt in range(16):
                with nc.Block() as block:
                    @block.tensor
                    def _(tensor):
                        ins = tensor.transpose(pmk[:, 0:1],
                                               mrow[0:1, kt * 128:(kt + 1) * 128],
                                               K["ident32"][0:1, 0:1])
                        b.inc(ins, "te", 1)

                    @block.vector
                    def _(vector):
                        b.wait(vector, "te")
                        vector.tensor_copy(out=maskcols[:, kt:kt + 1], in_=pmk[:, 0:1])
                        vector.drain()
                        b.mark(vector, b.tagv, "ve", 1)

        with contextlib.ExitStack() as pl:
            qp = b.sb(pl, "la_qp", [128, T], BF16)
            kp = b.sb(pl, "la_kp", [128, T], BF16)
            vpair = [b.sb(pl, "la_vp", [128, 130], BF16) for _ in range(NT128)]
            PT = [b.sb(pl, "la_pt", [128, 2 * TB], BF16) for _ in range(2)]
            wq_p = b.sb(pl, "la_wqp", [128, NCH * 128], BF16)
            wk_p = b.sb(pl, "la_wkp", [128, NCH * 128], BF16)
            wv_p = b.sb(pl, "la_wvp", [128, NCH * 128], BF16)
            bvt = b.sb(pl, "la_bvt", [128, 128], F32)
            sqb = b.sb(pl, "la_sqb", [128, TB], BF16)
            n11 = b.sb(pl, "la_n11", [1, 4], F32)
            bcol = b.sb(pl, "la_bcol", [128, 1], F32)
            bcols16 = b.sb(pl, "la_bc16", [128, 16], F32)
            rr = b.sb(pl, "la_rr", [1, TB], F32)
            rbx = b.sb(pl, "la_rbx", [64, TB], F32)
            cxst = b.sb(pl, "la_cxst", [128, TB], BF16)
            bqc = b.sb(pl, "la_bqc", [128, NCH], F32)
            bkc = b.sb(pl, "la_bkc", [128, NCH], F32)
            with nc.Block() as block:
                @block.sync
                def _(sync):
                    b.dma(sync, bqc[:], W2["bq"], "dma")
                    b.dma(sync, bkc[:], W2["bk"], "dma")
                    b.wait(sync, "dma")
            for p in range(NCH):
                _emit_mha_pair(b, K, W2, HS, p, xn, qp, kp, vpair, PT, wq_p, wk_p,
                               wv_p, bvt, sqb, n11, bcol, bcols16, maskcols,
                               rr, rbx, cxst, bqc, bkc)

        with contextlib.ExitStack() as oa:
            wo = [b.sb(oa, "la_wo", [128, D], BF16) for _ in range(NCH)]
            cx = [b.sb(oa, "la_cx", [128, TB], BF16) for _ in range(NCH)]
            boc = b.sb(oa, "la_boc", [128, NCH], F32)
            po = [b.ps(oa, "la_po", [128, TB]) for _ in range(2)]
            with nc.Block() as block:
                @block.gpsimd
                def _(gpsimd):
                    for s in range(NCH):
                        b.dma(gpsimd, wo[s][:], W2["woT"][s * 128:(s + 1) * 128, :],
                              f"gld{s % 2}")
                    b.wait(gpsimd, "gld0")
                    b.wait(gpsimd, "gld1")

                @block.sync
                def _(sync):
                    b.dma(sync, boc[:], W2["bo"], "dma")
                    b.wait(sync, "dma")
            for qb in range(NTB):
                qsl = slice(qb * TB, (qb + 1) * TB)
                with nc.Block() as block:
                    @block.sync
                    def _(sync):
                        if qb == 0:
                            b.wait(sync, "cxd0")
                            b.wait(sync, "cxd1")
                        for s in range(NCH):
                            b.dma(sync, cx[s][:], HS["ctx"][s * 128:(s + 1) * 128, qsl],
                                  "dma")
                        b.wait(sync, "dma")
                (te0, sc0, ve0) = b.base("te", "sc", "ve")
                with nc.Block() as block:
                    @block.tensor
                    def _(tensor):
                        for oc in range(NCH):
                            if oc >= 2:
                                tensor.wait_ge(b.sem("ve"), ve0 + oc - 1)
                            for s in range(NCH):
                                ins = tensor.matmul(po[oc % 2][:, :],
                                                    wo[s][:, oc * 128:(oc + 1) * 128],
                                                    cx[s][:], start=(s == 0),
                                                    stop=(s == NCH - 1))
                            b.inc(ins, "te", 1)

                    @block.scalar
                    def _(scalar):
                        for oc in range(NCH):
                            scalar.wait_ge(b.sem("te"), te0 + oc + 1)
                            if oc >= 1:
                                scalar.wait_ge(b.sem("ve"), ve0 + oc)
                            scalar.activation(out=K["ftmp32"][:, 0:TB], in_=po[oc % 2][:, :],
                                              func=AF.Identity, bias=boc[:, oc:oc + 1],
                                              scale=1.0)
                            b.mark(scalar, b.tags, "sc", 1)

                    @block.vector
                    def _(vector):
                        for oc in range(NCH):
                            vector.wait_ge(b.sem("sc"), sc0 + oc + 1)
                            vector.tensor_add(out=X[oc][:, qsl], in0=X[oc][:, qsl],
                                              in1=K["ftmp32"][:, 0:TB])
                            vector.drain()
                            b.mark(vector, b.tagv, "ve", 1)


def _emit_mha_pair(b, K, W2, HS, p, xn, qp, kp, vpair, PT, wq_p, wk_p, wv_p,
                   bvt, sqb, n11, bcol, bcols16, maskcols, rr, rbx, cxst, bqc, bkc):
    nc = b.nc
    with contextlib.ExitStack() as sc_:
        with contextlib.ExitStack() as proj_sc:
            pq = b.ps(proj_sc, "ma_pq", [128, TB])
            pv = [b.ps(proj_sc, "ma_pv", [128, 128]) for _ in range(2)]
            pn = b.ps(proj_sc, "ma_pn", [1, TB])
            pb1 = b.ps(proj_sc, "ma_pb1", [128, 1])
            pbv = b.ps(proj_sc, "ma_pbv", [128, 128])

            with nc.Block() as block:
                @block.gpsimd
                def _(gpsimd):
                    b.dma(gpsimd, wq_p[:].rearrange("p (s c) -> p s c", c=128),
                          W2["wqT"][:, p * 128:(p + 1) * 128]
                          .rearrange("(s p) c -> p s c", p=128), "gld0")
                    b.dma(gpsimd, wk_p[:].rearrange("p (s c) -> p s c", c=128),
                          W2["wkT"][:, p * 128:(p + 1) * 128]
                          .rearrange("(s p) c -> p s c", p=128), "gld1")
                    b.dma(gpsimd, wv_p[:].rearrange("p (s c) -> p s c", c=128),
                          W2["wvT"][:, p * 128:(p + 1) * 128]
                          .rearrange("(s p) c -> p s c", p=128), "gld0")
                    b.wait(gpsimd, "gld0")
                    b.wait(gpsimd, "gld1")

            with nc.Block() as block:
                @block.tensor
                def _(tensor):
                    ins = tensor.matmul(pbv[:, :], K["ones_row32"][:],
                                        W2["bv_sb"][0:1, p * 128:(p + 1) * 128],
                                        start=True, stop=True)
                    b.inc(ins, "te", 1)

                @block.scalar
                def _(scalar):
                    b.wait(scalar, "te")
                    scalar.copy(out=bvt[:], in_=pbv[:, :])
                    b.mark(scalar, b.tags, "sc", 1)

            for (dst, wsrc, bc) in [(qp, wq_p, bqc), (kp, wk_p, bkc)]:
                for tb in range(NTB):
                    sl = slice(tb * TB, (tb + 1) * TB)
                    with nc.Block() as block:
                        @block.tensor
                        def _(tensor):
                            for s in range(NCH):
                                ins = tensor.matmul(pq[:, :],
                                                    wsrc[:, s * 128:(s + 1) * 128],
                                                    xn[s][:, sl],
                                                    start=(s == 0), stop=(s == NCH - 1))
                            b.inc(ins, "te", 1)

                        @block.scalar
                        def _(scalar):
                            b.wait(scalar, "te")
                            scalar.activation(out=dst[:, sl], in_=pq[:, :],
                                              func=AF.Identity, bias=bc[:, p:p + 1],
                                              scale=1.0)
                            b.mark(scalar, b.tags, "sc", 1)

            (te0, ve0) = b.base("te", "ve")
            with nc.Block() as block:
                @block.tensor
                def _(tensor):
                    for t in range(NT128):
                        if t >= 2:
                            tensor.wait_ge(b.sem("ve"), ve0 + t - 1)
                        for s in range(NCH):
                            ins = tensor.matmul(pv[t % 2][:, :],
                                                xn[s][:, t * 128:(t + 1) * 128],
                                                wv_p[:, s * 128:(s + 1) * 128],
                                                start=(s == 0), stop=(s == NCH - 1))
                        b.inc(ins, "te", 1)

                @block.vector
                def _(vector):
                    for t in range(NT128):
                        vector.wait_ge(b.sem("te"), te0 + t + 1)
                        vector.tensor_add(out=vpair[t][:, 0:64], in0=pv[t % 2][:, 0:64],
                                          in1=bvt[:, 0:64])
                        vector.drain()
                        vector.tensor_add(out=vpair[t][:, 65:129],
                                          in0=pv[t % 2][:, 64:128], in1=bvt[:, 64:128])
                        vector.drain()
                        vector.memset(vpair[t][:, 64:65], 1.0)
                        vector.memset(vpair[t][:, 129:130], 1.0)
                        vector.drain()
                        b.mark(vector, b.tagv, "ve", 1)

            with nc.Block() as block:
                @block.vector
                def _(vector):
                    vector.memset(n11[0:1, 0:2], 0.0)
                    vector.drain()
                    b.mark(vector, b.tagv, "ve", 1)
            for (i, srcb) in [(0, qp), (1, kp)]:
                for tb in range(NTB):
                    sl = slice(tb * TB, (tb + 1) * TB)
                    with nc.Block() as block:
                        @block.vector
                        def _(vector):
                            vector.tensor_mul(out=sqb[:], in0=srcb[:, sl],
                                              in1=srcb[:, sl])
                            vector.drain()
                            b.mark(vector, b.tagv, "ve", 1)
                    with nc.Block() as block:
                        @block.tensor
                        def _(tensor):
                            ins = tensor.matmul(pn[0:1, :], K["onesm"][:], sqb[:],
                                                start=True, stop=True)
                            b.inc(ins, "te", 1)

                        @block.vector
                        def _(vector):
                            b.wait(vector, "te")
                            vector.tensor_reduce(out=n11[0:1, 2:3], in_=pn[0:1, :],
                                                 axis=AX.X, op=ALU.max)
                            vector.drain()
                            vector.tensor_max(out=n11[0:1, i:i + 1],
                                              in0=n11[0:1, i:i + 1], in1=n11[0:1, 2:3])
                            vector.drain()
                            b.mark(vector, b.tagv, "ve", 1)
            with nc.Block() as block:
                @block.scalar
                def _(scalar):
                    scalar.activation(out=n11[0:1, 0:2], in_=n11[0:1, 0:2], func=AF.Sqrt)
                    scalar.drain()
                    b.mark(scalar, b.tags, "sc", 1)

                @block.vector
                def _(vector):
                    b.wait(vector, "sc")
                    vector.tensor_mul(out=n11[0:1, 3:4], in0=n11[0:1, 0:1],
                                      in1=n11[0:1, 1:2])
                    vector.drain()
                    vector.tensor_scalar_mul(out=n11[0:1, 3:4], in0=n11[0:1, 3:4],
                                             scalar1=-128.0)
                    vector.drain()
                    b.mark(vector, b.tagv, "ve", 1)
            with nc.Block() as block:
                @block.tensor
                def _(tensor):
                    ins = tensor.matmul(pb1[:, 0:1], K["ones_row32"][:], n11[0:1, 3:4],
                                        start=True, stop=True)
                    b.inc(ins, "te", 1)

                @block.vector
                def _(vector):
                    b.wait(vector, "te")
                    vector.tensor_copy(out=bcol[:], in_=pb1[:, 0:1])
                    vector.drain()
                    vector.tensor_single_scalar(out=bcols16[:], in_=maskcols[:],
                                                scalar=bcol[:], op=ALU.add)
                    vector.drain()
                    b.mark(vector, b.tagv, "ve", 1)

        for qb in range(NTB):
            _emit_pair_scores(b, K, W2, HS, p, qb, qp, kp, vpair, PT, None, bcol,
                              rr, rbx, cxst, sc_, causal=False, bcols16=bcols16)


def kernel(**inputs):
    import numpy as np
    from concourse.bass_utils import run_bass_kernel_spmd
    nc = build_program(n_cores=8, stages=6)
    in_maps = [make_inputs(inputs, c) for c in range(8)]
    res = run_bass_kernel_spmd(nc, in_maps, core_ids=list(range(8)))
    out = np.stack([np.ascontiguousarray(res.results[c]["out_fm"].T)
                    for c in range(8)], axis=0)
    return out.astype(np.float32)


# version 02
# speedup vs baseline: 1.0461x; 1.0461x over previous
import sys
if "/opt/trn_rl_repo" not in sys.path:
    sys.path.insert(0, "/opt/trn_rl_repo")
import contextlib
import numpy as np
import concourse.bass as bass
import concourse.mybir as mybir

F32 = mybir.dt.float32
BF16 = mybir.dt.bfloat16
I8 = mybir.dt.int8
AF = mybir.ActivationFunctionType
ALU = mybir.AluOpType
AX = mybir.AxisListType

D, T, NCH, TB, NTB = 1024, 2048, 8, 512, 4
I0, NI0 = 2816, 22
QMAX = 127.0
EPS = 1e-5


class B:
    def __init__(self, nc, es):
        self.nc, self.es = nc, es
        self.sems, self.cnt, self.uid = {}, {}, 0

    def name(self, base):
        self.uid += 1
        return f"{base}_{self.uid}"

    def sem(self, name):
        if name not in self.sems:
            self.sems[name] = self.es.enter_context(self.nc.semaphore(name))
            self.cnt[name] = 0
        return self.sems[name]

    def inc(self, instr, name, n=1):
        self.sem(name)
        self.cnt[name] += n
        instr.then_inc(self.sem(name), n)
        return self.cnt[name]

    def dma(self, eng, out, in_, name):
        return self.inc(eng.dma_start(out=out, in_=in_), name, 16)

    def mark(self, eng, tag, name, n=1):
        self.sem(name)
        eng.wait_ge(self.sems[name], self.cnt[name])
        self.inc(eng.memzero(tag[:]), name, n)

    def wait(self, eng, name, val=None):
        self.sem(name)
        eng.wait_ge(self.sems[name], self.cnt[name] if val is None else val)

    def base(self, *names):
        return tuple(self.cnt.setdefault(n, (self.sem(n), 0)[1]) for n in names)

    def sb(self, scope, base, shape, dt):
        return scope.enter_context(self.nc.sbuf_tensor(self.name(base), shape, dt))

    def ps(self, scope, base, shape, dt=F32):
        return scope.enter_context(self.nc.psum_tensor(self.name(base), shape, dt))


def emit_stats(b, X, K, kind):
    nc, R = b.nc, K["R"]
    with contextlib.ExitStack() as sc:
        R = dict(R)
        R["t4"] = b.sb(sc, "row_t4", [1, T], F32)
        R["t5"] = b.sb(sc, "row_t5", [1, T], F32)
        xbf = [b.sb(sc, "st_xbf", [128, T], BF16) for _ in range(2)]
        ebf = [b.sb(sc, "st_ebf", [128, T], BF16) for _ in range(2)]
        xsq32 = [b.sb(sc, "st_xs32", [128, T], F32) for _ in range(2)]
        xsq = [b.sb(sc, "st_xsq", [128, T], BF16) for _ in range(2)]
        e2bf = [b.sb(sc, "st_e2", [128, T], BF16) for _ in range(2)]
        pst_mu = [b.ps(sc, "st_pmu", [1, TB]) for _ in range(NTB)]
        pst_sq = [b.ps(sc, "st_psq", [1, TB]) for _ in range(NTB)]
        (te0, sc0, ve0) = b.base("te", "sc", "ve")
        with nc.Block() as block:
            @block.scalar
            def _(scalar):
                for c in range(NCH):
                    if c >= 2:
                        scalar.wait_ge(b.sem("te"), te0 + c - 1)
                    scalar.activation(out=xsq32[c % 2][:], in_=X[c][:], func=AF.Square)
                    scalar.drain()
                    scalar.copy(out=xsq[c % 2][:], in_=xsq32[c % 2][:])
                    if kind == "ln":
                        scalar.copy(out=xbf[c % 2][:], in_=X[c][:])
                    b.mark(scalar, b.tags, "sc", 1)

            @block.vector
            def _(vector):
                for c in range(NCH):
                    vector.wait_ge(b.sem("sc"), sc0 + c + 1)
                    if c >= 2:
                        vector.wait_ge(b.sem("te"), te0 + c - 1)
                    if kind == "ln":
                        vector.tensor_sub(out=ebf[c % 2][:], in0=X[c][:], in1=xbf[c % 2][:])
                        vector.drain()
                    vector.tensor_sub(out=e2bf[c % 2][:], in0=xsq32[c % 2][:],
                                      in1=xsq[c % 2][:])
                    b.mark(vector, b.tagv, "ve", 1)

            @block.tensor
            def _(tensor):
                for c in range(NCH):
                    tensor.wait_ge(b.sem("ve"), ve0 + c + 1)
                    for tb in range(NTB):
                        sl = slice(tb * TB, (tb + 1) * TB)
                        if kind == "ln":
                            tensor.matmul(pst_mu[tb][0:1, :], K["onesm"][:],
                                          xbf[c % 2][:, sl],
                                          start=(c == 0), stop=False)
                            tensor.matmul(pst_mu[tb][0:1, :], K["onesm"][:],
                                          ebf[c % 2][:, sl],
                                          start=False, stop=(c == NCH - 1))
                        tensor.matmul(pst_sq[tb][0:1, :], K["onesm"][:],
                                      xsq[c % 2][:, sl],
                                      start=(c == 0), stop=False)
                        ins = tensor.matmul(pst_sq[tb][0:1, :], K["onesm"][:],
                                            e2bf[c % 2][:, sl],
                                            start=False, stop=(c == NCH - 1))
                    b.inc(ins, "te", 1)

        with nc.Block() as block:
            @block.vector
            def _(vector):
                for tb in range(NTB):
                    if kind == "ln":
                        vector.tensor_copy(out=R["mu"][0:1, tb * TB:(tb + 1) * TB],
                                           in_=pst_mu[tb][0:1, :])
                    vector.tensor_copy(out=R["msq"][0:1, tb * TB:(tb + 1) * TB],
                                       in_=pst_sq[tb][0:1, :])
                vector.drain()
                if kind == "ln":
                    vector.tensor_mul(out=R["t4"][0:1, :], in0=R["mu"][0:1, :],
                                      in1=R["mu"][0:1, :])
                    vector.drain()
                    vector.tensor_sub(out=R["t4"][0:1, :], in0=R["msq"][0:1, :],
                                      in1=R["t4"][0:1, :])
                b.mark(vector, b.tagv, "ve", 1)

            @block.scalar
            def _(scalar):
                b.wait(scalar, "ve")
                if kind == "ln":
                    scalar.activation(out=R["t5"][0:1, :], in_=R["t4"][0:1, :],
                                      func=AF.Sqrt, bias=K["eps11"][:], scale=1.0)
                else:
                    scalar.activation(out=R["t5"][0:1, :], in_=R["msq"][0:1, :],
                                      func=AF.Sqrt)
                b.mark(scalar, b.tags, "sc", 1)

        with nc.Block() as block:
            @block.vector
            def _(vector):
                vector.reciprocal(out=R["r"][0:1, :], in_=R["t5"][0:1, :])
                vector.drain()
                if kind == "ln":
                    vector.tensor_mul(out=R["mur"][0:1, :], in0=R["mu"][0:1, :],
                                      in1=R["r"][0:1, :])
                b.mark(vector, b.tagv, "ve", 1)


def emit_normalize(b, K, X, dst32, dst_bf, kind="ln", wcol=None):
    nc, R = b.nc, K["R"]
    with contextlib.ExitStack() as sc:
        pr = [b.ps(sc, "no_pr", [128, TB]) for _ in range(2)]
        pb = [b.ps(sc, "no_pb", [128, TB]) for _ in range(2)]
        rbs = [b.sb(sc, "no_rbs", [128, TB], F32) for _ in range(2)]
        (te0, sc0, ve0) = b.base("te", "sc", "ve")
        with nc.Block() as block:
            @block.tensor
            def _(tensor):
                for tb in range(NTB):
                    if tb >= 2:
                        tensor.wait_ge(b.sem("ve"), ve0 + tb - 1)
                    ins = tensor.matmul(pr[tb % 2][:, :], K["ones_row32"][:],
                                        R["r"][0:1, tb * TB:(tb + 1) * TB],
                                        start=True, stop=True)
                    if kind == "ln":
                        ins = tensor.matmul(pb[tb % 2][:, :], K["ones_row32"][:],
                                            R["mur"][0:1, tb * TB:(tb + 1) * TB],
                                            start=True, stop=True)
                    b.inc(ins, "te", 1)

            @block.scalar
            def _(scalar):
                if kind == "ln":
                    for tb in range(NTB):
                        scalar.wait_ge(b.sem("te"), te0 + tb + 1)
                        if tb >= 2:
                            scalar.wait_ge(b.sem("ve"), ve0 + tb - 1)
                        scalar.copy(out=rbs[tb % 2][:], in_=pr[tb % 2][:, :])
                        b.mark(scalar, b.tags, "sc", 1)

            @block.vector
            def _(vector):
                for tb in range(NTB):
                    sl = slice(tb * TB, (tb + 1) * TB)
                    if kind == "ln":
                        vector.wait_ge(b.sem("sc"), sc0 + tb + 1)
                    else:
                        vector.wait_ge(b.sem("te"), te0 + tb + 1)
                    for c in range(NCH):
                        if kind == "ln":
                            tgt = (dst32[c] if dst32 else dst_bf[c])[:, sl]
                            vector.tensor_mul(out=tgt, in0=X[c][:, sl], in1=rbs[tb % 2][:])
                            vector.drain()
                            vector.scalar_tensor_tensor(out=tgt, in0=tgt, scalar=1.0,
                                                        in1=pb[tb % 2][:, :],
                                                        op0=ALU.mult, op1=ALU.subtract)
                            vector.drain()
                            if dst32 is not None and dst_bf is not None:
                                vector.tensor_copy(out=dst_bf[c][:, sl], in_=tgt)
                                vector.drain()
                        else:
                            vector.scalar_tensor_tensor(out=dst_bf[c][:, sl],
                                                        in0=X[c][:, sl], scalar=wcol[c],
                                                        in1=pr[tb % 2][:, :],
                                                        op0=ALU.mult, op1=ALU.mult)
                    b.mark(vector, b.tagv, "ve", 1)



def emit_bcast_full(b, K, row_ap, dst_full):
    nc = b.nc
    with contextlib.ExitStack() as sc:
        pbc = [b.ps(sc, "bc_ps", [128, TB]) for _ in range(2)]
        (te0, sc0) = b.base("te", "sc")
        with nc.Block() as block:
            @block.tensor
            def _(tensor):
                for tb in range(NTB):
                    if tb >= 2:
                        tensor.wait_ge(b.sem("sc"), sc0 + tb - 1)
                    ins = tensor.matmul(pbc[tb % 2][:, :], K["ones_row32"][:],
                                        row_ap[0:1, tb * TB:(tb + 1) * TB],
                                        start=True, stop=True)
                    b.inc(ins, "te", 1)

            @block.scalar
            def _(scalar):
                for tb in range(NTB):
                    scalar.wait_ge(b.sem("te"), te0 + tb + 1)
                    scalar.copy(out=dst_full[:, tb * TB:(tb + 1) * TB],
                                in_=pbc[tb % 2][:, :])
                    b.mark(scalar, b.tags, "sc", 1)


def emit_global_max(b, K, col, dst11, cc_in, cc_out, n_cores, sc_):
    nc = b.nc
    tp = b.ps(sc_, "gm_tp", [128, 128])
    loc = b.sb(sc_, "gm_loc", [1, 1], F32)
    row = b.sb(sc_, "gm_row", [1, 128], F32)
    with nc.Block() as block:
        @block.tensor
        def _(tensor):
            ins = tensor.transpose(tp[0:1, 0:128], col[:], K["ident32"][:])
            b.inc(ins, "te", 1)

        @block.vector
        def _(vector):
            b.wait(vector, "te")
            vector.tensor_copy(out=row[:], in_=tp[0:1, 0:128])
            vector.drain()
            vector.tensor_reduce(out=loc[:], in_=row[:], axis=AX.X, op=ALU.max)
            b.mark(vector, b.tagv, "ve", 1)

    with nc.Block() as block:
        @block.sync
        def _(sync):
            b.dma(sync, cc_in, loc[:], "ccd")

        @block.gpsimd
        def _(gpsimd):
            b.wait(gpsimd, "ccd")
            ins = gpsimd.collective_compute("AllReduce", ALU.max,
                                            replica_groups=[list(range(n_cores))],
                                            ins=[cc_in], outs=[cc_out])
            b.inc(ins, "cc", 1)
            b.wait(gpsimd, "cc")

    with nc.Block() as block:
        @block.sync
        def _(sync):
            b.dma(sync, dst11[:], cc_out, "ccd")
            b.wait(sync, "ccd")


def emit_scale_rows(b, K, clip11, gmax11, inv_col, s_col, sc_):
    nc = b.nc
    t = b.sb(sc_, "sr_t", [1, 4], F32)
    pbc = b.ps(sc_, "sr_ps", [128, 2])
    with nc.Block() as block:
        @block.scalar
        def _(scalar):
            scalar.activation(out=t[:, 0:1], in_=clip11, func=AF.Sigmoid)
            b.mark(scalar, b.tags, "sc", 1)

        @block.vector
        def _(vector):
            b.wait(vector, "sc")
            vector.tensor_mul(out=t[:, 1:2], in0=t[:, 0:1], in1=gmax11[:])
            vector.drain()
            vector.tensor_scalar_mul(out=t[:, 1:2], in0=t[:, 1:2], scalar1=1.0 / QMAX)
            vector.drain()
            vector.reciprocal(out=t[:, 2:3], in_=t[:, 1:2])
            b.mark(vector, b.tagv, "ve", 1)

    with nc.Block() as block:
        @block.tensor
        def _(tensor):
            tensor.matmul(pbc[:, 0:1], K["ones_row32"][:], t[:, 1:2], start=True, stop=True)
            ins = tensor.matmul(pbc[:, 1:2], K["ones_row32"][:], t[:, 2:3], start=True, stop=True)
            b.inc(ins, "te", 1)

        @block.vector
        def _(vector):
            b.wait(vector, "te")
            vector.tensor_copy(out=s_col[:], in_=pbc[:, 0:1])
            vector.drain()
            vector.tensor_copy(out=inv_col[:], in_=pbc[:, 1:2])
            b.mark(vector, b.tagv, "ve", 1)


def emit_quant_away(b, src, dst_bf, W, Q, inv_col):
    nc = b.nc
    (ve0,) = b.base("ve")
    with nc.Block() as block:
        @block.scalar
        def _(scalar):
            scalar.activation(out=Q["a"][:, 0:W], in_=src, func=AF.Abs, scale=inv_col[:])
            scalar.activation(out=Q["sgn"][:, 0:W], in_=src, func=AF.Sign)
            b.mark(scalar, b.tags, "sc", 1)
            scalar.wait_ge(b.sem("ve"), ve0 + 1)
            scalar.copy(out=Q["c1f"][:, 0:W], in_=Q["i8"][:, 0:W])
            b.mark(scalar, b.tags, "sc", 1)

        @block.vector
        def _(vector):
            b.wait(vector, "sc", b.cnt["sc"] - 1)
            vector.tensor_scalar(out=Q["i8"][:, 0:W], in0=Q["a"][:, 0:W],
                                 scalar1=127.49, scalar2=None, op0=ALU.min)
            b.mark(vector, b.tagv, "ve", 1)

    with nc.Block() as block:
        @block.vector
        def _(vector):
            vector.scalar_tensor_tensor(out=Q["t1"][:, 0:W], in0=Q["a"][:, 0:W],
                                        scalar=1.0, in1=Q["c1f"][:, 0:W],
                                        op0=ALU.mult, op1=ALU.subtract)
            vector.drain()
            vector.scalar_tensor_tensor(out=Q["t1"][:, 0:W], in0=Q["t1"][:, 0:W],
                                        scalar=0.5, in1=Q["c1f"][:, 0:W],
                                        op0=ALU.is_ge, op1=ALU.add)
            vector.drain()
            vector.scalar_tensor_tensor(out=Q["t1"][:, 0:W], in0=Q["t1"][:, 0:W],
                                        scalar=128.0, in1=Q["sgn"][:, 0:W],
                                        op0=ALU.min, op1=ALU.mult)
            vector.drain()
            vector.tensor_scalar(out=dst_bf, in0=Q["t1"][:, 0:W],
                                 scalar1=127.0, scalar2=-128.0, op0=ALU.min, op1=ALU.max)
            b.mark(vector, b.tagv, "ve", 1)


def emit_l0_weight_quant(b, K, w_gate, w_up, w_down, qgT, quT, qdT,
                         sg_sc, su_sc, sd_sc):
    nc = b.nc
    jobs = [(w_gate, qgT, sg_sc, NI0, D), (w_up, quT, su_sc, NI0, D),
            (w_down, qdT, sd_sc, NCH, I0)]
    with contextlib.ExitStack() as sc:
        wt = [b.sb(sc, "wq_w", [128, I0], F32) for _ in range(2)]
        a_t = b.sb(sc, "wq_a", [128, I0], F32)
        i8t = b.sb(sc, "wq_i8", [128, I0], I8)
        odd8 = b.sb(sc, "wq_o8", [128, I0], I8)
        c1f = b.sb(sc, "wq_c1f", [128, I0], F32)
        oddf = b.sb(sc, "wq_oddf", [128, I0], F32)
        dl = b.sb(sc, "wq_dl", [128, I0], F32)
        gt = b.sb(sc, "wq_gt", [128, I0], F32)
        sgn = b.sb(sc, "wq_sgn", [128, I0], F32)
        qt = b.sb(sc, "wq_q", [128, I0], BF16)
        rmx = b.sb(sc, "wq_rmx", [128, 1], F32)
        inw = b.sb(sc, "wq_inw", [128, 1], F32)
        tpe = [b.sb(sc, "wq_tpe", [128, 128], BF16) for _ in range(2)]
        tp_ps = [b.ps(sc, "wq_tp", [128, 128], BF16) for _ in range(2)]

        for (src, dstT, scs, nrc, fd) in jobs:
            nsub = fd // 128
            for rc in range(nrc):
                wbuf = wt[rc % 2]
                fsl = slice(0, fd)
                with nc.Block() as block:
                    @block.sync
                    def _(sync):
                        b.dma(sync, wbuf[:, fsl], src[rc * 128:(rc + 1) * 128, :], "dma")

                    @block.vector
                    def _(vector):
                        b.wait(vector, "dma")
                        vector.tensor_reduce(out=rmx[:], in_=wbuf[:, fsl], axis=AX.X,
                                             op=ALU.max, apply_absolute_value=True)
                        vector.drain()
                        vector.tensor_scalar_mul(out=scs[:, rc:rc + 1], in0=rmx[:],
                                                 scalar1=1.0 / QMAX)
                        vector.reciprocal(out=inw[:], in_=rmx[:])
                        vector.drain()
                        vector.tensor_scalar_mul(out=inw[:], in0=inw[:], scalar1=QMAX)
                        b.mark(vector, b.tagv, "ve", 1)

                    @block.scalar
                    def _(scalar):
                        b.wait(scalar, "ve")
                        scalar.activation(out=a_t[:, fsl], in_=wbuf[:, fsl],
                                          func=AF.Abs, scale=inw[:])
                        scalar.activation(out=sgn[:, fsl], in_=wbuf[:, fsl], func=AF.Sign)
                        b.mark(scalar, b.tags, "sc", 1)

                with nc.Block() as block:
                    @block.vector
                    def _(vector):
                        vector.tensor_scalar(out=i8t[:, fsl], in0=a_t[:, fsl],
                                             scalar1=127.49, scalar2=None, op0=ALU.min)
                        vector.drain()
                        vector.tensor_single_scalar(out=odd8[:, fsl], in_=i8t[:, fsl],
                                                    scalar=1, op=ALU.bitwise_and)
                        b.mark(vector, b.tagv, "ve", 1)

                    @block.scalar
                    def _(scalar):
                        b.wait(scalar, "ve")
                        scalar.copy(out=c1f[:, fsl], in_=i8t[:, fsl])
                        scalar.copy(out=oddf[:, fsl], in_=odd8[:, fsl])
                        b.mark(scalar, b.tags, "sc", 1)

                with nc.Block() as block:
                    @block.vector
                    def _(vector):
                        vector.scalar_tensor_tensor(out=dl[:, fsl], in0=a_t[:, fsl],
                                                    scalar=1.0, in1=c1f[:, fsl],
                                                    op0=ALU.mult, op1=ALU.subtract)
                        vector.drain()
                        vector.tensor_single_scalar(out=gt[:, fsl], in_=dl[:, fsl],
                                                    scalar=0.5, op=ALU.is_gt)
                        vector.drain()
                        vector.tensor_single_scalar(out=dl[:, fsl], in_=dl[:, fsl],
                                                    scalar=0.5, op=ALU.is_equal)
                        vector.drain()
                        vector.tensor_mul(out=dl[:, fsl], in0=dl[:, fsl], in1=oddf[:, fsl])
                        vector.drain()
                        vector.tensor_add(out=gt[:, fsl], in0=gt[:, fsl], in1=dl[:, fsl])
                        vector.drain()
                        vector.tensor_add(out=gt[:, fsl], in0=gt[:, fsl], in1=c1f[:, fsl])
                        vector.drain()
                        vector.tensor_mul(out=qt[:, fsl], in0=gt[:, fsl], in1=sgn[:, fsl])
                        b.mark(vector, b.tagv, "ve", 1)

                (vt0, tt0) = b.base("vt", "tt")
                sd_par0 = [b.base("sdma0")[0], b.base("sdma1")[0]]
                with nc.Block() as block:
                    @block.tensor
                    def _(tensor):
                        for s in range(nsub):
                            if s >= 2:
                                tensor.wait_ge(b.sem("vt"), vt0 + s - 1)
                            ins = tensor.transpose(tp_ps[s % 2][:, :],
                                             qt[:, s * 128:(s + 1) * 128], K["ident"][:])
                            b.inc(ins, "tt", 1)

                    @block.vector
                    def _(vector):
                        for s in range(nsub):
                            vector.wait_ge(b.sem("tt"), tt0 + s + 1)
                            if s >= 2:
                                vector.wait_ge(b.sem(f"sdma{s % 2}"),
                                               sd_par0[s % 2] + 16 * ((s - 2) // 2 + 1))
                            ins = vector.tensor_copy(out=tpe[s % 2][:], in_=tp_ps[s % 2][:])
                            b.inc(ins, "vt", 1)
                            vector.drain()

                    @block.scalar
                    def _(scalar):
                        for s in range(nsub):
                            scalar.wait_ge(b.sem("vt"), vt0 + s + 1)
                            b.dma(scalar,
                                  dstT[s * 128:(s + 1) * 128, rc * 128:(rc + 1) * 128],
                                  tpe[s % 2][:], f"sdma{s % 2}")
                        b.wait(scalar, "sdma0")
                        b.wait(scalar, "sdma1")


def emit_l0_ffn(b, K, X, s_clip, qgT, quT, qdT, sg_sc, su_sc, sd_sc,
                h_spill, CC, n_cores):
    nc = b.nc
    with contextlib.ExitStack() as osc:
        gmax = b.sb(osc, "ff_gmax", [1, 1], F32)
        inv_col = b.sb(osc, "ff_inv", [128, 1], F32)
        sx_col = b.sb(osc, "ff_sxc", [128, 1], F32)
        hgmax = b.sb(osc, "ff_hgm", [1, 1], F32)
        hinv_col = b.sb(osc, "ff_hinv", [128, 1], F32)
        sh_col = b.sb(osc, "ff_shc", [128, 1], F32)
        sgo = b.sb(osc, "ff_sgo", [128, NI0], F32)
        suo = b.sb(osc, "ff_suo", [128, NI0], F32)
        sdo = b.sb(osc, "ff_sdo", [128, NCH], F32)
        hmax = b.sb(osc, "ff_hmax", [128, 1], F32)
        xq = [b.sb(osc, "ff_xq", [128, T], BF16) for _ in range(NCH)]

        emit_stats(b, X, K, "ln")
        with contextlib.ExitStack() as sc1:
            rbs_full = b.sb(sc1, "ff_rbs", [128, T], F32)
            mbs_full = b.sb(sc1, "ff_mbs", [128, T], F32)
            lnc = b.sb(sc1, "ff_lnc", [128, T], F32)
            amax = b.sb(sc1, "ff_amax", [128, 1], F32)
            emit_bcast_full(b, K, K["R"]["r"], rbs_full)
            emit_bcast_full(b, K, K["R"]["mur"], mbs_full)

            with nc.Block() as block:
                @block.vector
                def _(vector):
                    for c in range(NCH):
                        vector.tensor_mul(out=lnc[:], in0=X[c][:], in1=rbs_full[:])
                        vector.drain()
                        vector.tensor_sub(out=lnc[:], in0=lnc[:], in1=mbs_full[:])
                        vector.drain()
                        dst = amax if c == 0 else K["col1"]
                        vector.tensor_reduce(out=dst[:], in_=lnc[:], axis=AX.X,
                                             op=ALU.max, apply_absolute_value=True)
                        vector.drain()
                        if c > 0:
                            vector.tensor_max(out=amax[:], in0=amax[:], in1=K["col1"][:])
                    b.mark(vector, b.tagv, "ve", 1)
            emit_global_max(b, K, amax, gmax, CC["in"][CC["i"]], CC["out"][CC["i"]],
                            n_cores, sc1)
            CC["i"] += 1
            emit_scale_rows(b, K, s_clip[:, 0:1], gmax, inv_col, sx_col, sc1)

            Q = {"a": b.sb(sc1, "ffq_a", [128, T], F32),
                 "i8": b.sb(sc1, "ffq_i8", [128, T], I8),
                 "c1f": b.sb(sc1, "ffq_c1f", [128, T], F32),
                 "sgn": b.sb(sc1, "ffq_sgn", [128, T], F32),
                 "t1": b.sb(sc1, "ffq_t1", [128, T], F32)}
            for c in range(NCH):
                with nc.Block() as block:
                    @block.vector
                    def _(vector):
                        vector.tensor_mul(out=lnc[:], in0=X[c][:], in1=rbs_full[:])
                        vector.drain()
                        vector.tensor_sub(out=lnc[:], in0=lnc[:], in1=mbs_full[:])
                        b.mark(vector, b.tagv, "ve", 1)
                emit_quant_away(b, lnc[:], xq[c][:], T, Q, inv_col)

        with nc.Block() as block:
            @block.vector
            def _(vector):
                for i in range(NI0):
                    vector.tensor_mul(out=sgo[:, i:i + 1], in0=sg_sc[:, i:i + 1],
                                      in1=sx_col[:])
                    vector.tensor_mul(out=suo[:, i:i + 1], in0=su_sc[:, i:i + 1],
                                      in1=sx_col[:])
                b.mark(vector, b.tagv, "ve", 1)

        with contextlib.ExitStack() as sc2:
            lg = [b.sb(sc2, "ff_lg", [128, D], BF16) for _ in range(2)]
            lu = [b.sb(sc2, "ff_lu", [128, D], BF16) for _ in range(2)]
            gs = b.sb(sc2, "ff_gs", [128, TB], F32)
            us = b.sb(sc2, "ff_us", [128, TB], F32)
            ht = [b.sb(sc2, "ff_ht", [128, TB], F32) for _ in range(2)]
            pg = [b.ps(sc2, "ff_pg", [128, TB]) for _ in range(2)]
            pu = [b.ps(sc2, "ff_pu", [128, TB]) for _ in range(2)]

            niter = NTB * NI0
            (dm0, te0, sc0, ve0) = b.base("dma", "te", "sc", "ve")
            hs_par0 = [b.base("hsp0")[0], b.base("hsp1")[0]]
            wld0 = [b.base("wld0")[0], b.base("wld1")[0]]
            with nc.Block() as block:
                @block.sync
                def _(sync):
                    it = 0
                    for tb in range(NTB):
                        for ic in range(NI0):
                            sync.wait_ge(b.sem("te"), te0 + max(0, it - 1))
                            csl = slice(ic * 128, (ic + 1) * 128)
                            b.dma(sync,
                                  lg[it % 2][:].rearrange("p (s c) -> p s c", c=128),
                                  qgT[:, csl].rearrange("(s p) c -> p s c", p=128),
                                  f"wld{it % 2}")
                            b.dma(sync,
                                  lu[it % 2][:].rearrange("p (s c) -> p s c", c=128),
                                  quT[:, csl].rearrange("(s p) c -> p s c", p=128),
                                  f"wld{it % 2}")
                            it += 1

                @block.tensor
                def _(tensor):
                    it = 0
                    for tb in range(NTB):
                        sl = slice(tb * TB, (tb + 1) * TB)
                        for ic in range(NI0):
                            tensor.wait_ge(b.sem(f"wld{it % 2}"), wld0[it % 2] + 32 * (it // 2 + 1))
                            if it >= 2:
                                tensor.wait_ge(b.sem("sc"), sc0 + 2 * (it - 1))
                            for s in range(NCH):
                                tensor.matmul(pg[it % 2][:, :],
                                              lg[it % 2][:, s * 128:(s + 1) * 128],
                                              xq[s][:, sl],
                                              start=(s == 0), stop=(s == NCH - 1))
                            for s in range(NCH):
                                ins = tensor.matmul(pu[it % 2][:, :],
                                                    lu[it % 2][:, s * 128:(s + 1) * 128],
                                                    xq[s][:, sl],
                                                    start=(s == 0), stop=(s == NCH - 1))
                            b.inc(ins, "te", 1)
                            it += 1

                @block.scalar
                def _(scalar):
                    it = 0
                    spills = []
                    for tb in range(NTB):
                        for ic in range(NI0):
                            spills.append((ic, tb))
                            scalar.wait_ge(b.sem("te"), te0 + it + 1)
                            if it >= 1:
                                scalar.wait_ge(b.sem("ve"), ve0 + it)
                            scalar.activation(out=gs[:], in_=pg[it % 2][:, :],
                                              func=AF.Silu, scale=sgo[:, ic:ic + 1])
                            scalar.activation(out=us[:], in_=pu[it % 2][:, :],
                                              func=AF.Copy, scale=suo[:, ic:ic + 1])
                            b.mark(scalar, b.tags, "sc", 2)
                            if it >= 1:
                                pic, ptb = spills[it - 1]
                                b.dma(scalar, h_spill[pic * 128:(pic + 1) * 128,
                                                      ptb * TB:(ptb + 1) * TB],
                                      ht[(it - 1) % 2][:], f"hsp{(it - 1) % 2}")
                            it += 1
                    scalar.wait_ge(b.sem("ve"), ve0 + it)
                    pic, ptb = spills[it - 1]
                    b.dma(scalar, h_spill[pic * 128:(pic + 1) * 128,
                                          ptb * TB:(ptb + 1) * TB],
                          ht[(it - 1) % 2][:], f"hsp{(it - 1) % 2}")
                    b.wait(scalar, "hsp0")
                    b.wait(scalar, "hsp1")

                @block.vector
                def _(vector):
                    it = 0
                    for tb in range(NTB):
                        for ic in range(NI0):
                            vector.wait_ge(b.sem("sc"), sc0 + 2 * (it + 1))
                            if it >= 2:
                                vector.wait_ge(b.sem(f"hsp{it % 2}"),
                                               hs_par0[it % 2] + 16 * ((it - 2) // 2 + 1))
                            vector.tensor_mul(out=ht[it % 2][:], in0=gs[:], in1=us[:])
                            vector.drain()
                            vector.tensor_reduce(out=K["col1"][:], in_=ht[it % 2][:],
                                                 axis=AX.X, op=ALU.max,
                                                 apply_absolute_value=True)
                            vector.drain()
                            if it == 0:
                                vector.tensor_copy(out=hmax[:], in_=K["col1"][:])
                            else:
                                vector.tensor_max(out=hmax[:], in0=hmax[:],
                                                  in1=K["col1"][:])
                            b.mark(vector, b.tagv, "ve", 1)
                            it += 1

        with contextlib.ExitStack() as sc3:
            emit_global_max(b, K, hmax, hgmax, CC["in"][CC["i"]], CC["out"][CC["i"]],
                            n_cores, sc3)
            CC["i"] += 1
            emit_scale_rows(b, K, s_clip[:, 1:2], hgmax, hinv_col, sh_col, sc3)
        with nc.Block() as block:
            @block.vector
            def _(vector):
                for hc in range(NCH):
                    vector.tensor_mul(out=sdo[:, hc:hc + 1], in0=sd_sc[:, hc:hc + 1],
                                      in1=sh_col[:])
                vector.drain()
                vector.tensor_scalar_mul(out=sdo[:], in0=sdo[:], scalar1=0.5)
                b.mark(vector, b.tagv, "ve", 1)

        with contextlib.ExitStack() as sc4:
            hb = [b.sb(sc4, "ff_hb", [128, TB], F32) for _ in range(2)]
            qh = [b.sb(sc4, "ff_qh", [128, TB], BF16) for _ in range(2)]
            ld = [b.sb(sc4, "ff_ld", [128, D], BF16) for _ in range(2)]
            pd = [b.ps(sc4, "ff_pd", [128, TB]) for _ in range(NCH)]
            Q2 = {"a": b.sb(sc4, "ffq2_a", [128, TB], F32),
                  "i8": b.sb(sc4, "ffq2_i8", [128, TB], I8),
                  "c1f": b.sb(sc4, "ffq2_c1f", [128, TB], F32),
                  "sgn": b.sb(sc4, "ffq2_sgn", [128, TB], F32),
                  "t1": b.sb(sc4, "ffq2_t1", [128, TB], F32)}

            for tb in range(NTB):
                sl = slice(tb * TB, (tb + 1) * TB)
                for ic in range(NI0):
                    it = ic
                    with nc.Block() as block:
                        @block.sync
                        def _(sync):
                            b.dma(sync, hb[it % 2][:],
                                  h_spill[ic * 128:(ic + 1) * 128, sl], "dma")
                            b.dma(sync, ld[it % 2][:],
                                  qdT[ic * 128:(ic + 1) * 128, :], "dma")

                        @block.vector
                        def _(vector):
                            b.wait(vector, "dma")
                            b.mark(vector, b.tagv, "ve", 1)
                    emit_quant_away(b, hb[it % 2][:], qh[it % 2][:], TB, Q2, hinv_col)
                    with nc.Block() as block:
                        @block.tensor
                        def _(tensor):
                            for hc in range(NCH):
                                ins = tensor.matmul(pd[hc][:, :],
                                                    ld[it % 2][:, hc * 128:(hc + 1) * 128],
                                                    qh[it % 2][:],
                                                    start=(ic == 0), stop=(ic == NI0 - 1))
                            b.inc(ins, "te", 1)
                with nc.Block() as block:
                    @block.vector
                    def _(vector):
                        for hc in range(NCH):
                            vector.scalar_tensor_tensor(out=X[hc][:, sl], in0=pd[hc][:, :],
                                                        scalar=sdo[:, hc:hc + 1],
                                                        in1=X[hc][:, sl],
                                                        op0=ALU.mult, op1=ALU.add)
                        b.mark(vector, b.tagv, "ve", 1)



def emit_l0_attn(b, K, X, WrT, WfT, att_r_col_h, gamma_col_h, beta_col_h, CC):
    nc = b.nc
    emit_stats(b, X, K, "ln")
    with contextlib.ExitStack() as osc:
        rbs = b.sb(osc, "at_rbs", [128, T], F32)
        mbs = b.sb(osc, "at_mbs", [128, T], F32)
        emit_bcast_full(b, K, K["R"]["r"], rbs)
        emit_bcast_full(b, K, K["R"]["mur"], mbs)
        fr = [b.sb(osc, "at_fr", [128, T], BF16) for _ in range(NCH)]
        wbuf = [b.sb(osc, "at_w", [128, D], BF16) for _ in range(NCH)]
        attr = b.sb(osc, "at_attr", [128, 1], F32)
        gcol = b.sb(osc, "at_g", [128, NCH], F32)
        bcol = b.sb(osc, "at_b", [128, NCH], F32)
        srows = b.sb(osc, "at_sr", [8, T], F32)
        prows = b.sb(osc, "at_p", [8, T], BF16)
        ctx_sb = b.sb(osc, "at_ctx", [128, NCH], BF16)
        fh_sb = b.sb(osc, "at_fh", [128, NCH], F32)
        muy, vy, ry = K["R"]["mu"], K["R"]["msq"], K["R"]["r"]
        sc11 = b.sb(osc, "at_s11", [1, 4], F32)

        with nc.Block() as block:
            @block.gpsimd
            def _(gpsimd):
                for s in range(NCH):
                    b.dma(gpsimd, wbuf[s][:], WrT[s * 128:(s + 1) * 128, :], f"gld{s % 2}")

            @block.sync
            def _(sync):
                b.dma(sync, attr[:], att_r_col_h, "dma")
                b.dma(sync, gcol[:], gamma_col_h, "dma")
                b.dma(sync, bcol[:], beta_col_h, "dma")
                b.wait(sync, "dma")

        with contextlib.ExitStack() as sc2:
            hbf = [b.sb(sc2, "at_hbf", [128, TB], BF16) for _ in range(NCH)]
            frp = [b.ps(sc2, "at_frp", [128, TB]) for _ in range(2)]
            (ve0, te0, sc0) = b.base("ve", "te", "sc")
            gl = [b.cnt["gld0"], b.cnt["gld1"]]
            with nc.Block() as block:
                @block.vector
                def _(vector):
                    for tb in range(NTB):
                        sl = slice(tb * TB, (tb + 1) * TB)
                        if tb >= 1:
                            vector.wait_ge(b.sem("te"), te0 + tb * NCH)
                        for c in range(NCH):
                            vector.tensor_mul(out=K["ftmp"][:, 0:TB], in0=X[c][:, sl],
                                              in1=rbs[:, sl])
                            vector.drain()
                            vector.tensor_sub(out=hbf[c][:], in0=K["ftmp"][:, 0:TB],
                                              in1=mbs[:, sl])
                        b.mark(vector, b.tagv, "ve", 1)

                @block.tensor
                def _(tensor):
                    tensor.wait_ge(b.sem("gld0"), gl[0])
                    tensor.wait_ge(b.sem("gld1"), gl[1])
                    for tb in range(NTB):
                        tensor.wait_ge(b.sem("ve"), ve0 + tb + 1)
                        for oc in range(NCH):
                            it = tb * NCH + oc
                            if it >= 2:
                                tensor.wait_ge(b.sem("sc"), sc0 + it - 1)
                            for s in range(NCH):
                                ins = tensor.matmul(frp[it % 2][:, :],
                                                    wbuf[s][:, oc * 128:(oc + 1) * 128],
                                                    hbf[s][:],
                                                    start=(s == 0), stop=(s == NCH - 1))
                            b.inc(ins, "te", 1)

                @block.scalar
                def _(scalar):
                    for tb in range(NTB):
                        for oc in range(NCH):
                            it = tb * NCH + oc
                            scalar.wait_ge(b.sem("te"), te0 + it + 1)
                            scalar.copy(out=fr[oc][:, tb * TB:(tb + 1) * TB],
                                        in_=frp[it % 2][:, :])
                            b.mark(scalar, b.tags, "sc", 1)

        with contextlib.ExitStack() as sc3:
            lk = b.sb(sc3, "at_lk", [128, TB], F32)
            lk2 = b.sb(sc3, "at_lk2", [128, TB], F32)
            srstage = [b.sb(sc3, "at_srst", [1, TB], F32) for _ in range(2)]
            srp = b.ps(sc3, "at_srp", [1, TB])
            (ve0, te0) = b.base("ve", "te")
            for h in range(NCH):
                for tb in range(NTB):
                    with nc.Block() as block:
                        @block.vector
                        def _(vector):
                            sl = slice(tb * TB, (tb + 1) * TB)
                            vector.tensor_scalar_mul(out=lk2[:], in0=fr[h][:, sl],
                                                     scalar1=0.01)
                            vector.drain()
                            vector.tensor_max(out=lk[:], in0=fr[h][:, sl], in1=lk2[:])
                            b.mark(vector, b.tagv, "ve", 1)

                    with nc.Block() as block:
                        @block.tensor
                        def _(tensor):
                            ins = tensor.matmul(srp[0:1, :], attr[:], lk[:],
                                                start=True, stop=True)
                            b.inc(ins, "te", 1)

                        @block.vector
                        def _(vector):
                            b.wait(vector, "te")
                            it_sr = h * NTB + tb
                            if it_sr >= 2:
                                b.wait(vector, f"srd{it_sr % 2}")
                            vector.tensor_copy(out=srstage[(h * NTB + tb) % 2][0:1, :], in_=srp[0:1, :])
                            b.mark(vector, b.tagv, "ve", 1)

                        @block.scalar
                        def _(scalar):
                            b.wait(scalar, "ve")
                            b.dma(scalar, srows[h:h + 1, tb * TB:(tb + 1) * TB],
                                  srstage[(h * NTB + tb) % 2][0:1, :],
                                  f"srd{(h * NTB + tb) % 2}")

        (ve0s, sc0s) = b.base("ve", "sc")
        with nc.Block() as block:
            @block.vector
            def _(vector):
                b.wait(vector, "srd0")
                b.wait(vector, "srd1")
                vector.tensor_reduce(out=K["col8a"][0:8, :], in_=srows[:], axis=AX.X,
                                     op=ALU.max)
                vector.drain()
                vector.tensor_scalar_mul(out=K["col8a"][0:8, :], in0=K["col8a"][0:8, :],
                                         scalar1=-1.0)
                b.mark(vector, b.tagv, "ve", 1)
                vector.wait_ge(b.sem("sc"), sc0s + 1)
                vector.reciprocal(out=K["col8a"][0:8, :], in_=K["col8b"][0:8, :])
                vector.drain()
                vector.tensor_scalar_mul(out=prows[:], in0=srows[:],
                                         scalar1=K["col8a"][0:8, :])
                b.mark(vector, b.tagv, "ve", 1)

            @block.scalar
            def _(scalar):
                scalar.wait_ge(b.sem("ve"), ve0s + 1)
                scalar.activation(out=srows[:], in_=srows[:], func=AF.Exp,
                                  bias=K["col8a"][0:8, :], scale=1.0,
                                  accum_out=K["col8b"][0:8, :])
                b.mark(scalar, b.tags, "sc", 1)

        with contextlib.ExitStack() as sc4:
            ptp = b.ps(sc4, "at_ptp", [128, 8], BF16)
            pT = b.sb(sc4, "at_pT", [128, 128], BF16)
            ftp = [b.ps(sc4, "at_ftp", [128, 128], BF16) for _ in range(2)]
            fts = [b.sb(sc4, "at_fts", [128, 128], BF16) for _ in range(2)]
            cxp = b.ps(sc4, "at_cxp", [128, 8])
            (te0, ve0) = b.base("te", "ve")
            with nc.Block() as block:
                @block.tensor
                def _(tensor):
                    for m in range(16):
                        if m >= 1:
                            tensor.wait_ge(b.sem("ve"), ve0 + m)
                        ins = tensor.transpose(ptp[:, :],
                                               prows[:, m * 128:(m + 1) * 128],
                                               K["ident"][0:8, 0:8])
                        b.inc(ins, "te", 1)

                @block.vector
                def _(vector):
                    for m in range(16):
                        vector.wait_ge(b.sem("te"), te0 + m + 1)
                        vector.tensor_copy(out=pT[:, m * 8:(m + 1) * 8], in_=ptp[:, :])
                        b.mark(vector, b.tagv, "ve", 1)

            (te0, ve0, tt0, vt0) = b.base("te", "ve", "tt", "vt")
            with nc.Block() as block:
                @block.tensor
                def _(tensor):
                    it = 0
                    for h in range(NCH):
                        for m in range(16):
                            if it >= 2:
                                tensor.wait_ge(b.sem("vt"), vt0 + it - 1)
                            ins = tensor.transpose(ftp[it % 2][:, :],
                                                   fr[h][:, m * 128:(m + 1) * 128],
                                                   K["ident"][:])
                            b.inc(ins, "tt", 1)
                            tensor.wait_ge(b.sem("vt"), vt0 + it + 1)
                            ins = tensor.matmul(cxp[:, h:h + 1], fts[it % 2][:],
                                                pT[:, m * 8 + h:m * 8 + h + 1],
                                                start=(m == 0), stop=(m == 15))
                            b.inc(ins, "te", 1)
                            it += 1

                @block.vector
                def _(vector):
                    it = 0
                    for h in range(NCH):
                        for m in range(16):
                            vector.wait_ge(b.sem("tt"), tt0 + it + 1)
                            if it >= 2:
                                vector.wait_ge(b.sem("te"), te0 + it - 1)
                            ins = vector.tensor_copy(out=fts[it % 2][:], in_=ftp[it % 2][:])
                            b.inc(ins, "vt", 1)
                            vector.drain()
                            it += 1
                    vector.wait_ge(b.sem("te"), te0 + it)
                    vector.tensor_copy(out=ctx_sb[:], in_=cxp[:, :])
                    b.mark(vector, b.tagv, "ve", 1)


        with contextlib.ExitStack() as sc5:
            fhp = b.ps(sc5, "at_fhp", [128, 8])
            gl = None
            with nc.Block() as block:
                @block.gpsimd
                def _(gpsimd):
                    for s in range(NCH):
                        b.dma(gpsimd, wbuf[s][:], WfT[s * 128:(s + 1) * 128, :],
                              f"gld{s % 2}")

            with nc.Block() as block:
                @block.tensor
                def _(tensor):
                    b.wait(tensor, "gld0")
                    b.wait(tensor, "gld1")
                    for oc in range(NCH):
                        for ic in range(NCH):
                            ins = tensor.matmul(fhp[:, oc:oc + 1],
                                                wbuf[ic][:, oc * 128:(oc + 1) * 128],
                                                ctx_sb[:, ic:ic + 1],
                                                start=(ic == 0), stop=(ic == NCH - 1))
                    b.inc(ins, "te", 1)

                @block.vector
                def _(vector):
                    b.wait(vector, "te")
                    vector.tensor_copy(out=fh_sb[:], in_=fhp[:, :])
                    b.mark(vector, b.tagv, "ve", 1)

        with contextlib.ExitStack() as sc6:
            hsq = b.sb(sc6, "at_hsq", [128, TB], F32)
            htmp = b.sb(sc6, "at_htmp", [128, TB], F32)
            fhbits = b.sb(sc6, "at_fhb", [128, 2], F32)
            fb2 = b.sb(sc6, "at_fb2", [2, 128], F32)
            fb1 = b.sb(sc6, "at_fb1", [2, 1], F32)
            p_mu = [b.ps(sc6, "at_pmu", [1, TB])]
            p_sq = [b.ps(sc6, "at_psq", [1, TB])]
            p_hf = [b.ps(sc6, "at_phf", [1, TB])]
            tp2 = b.ps(sc6, "at_tp2", [128, 128])
            (ve0, te0, sc0) = b.base("ve", "te", "sc")
            with nc.Block() as block:
                @block.vector
                def _(vector):
                    vector.tensor_reduce(out=fhbits[:, 0:1], in_=fh_sb[:], axis=AX.X,
                                         op=ALU.add)
                    vector.drain()
                    vector.tensor_mul(out=K["ftmp"][:, 0:NCH], in0=fh_sb[:], in1=fh_sb[:])
                    vector.drain()
                    vector.tensor_reduce(out=fhbits[:, 1:2], in_=K["ftmp"][:, 0:NCH],
                                         axis=AX.X, op=ALU.add)
                    b.mark(vector, b.tagv, "ve", 1)

                @block.tensor
                def _(tensor):
                    b.wait(tensor, "ve")
                    ins = tensor.transpose(tp2[0:2, 0:128], fhbits[:, 0:2],
                                           K["ident32"][:])
                    b.inc(ins, "te", 1)

                @block.vector
                def _(vector):
                    b.wait(vector, "te")
                    vector.tensor_copy(out=fb2[:], in_=tp2[0:2, 0:128])
                    vector.drain()
                    vector.tensor_reduce(out=fb1[:], in_=fb2[:], axis=AX.X, op=ALU.add)
                    b.mark(vector, b.tagv, "ve", 1)

            with nc.Block() as block:
                @block.tensor
                def _(tensor):
                    ins = tensor.transpose(tp2[0:1, 0:2], fb1[:], K["ident32"][0:2, 0:2])
                    b.inc(ins, "te", 1)

                @block.vector
                def _(vector):
                    b.wait(vector, "te")
                    vector.tensor_copy(out=sc11[0:1, 0:2], in_=tp2[0:1, 0:2])
                    vector.drain()
                    vector.tensor_scalar_mul(out=sc11[0:1, 0:2], in0=sc11[0:1, 0:2],
                                             scalar1=1.0 / 1024.0)
                    b.mark(vector, b.tagv, "ve", 1)

            for tb in range(NTB):
                sl = slice(tb * TB, (tb + 1) * TB)
                for c in range(NCH):
                    with nc.Block() as block:
                        @block.vector
                        def _(vector):
                            vector.tensor_mul(out=htmp[:, 0:TB], in0=X[c][:, sl],
                                              in1=rbs[:, sl])
                            vector.drain()
                            vector.tensor_sub(out=htmp[:, 0:TB], in0=htmp[:, 0:TB],
                                              in1=mbs[:, sl])
                            b.mark(vector, b.tagv, "ve", 1)

                        @block.scalar
                        def _(scalar):
                            b.wait(scalar, "ve")
                            scalar.activation(out=hsq[:, 0:TB], in_=htmp[:, 0:TB],
                                              func=AF.Square)
                            b.mark(scalar, b.tags, "sc", 1)

                    with nc.Block() as block:
                        @block.tensor
                        def _(tensor):
                            tensor.matmul(p_mu[0][0:1, :], K["onesm32"][:],
                                          htmp[:, 0:TB],
                                          start=(c == 0), stop=(c == NCH - 1))
                            tensor.matmul(p_sq[0][0:1, :], K["onesm32"][:],
                                          hsq[:, 0:TB],
                                          start=(c == 0), stop=(c == NCH - 1))
                            ins = tensor.matmul(p_hf[0][0:1, :], fh_sb[:, c:c + 1],
                                                htmp[:, 0:TB],
                                                start=(c == 0), stop=(c == NCH - 1))
                            b.inc(ins, "te", 1)
                with nc.Block() as block:
                    @block.vector
                    def _(vector):
                        vector.tensor_copy(out=muy[0:1, sl], in_=p_mu[0][0:1, :])
                        vector.tensor_copy(out=vy[0:1, sl], in_=p_sq[0][0:1, :])
                        vector.tensor_copy(out=ry[0:1, sl], in_=p_hf[0][0:1, :])
                        b.mark(vector, b.tagv, "ve", 1)

            with nc.Block() as block:
                @block.vector
                def _(vector):
                    vector.tensor_single_scalar(out=muy[0:1, :], in_=muy[0:1, :],
                                                scalar=sc11[0:1, 0:1], op=ALU.add)
                    vector.drain()
                    vector.tensor_scalar_mul(out=ry[0:1, :], in0=ry[0:1, :],
                                             scalar1=2.0 / 1024.0)
                    vector.drain()
                    vector.tensor_add(out=vy[0:1, :], in0=vy[0:1, :], in1=ry[0:1, :])
                    vector.drain()
                    vector.tensor_single_scalar(out=vy[0:1, :], in_=vy[0:1, :],
                                                scalar=sc11[0:1, 1:2], op=ALU.add)
                    vector.drain()
                    vector.tensor_mul(out=ry[0:1, :], in0=muy[0:1, :], in1=muy[0:1, :])
                    vector.drain()
                    vector.tensor_sub(out=vy[0:1, :], in0=vy[0:1, :], in1=ry[0:1, :])
                    b.mark(vector, b.tagv, "ve", 1)

                @block.scalar
                def _(scalar):
                    b.wait(scalar, "ve")
                    scalar.activation(out=vy[0:1, :], in_=vy[0:1, :], func=AF.Sqrt,
                                      bias=K["eps11"][:], scale=1.0)
                    b.mark(scalar, b.tags, "sc", 1)

                @block.vector
                def _(vector):
                    b.wait(vector, "sc")
                    vector.reciprocal(out=ry[0:1, :], in_=vy[0:1, :])
                    b.mark(vector, b.tagv, "ve", 1)

        with contextlib.ExitStack() as sc7:
            pmy = [b.ps(sc7, "at_pmy", [128, TB]) for _ in range(2)]
            pry = [b.ps(sc7, "at_pry", [128, TB]) for _ in range(2)]
            rysb = [b.sb(sc7, "at_rysb", [128, TB], F32) for _ in range(2)]
            v1 = b.sb(sc7, "at_v1", [128, TB], F32)
            v2 = b.sb(sc7, "at_v2", [128, TB], F32)
            (te0, sc0, ve0) = b.base("te", "sc", "ve")
            with nc.Block() as block:
                @block.tensor
                def _(tensor):
                    for tb in range(NTB):
                        if tb >= 2:
                            tensor.wait_ge(b.sem("ve"), ve0 + tb - 1)
                        tensor.matmul(pmy[tb % 2][:, :], K["ones_row32"][:],
                                      muy[0:1, tb * TB:(tb + 1) * TB],
                                      start=True, stop=True)
                        ins = tensor.matmul(pry[tb % 2][:, :], K["ones_row32"][:],
                                            ry[0:1, tb * TB:(tb + 1) * TB],
                                            start=True, stop=True)
                        b.inc(ins, "te", 1)

                @block.scalar
                def _(scalar):
                    for tb in range(NTB):
                        scalar.wait_ge(b.sem("te"), te0 + tb + 1)
                        if tb >= 2:
                            scalar.wait_ge(b.sem("ve"), ve0 + tb - 1)
                        scalar.copy(out=rysb[tb % 2][:], in_=pry[tb % 2][:, :])
                        b.mark(scalar, b.tags, "sc", 1)

                @block.vector
                def _(vector):
                    for tb in range(NTB):
                        sl = slice(tb * TB, (tb + 1) * TB)
                        vector.wait_ge(b.sem("sc"), sc0 + tb + 1)
                        for c in range(NCH):
                            vector.tensor_mul(out=v1[:], in0=X[c][:, sl], in1=rbs[:, sl])
                            vector.drain()
                            vector.tensor_sub(out=v1[:], in0=v1[:], in1=mbs[:, sl])
                            vector.drain()
                            vector.scalar_tensor_tensor(out=v2[:], in0=v1[:],
                                                        scalar=fh_sb[:, c:c + 1],
                                                        in1=pmy[tb % 2][:, :],
                                                        op0=ALU.add, op1=ALU.subtract)
                            vector.drain()
                            vector.tensor_mul(out=v2[:], in0=v2[:], in1=rysb[tb % 2][:])
                            vector.drain()
                            vector.scalar_tensor_tensor(out=v2[:], in0=v2[:],
                                                        scalar=gcol[:, c:c + 1],
                                                        in1=X[c][:, sl],
                                                        op0=ALU.mult, op1=ALU.add)
                            vector.drain()
                            vector.tensor_single_scalar(out=X[c][:, sl], in_=v2[:],
                                                        scalar=bcol[:, c:c + 1],
                                                        op=ALU.add)
                        b.mark(vector, b.tagv, "ve", 1)


def build_program(n_cores=8, stages=1):
    nc = bass.Bass("TRN2", target_bir_lowering=False, debug=False)

    def hin(name, shape, dt=F32):
        return nc.dram_tensor(name, shape, dt, kind="ExternalInput").ap()

    def hscr(name, shape, dt):
        return nc.dram_tensor(name, shape, dt, kind="Internal").ap()

    x_in = hin("x_fm", [D, T])
    out_fm = nc.dram_tensor("out_fm", [D, T], F32, kind="ExternalOutput").ap()
    ident = hin("ident", [128, 128], BF16)
    ident32 = hin("ident32", [128, 128], F32)
    onesm = hin("onesm", [128, 1], BF16)
    ones_row32 = hin("ones_row32", [1, 128])
    w_gate = hin("w_gate", [I0, D])
    w_up = hin("w_up", [I0, D])
    w_down = hin("w_down", [D, I0])
    clip2 = hin("clip2", [1, 2])
    WrT = hin("WrT", [D, D])
    WfT = hin("WfT", [D, D])
    att_r_col = hin("att_r_col", [128, 1])
    gamma_col = hin("gamma_col", [128, NCH])
    beta_col = hin("beta_col", [128, NCH])
    onesm32 = hin("onesm32", [128, 1])
    W1 = {
        "W_kvd": hin("W_kvd", [D, 256]),
        "W_qd": hin("W_qd", [D, 256]),
        "W_qcomb": hin("W_qcomb", [256, D]),
        "W_ku": hin("W_ku", [256, D // 2]),
        "W_ropek": hin("W_ropek", [D, D // 2]),
        "W_vu": hin("W_vu", [256, D]),
        "W_o": hin("W_o", [D, D]),
        "Wg_s": hin("Wg_s", [D, D]),
        "Wu_s": hin("Wu_s", [D, D]),
        "Wd_s": hin("Wd_s", [D, D]),
        "W_router": hin("W_router", [D, 7]),
        "rbias": hin("rbias", [7, 1]),
        "cos_t": hin("cos_t", [128, T]),
        "sin_t": hin("sin_t", [128, T]),
        "perm": hin("perm", [128, 128], BF16),
        "w1col": hin("w1col", [128, NCH]),
        "w2col": hin("w2col", [128, NCH]),
    }
    _wgr = hin("Wg_r", [7 * D, D])
    _wur = hin("Wu_r", [7 * D, D])
    _wdr = hin("Wd_r", [7 * D, D])
    W1["Wg_r"] = [_wgr[e * D:(e + 1) * D, :] for e in range(7)]
    W1["Wu_r"] = [_wur[e * D:(e + 1) * D, :] for e in range(7)]
    W1["Wd_r"] = [_wdr[e * D:(e + 1) * D, :] for e in range(7)]
    W1["cmask"] = [hin(f"cmask{i}", [128, 2 * TB], BF16) for i in range(4)]
    W2 = {
        "wqT": hin("wqT", [D, D]),
        "wkT": hin("wkT", [D, D]),
        "wvT": hin("wvT", [D, D]),
        "woT": hin("woT", [D, D]),
        "bq": hin("bq_c", [128, NCH]),
        "bk": hin("bk_c", [128, NCH]),
        "bo": hin("bo_c", [128, NCH]),
        "bv_row": hin("bv_row", [1, D]),
    }

    h_spill = hscr("h_spill", [I0, T], F32)
    qgT = hscr("qgT", [D, I0], BF16)
    quT = hscr("quT", [D, I0], BF16)
    qdT = hscr("qdT", [I0, D], BF16)
    CC = {"in": [hscr(f"cc_in{i}", [1, 1], F32) for i in range(4)],
          "out": [hscr(f"cc_out{i}", [1, 1], F32) for i in range(4)],
          "i": 0}
    HS = {"zbf": hscr("zbf_spill", [D, T], BF16),
          "ctx": hscr("ctx_spill", [D, T], BF16),
          "wts": hscr("wts_spill", [7, T], F32)}

    es = contextlib.ExitStack()
    with es:
        b = B(nc, es)
        b.tagv = b.sb(es, "tagv", [1, 1], F32)
        b.tags = b.sb(es, "tags", [1, 1], F32)
        b.tagg = b.sb(es, "tagg", [1, 1], F32)
        X = [b.sb(es, "x", [128, T], F32) for _ in range(NCH)]
        K = {"ident": b.sb(es, "ident", [128, 128], BF16),
             "ident32": b.sb(es, "ident32", [128, 128], F32),
             "onesm": b.sb(es, "onesm", [128, 1], BF16),
             "ones_row32": b.sb(es, "onesr32", [1, 128], F32),
             "R": {nm: b.sb(es, "row_" + nm, [1, T], F32)
                   for nm in ["mu", "msq", "r", "mur"]},
             "col1": b.sb(es, "col1", [128, 1], F32),
             "col8a": b.sb(es, "col8a", [8, 1], F32),
             "col8b": b.sb(es, "col8b", [8, 1], F32),
             "ftmp": b.sb(es, "ftmp", [128, TB], F32),
             "onesm32": b.sb(es, "onesm32_t", [128, 1], F32),
             "eps11": b.sb(es, "eps11", [1, 1], F32)}
        K["ftmp32"] = K["ftmp"]
        s_clip = b.sb(es, "clip", [1, 2], F32)
        sg_sc = b.sb(es, "sg_sc", [128, NI0], F32)
        su_sc = b.sb(es, "su_sc", [128, NI0], F32)
        sd_sc = b.sb(es, "sd_sc", [128, NCH], F32)

        with nc.Block() as block:
            @block.sync
            def _(sync):
                for c in range(NCH):
                    b.dma(sync, X[c][:], x_in[c * 128:(c + 1) * 128, :], "dma")
                b.dma(sync, K["ident"][:], ident, "dma")
                b.dma(sync, K["ident32"][:], ident32, "dma")
                b.dma(sync, K["onesm"][:], onesm, "dma")
                b.dma(sync, K["ones_row32"][:], ones_row32, "dma")
                b.dma(sync, K["onesm32"][:], onesm32, "dma")
                b.dma(sync, s_clip[:], clip2, "dma")
                b.wait(sync, "dma")

            @block.vector
            def _(vector):
                vector.memset(K["eps11"][:], EPS)
                b.mark(vector, b.tagv, "ve", 1)

        emit_l0_weight_quant(b, K, w_gate, w_up, w_down, qgT, quT, qdT,
                             sg_sc, su_sc, sd_sc)
        emit_l0_ffn(b, K, X, s_clip, qgT, quT, qdT, sg_sc, su_sc, sd_sc,
                    h_spill, CC, n_cores)
        if stages >= 2:
            emit_l0_attn(b, K, X, WrT, WfT, att_r_col, gamma_col, beta_col, CC)
        if stages >= 3:
            emit_l0_ffn(b, K, X, s_clip, qgT, quT, qdT, sg_sc, su_sc, sd_sc,
                        h_spill, CC, n_cores)
        if stages >= 4:
            w1sb = b.sb(es, "w1sb", [128, NCH], F32)
            w2sb = b.sb(es, "w2sb", [128, NCH], F32)
            with nc.Block() as block:
                @block.sync
                def _(sync):
                    b.dma(sync, w1sb[:], W1["w1col"], "dma")
                    b.dma(sync, w2sb[:], W1["w2col"], "dma")
                    b.wait(sync, "dma")
            W1["w1col_sb"] = w1sb
            W1["w2col_sb"] = w2sb
            emit_l1_ffn(b, K, X, W1, CC, HS, n_cores)
        if stages >= 5:
            with contextlib.ExitStack() as s5:
                bv_sb = b.sb(s5, "bv_sb", [1, D], F32)
                with nc.Block() as block:
                    @block.sync
                    def _(sync):
                        b.dma(sync, bv_sb[:], W2["bv_row"], "dma")
                        b.wait(sync, "dma")
                W2["bv_sb"] = bv_sb
                emit_l1_attn(b, K, X, W2, HS)
        if stages >= 6:
            emit_l1_ffn(b, K, X, W1, CC, HS, n_cores)

        with nc.Block() as block:
            @block.sync
            def _(sync):
                for c in range(NCH):
                    b.dma(sync, out_fm[c * 128:(c + 1) * 128, :], X[c][:], "sdma")
                b.wait(sync, "sdma")
    return nc


def make_inputs(d, core):
    x = np.ascontiguousarray(np.asarray(d["L0_attn_h"])[core].T)
    return {
        "x_fm": x,
        "ident": _bf16(np.eye(128, dtype=np.float32)),
        "ident32": np.eye(128, dtype=np.float32),
        "onesm": _bf16(np.full((128, 1), 1.0 / 1024.0, np.float32)),
        "ones_row32": np.ones((1, 128), np.float32),
        "w_gate": np.asarray(d["L0_ffn_w_gate"]),
        "w_up": np.asarray(d["L0_ffn_w_up"]),
        "w_down": np.asarray(d["L0_ffn_w_down"]),
        "clip2": np.stack([np.asarray(d["L0_ffn_clip_x"]),
                           np.asarray(d["L0_ffn_clip_down"])], 1).astype(np.float32),
        "WrT": np.ascontiguousarray(np.asarray(d["L0_attn_Wr"]).T),
        "WfT": np.ascontiguousarray(np.asarray(d["L0_attn_Wf"]).T),
        "att_r_col": np.ascontiguousarray(np.asarray(d["L0_attn_att_r"]).reshape(128, 1)),
        "gamma_col": np.ascontiguousarray(np.asarray(d["L0_attn_ln_gamma"]).reshape(NCH, 128).T),
        "beta_col": np.ascontiguousarray(np.asarray(d["L0_attn_ln_beta"]).reshape(NCH, 128).T),
        "onesm32": np.full((128, 1), 1.0 / 1024.0, np.float32),
        **_l1_inputs(d),
        "wqT": np.ascontiguousarray(np.asarray(d["L1_attn_wq_w"]).T),
        "wkT": np.ascontiguousarray(np.asarray(d["L1_attn_wk_w"]).T),
        "wvT": np.ascontiguousarray(np.asarray(d["L1_attn_wv_w"]).T),
        "woT": np.ascontiguousarray(np.asarray(d["L1_attn_wo_w"]).T),
        "bq_c": np.ascontiguousarray(np.asarray(d["L1_attn_wq_b"]).reshape(NCH, 128).T),
        "bk_c": np.ascontiguousarray(np.asarray(d["L1_attn_wk_b"]).reshape(NCH, 128).T),
        "bo_c": np.ascontiguousarray(np.asarray(d["L1_attn_wo_b"]).reshape(NCH, 128).T),
        "bv_row": np.ascontiguousarray(np.asarray(d["L1_attn_wv_b"]).reshape(1, D)),
    }


def _bf16(a):
    import ml_dtypes
    return a.astype(ml_dtypes.bfloat16)


def _l1_inputs(d):
    g = lambda n: np.asarray(d[n])
    wqu, wrq = g("L1_ffn_W_qu"), g("L1_ffn_W_ropeq")
    qcomb = np.empty((256, D), np.float32)
    for h in range(16):
        qcomb[:, 64 * h:64 * h + 32] = wqu[:, 32 * h:32 * h + 32]
        qcomb[:, 64 * h + 32:64 * h + 64] = wrq[:, 32 * h:32 * h + 32]
    cos_t = np.ones((128, T), np.float32)
    sin_t = np.zeros((128, T), np.float32)
    t = np.arange(T, dtype=np.float64)
    for r in range(128):
        r64 = r % 64
        if r64 >= 32:
            dd = r64 - 32
            invf = 1.0 / 10000.0 ** (2 * (dd % 16) / 32.0)
            cos_t[r] = np.cos(t * invf).astype(np.float32)
            sin_t[r] = np.sin(t * invf).astype(np.float32)
    perm = np.zeros((128, 128), np.float32)
    for base in (0, 64):
        for dd in range(32):
            m = base + 32 + dd
            if dd < 16:
                perm[base + 32 + dd + 16, m] = -1.0
            else:
                perm[base + 32 + dd - 16, m] = 1.0
    cms = []
    for i in range(4):
        k = np.arange(128)[:, None]
        q = np.arange(TB)[None, :]
        m = (k + 128 * i <= q).astype(np.float32)
        cms.append(_bf16(np.concatenate([m, m], axis=1)))
    return {
        "W_kvd": g("L1_ffn_W_kvd"), "W_qd": g("L1_ffn_W_qd"),
        "W_qcomb": qcomb, "W_ku": g("L1_ffn_W_ku"),
        "W_ropek": g("L1_ffn_W_ropek"), "W_vu": g("L1_ffn_W_vu"),
        "W_o": g("L1_ffn_W_o"),
        "Wg_s": g("L1_ffn_Wg_s"), "Wu_s": g("L1_ffn_Wu_s"), "Wd_s": g("L1_ffn_Wd_s"),
        "Wg_r": g("L1_ffn_Wg_r").reshape(7 * D, D),
        "Wu_r": g("L1_ffn_Wu_r").reshape(7 * D, D),
        "Wd_r": g("L1_ffn_Wd_r").reshape(7 * D, D),
        "W_router": g("L1_ffn_W_router"),
        "rbias": g("L1_ffn_routing_bias").reshape(7, 1).astype(np.float32),
        "cos_t": cos_t, "sin_t": sin_t, "perm": _bf16(perm),
        "w1col": np.ascontiguousarray(g("L1_ffn_w_ln1").reshape(NCH, 128).T).astype(np.float32),
        "w2col": np.ascontiguousarray(g("L1_ffn_w_ln2").reshape(NCH, 128).T).astype(np.float32),
        **{f"cmask{i}": cms[i] for i in range(4)},
    }



NT128 = T // 128


def _ln_tile(b, vector, X_c_slice, rbs_slice, mbs_slice, out_ap):
    vector.tensor_mul(out=out_ap, in0=X_c_slice, in1=rbs_slice)
    vector.drain()
    vector.tensor_sub(out=out_ap, in0=out_ap, in1=mbs_slice)
    vector.drain()


def _rms_rows(b, K, msq_src_row, out_r_row):
    nc = b.nc
    with nc.Block() as block:
        @block.scalar
        def _(scalar):
            scalar.activation(out=K["R"]["mur"][0:1, :], in_=msq_src_row,
                              func=AF.Sqrt)
            b.mark(scalar, b.tags, "sc", 1)

        @block.vector
        def _(vector):
            b.wait(vector, "sc")
            vector.tensor_scalar_add(out=K["R"]["mur"][0:1, :],
                                     in0=K["R"]["mur"][0:1, :], scalar1=1e-5)
            vector.drain()
            vector.reciprocal(out=out_r_row, in_=K["R"]["mur"][0:1, :])
            b.mark(vector, b.tagv, "ve", 1)


def _bcast_col11(b, K, scope, src11, dst_col):
    nc = b.nc
    pbc = b.ps(scope, "bc11", [128, 1])
    with nc.Block() as block:
        @block.tensor
        def _(tensor):
            ins = tensor.matmul(pbc[:, 0:1], K["ones_row32"][:], src11, start=True,
                                stop=True)
            b.inc(ins, "te", 1)

        @block.vector
        def _(vector):
            b.wait(vector, "te")
            vector.tensor_copy(out=dst_col, in_=pbc[:, 0:1])
            b.mark(vector, b.tagv, "ve", 1)


def emit_l1_ffn(b, K, X, W, CC, HS, n_cores):
    nc = b.nc
    emit_stats(b, X, K, "ln")
    with contextlib.ExitStack() as osc:
        with contextlib.ExitStack() as asc:
            xn = [b.sb(asc, "l1_xn", [128, T], BF16) for _ in range(NCH)]
            kv = [b.sb(asc, "l1_kv", [128, T], BF16) for _ in range(2)]
            ql = [b.sb(asc, "l1_ql", [128, T], BF16) for _ in range(2)]
            with contextlib.ExitStack() as zsc:
                rbs = b.sb(zsc, "l1_rbs", [128, T], F32)
                mbs = b.sb(zsc, "l1_mbs", [128, T], F32)
                emit_bcast_full(b, K, K["R"]["r"], rbs)
                emit_bcast_full(b, K, K["R"]["mur"], mbs)
                r2row = K["R"]["mu"]
                _emit_z_passes(b, K, X, W, HS, rbs, mbs, r2row, xn, zsc)
            _emit_latents(b, K, W, xn, kv, ql, asc)
            _emit_mla_attention(b, K, W, HS, xn, kv, ql, asc)

        xn2 = [b.sb(osc, "l1_xn2", [128, T], BF16) for _ in range(NCH)]
        _emit_l1_after_attn(b, K, W, HS, X, xn2)
        _emit_moe(b, K, W, X, xn2, HS, osc)


def _emit_z_passes(b, K, X, W, HS, rbs, mbs, r2row, xn, asc):
    nc = b.nc
    with contextlib.ExitStack() as pa:
        zt = b.sb(pa, "z_zt", [128, T], F32)
        zsq32 = b.sb(pa, "z_zs32", [128, T], F32)
        zsq = [b.sb(pa, "z_zsq", [128, T], BF16)]
        ze2 = [b.sb(pa, "z_ze2", [128, T], BF16)]
        pzs = [b.ps(pa, "z_pzs", [1, TB]) for _ in range(NTB)]
        for c in range(NCH):
            (te0, sc0, ve0) = b.base("te", "sc", "ve")
            with nc.Block() as block:
                @block.vector
                def _(vector):
                    _ln_tile(b, vector, X[c][:], rbs[:], mbs[:], zt[:])
                    b.mark(vector, b.tagv, "ve", 1)

                @block.scalar
                def _(scalar):
                    scalar.wait_ge(b.sem("ve"), ve0 + 1)
                    scalar.activation(out=zsq32[:], in_=zt[:], func=AF.Square)
                    scalar.drain()
                    scalar.copy(out=zsq[0][:], in_=zsq32[:])
                    b.mark(scalar, b.tags, "sc", 1)
            with nc.Block() as block:
                @block.vector
                def _(vector):
                    vector.tensor_sub(out=ze2[0][:], in0=zsq32[:], in1=zsq[0][:])
                    b.mark(vector, b.tagv, "ve", 1)
            with nc.Block() as block:
                @block.tensor
                def _(tensor):
                    for tb in range(NTB):
                        sl = slice(tb * TB, (tb + 1) * TB)
                        tensor.matmul(pzs[tb][0:1, :], K["onesm"][:], zsq[0][:, sl],
                                      start=(c == 0), stop=False)
                        ins = tensor.matmul(pzs[tb][0:1, :], K["onesm"][:],
                                            ze2[0][:, sl],
                                            start=False, stop=(c == NCH - 1))
                    b.inc(ins, "te", 1)
        with nc.Block() as block:
            @block.vector
            def _(vector):
                for tb in range(NTB):
                    vector.tensor_copy(out=K["R"]["msq"][0:1, tb * TB:(tb + 1) * TB],
                                       in_=pzs[tb][0:1, :])
                b.mark(vector, b.tagv, "ve", 1)
        _rms_rows(b, K, K["R"]["msq"][0:1, :], r2row[0:1, :])

    with contextlib.ExitStack() as pb:
        r2b = b.sb(pb, "z_r2b", [128, T], F32)
        emit_bcast_full(b, K, r2row, r2b)
        zt = b.sb(pb, "z_ztB", [128, T], F32)
        zbf = [b.sb(pb, "z_zbf", [128, T], BF16) for _ in range(2)]
        zs0 = [b.base("zsp0")[0], b.base("zsp1")[0]]
        (ve0,) = b.base("ve")
        with nc.Block() as block:
            @block.vector
            def _(vector):
                for c in range(NCH):
                    _ln_tile(b, vector, X[c][:], rbs[:], mbs[:], zt[:])
                    if c >= 2:
                        b.wait(vector, f"zsp{c % 2}", zs0[c % 2] + 16 * ((c - 2) // 2 + 1))
                    vector.tensor_copy(out=zbf[c % 2][:], in_=zt[:])
                    vector.drain()
                    vector.scalar_tensor_tensor(out=xn[c][:], in0=zt[:],
                                                scalar=W["w1col_sb"][:, c:c + 1],
                                                in1=r2b[:], op0=ALU.mult, op1=ALU.mult)
                    b.mark(vector, b.tagv, "ve", 1)

            @block.scalar
            def _(scalar):
                for c in range(NCH):
                    scalar.wait_ge(b.sem("ve"), ve0 + c + 1)
                    b.dma(scalar, HS["zbf"][c * 128:(c + 1) * 128, :], zbf[c % 2][:],
                          f"zsp{c % 2}")
                b.wait(scalar, "zsp0")
                b.wait(scalar, "zsp1")


def _emit_latents(b, K, W, xn, kv, ql, asc):
    nc = b.nc
    with contextlib.ExitStack() as pj:
        wkd = [b.sb(pj, "lt_wkd", [128, 256], BF16) for _ in range(NCH)]
        wqd = [b.sb(pj, "lt_wqd", [128, 256], BF16) for _ in range(NCH)]
        pp = [b.ps(pj, "lt_pp", [128, TB]) for _ in range(2)]
        with nc.Block() as block:
            @block.gpsimd
            def _(gpsimd):
                for s in range(NCH):
                    b.dma(gpsimd, wkd[s][:], W["W_kvd"][s * 128:(s + 1) * 128, :],
                          f"gld{s % 2}")
                    b.dma(gpsimd, wqd[s][:], W["W_qd"][s * 128:(s + 1) * 128, :],
                          f"gld{s % 2}")
        gl = [b.cnt["gld0"], b.cnt["gld1"]]
        (te0, sc0) = b.base("te", "sc")
        with nc.Block() as block:
            @block.tensor
            def _(tensor):
                tensor.wait_ge(b.sem("gld0"), gl[0])
                tensor.wait_ge(b.sem("gld1"), gl[1])
                it = 0
                for (dst, wsrc) in [(kv, wkd), (ql, wqd)]:
                    for oc in range(2):
                        for tb in range(NTB):
                            if it >= 2:
                                tensor.wait_ge(b.sem("sc"), sc0 + it - 1)
                            for s in range(NCH):
                                ins = tensor.matmul(
                                    pp[it % 2][:, :],
                                    wsrc[s][:, oc * 128:(oc + 1) * 128],
                                    xn[s][:, tb * TB:(tb + 1) * TB],
                                    start=(s == 0), stop=(s == NCH - 1))
                            b.inc(ins, "te", 1)
                            it += 1

            @block.scalar
            def _(scalar):
                it = 0
                for (dst, wsrc) in [(kv, wkd), (ql, wqd)]:
                    for oc in range(2):
                        for tb in range(NTB):
                            scalar.wait_ge(b.sem("te"), te0 + it + 1)
                            scalar.copy(out=dst[oc][:, tb * TB:(tb + 1) * TB],
                                        in_=pp[it % 2][:, :])
                            b.mark(scalar, b.tags, "sc", 1)
                            it += 1


def _emit_mla_attention(b, K, W, HS, xn, kv, ql, asc):
    nc = b.nc
    with contextlib.ExitStack() as pl:
        cosb = b.sb(pl, "ml_cos", [128, T], F32)
        sinb = b.sb(pl, "ml_sin", [128, T], F32)
        perm = b.sb(pl, "ml_perm", [128, 128], BF16)
        cmask = [b.sb(pl, "ml_cm", [128, 2 * TB], BF16) for _ in range(4)]
        qq = b.sb(pl, "ml_qq", [128, T], BF16)
        kk = b.sb(pl, "ml_kk", [128, T], BF16)
        kkraw = b.sb(pl, "ml_kkr", [128, T], BF16)
        vpair = [b.sb(pl, "ml_vp", [128, 130], BF16) for _ in range(NT128)]
        PT = [b.sb(pl, "ml_pt", [128, 2 * TB], BF16) for _ in range(2)]
        wq_p = b.sb(pl, "ml_wqp", [128, 2 * 128], BF16)
        wku_p = b.sb(pl, "ml_wkup", [128, 2 * 64], BF16)
        wrk_p = b.sb(pl, "ml_wrkp", [128, NCH * 64], BF16)
        wvu_p = b.sb(pl, "ml_wvup", [128, 2 * 128], BF16)
        qraw = b.sb(pl, "ml_qraw", [128, TB], BF16)
        sqb = b.sb(pl, "ml_sqb", [128, TB], BF16)
        n11 = b.sb(pl, "ml_n11", [1, 4], F32)
        bcol = b.sb(pl, "ml_bcol", [128, 1], F32)
        rr = b.sb(pl, "ml_rr", [1, TB], F32)
        rbx = b.sb(pl, "ml_rbx", [64, TB], F32)
        cxst = b.sb(pl, "ml_cxst", [128, TB], BF16)

        with nc.Block() as block:
            @block.sync
            def _(sync):
                b.dma(sync, cosb[:], W["cos_t"], "dma")
                b.dma(sync, sinb[:], W["sin_t"], "dma")
                for i in range(4):
                    b.dma(sync, cmask[i][:], W["cmask"][i], "dma")
                b.wait(sync, "dma")

            @block.gpsimd
            def _(gpsimd):
                b.dma(gpsimd, perm[:], W["perm"], "gld0")
                b.wait(gpsimd, "gld0")

        for p in range(NCH):
            _emit_mla_pair(b, K, W, HS, p, xn, kv, ql, cosb, sinb, perm, cmask,
                           qq, kk, kkraw, vpair, PT, wq_p, wku_p, wrk_p, wvu_p,
                           qraw, sqb, n11, bcol, rr, rbx, cxst)


def _emit_mla_pair(b, K, W, HS, p, xn, kv, ql, cosb, sinb, perm, cmask,
                   qq, kk, kkraw, vpair, PT, wq_p, wku_p, wrk_p, wvu_p,
                   qraw, sqb, n11, bcol, rr, rbx, cxst):
    nc = b.nc
    with contextlib.ExitStack() as sc_:
      with contextlib.ExitStack() as proj_sc:
        pqq = b.ps(proj_sc, "mp_pqq", [128, TB])
        prot = b.ps(proj_sc, "mp_rot", [128, TB])
        pk = b.ps(proj_sc, "mp_pk", [64, TB])
        pv = [b.ps(proj_sc, "mp_pv", [128, 128]) for _ in range(2)]
        pn = b.ps(proj_sc, "mp_pn", [1, TB])
        pb1 = b.ps(proj_sc, "mp_pb1", [128, 1])
        stg96 = b.sb(proj_sc, "mp_s96", [32, TB], BF16)

        with nc.Block() as block:
            @block.gpsimd
            def _(gpsimd):
                b.dma(gpsimd, wq_p[:].rearrange("p (s c) -> p s c", c=128),
                      W["W_qcomb"][:, p * 128:(p + 1) * 128]
                      .rearrange("(s p) c -> p s c", p=128), "gld0")
                b.dma(gpsimd, wku_p[:].rearrange("p (s c) -> p s c", c=64),
                      W["W_ku"][:, p * 64:(p + 1) * 64]
                      .rearrange("(s p) c -> p s c", p=128), "gld1")
                b.dma(gpsimd, wrk_p[:].rearrange("p (s c) -> p s c", c=64),
                      W["W_ropek"][:, p * 64:(p + 1) * 64]
                      .rearrange("(s p) c -> p s c", p=128), "gld0")
                b.dma(gpsimd, wvu_p[:].rearrange("p (s c) -> p s c", c=128),
                      W["W_vu"][:, p * 128:(p + 1) * 128]
                      .rearrange("(s p) c -> p s c", p=128), "gld1")
                b.wait(gpsimd, "gld0")
                b.wait(gpsimd, "gld1")

        for tb in range(NTB):
            sl = slice(tb * TB, (tb + 1) * TB)
            with nc.Block() as block:
                @block.tensor
                def _(tensor):
                    for s in range(2):
                        ins = tensor.matmul(pqq[:, :], wq_p[:, s * 128:(s + 1) * 128],
                                            ql[s][:, sl], start=(s == 0), stop=(s == 1))
                    b.inc(ins, "te", 1)

                @block.scalar
                def _(scalar):
                    b.wait(scalar, "te")
                    scalar.copy(out=qraw[:], in_=pqq[:, :])
                    b.mark(scalar, b.tags, "sc", 1)

            with nc.Block() as block:
                @block.tensor
                def _(tensor):
                    ins = tensor.matmul(prot[:, :], perm[:], qraw[:], start=True,
                                        stop=True)
                    b.inc(ins, "te", 1)

                @block.vector
                def _(vector):
                    b.wait(vector, "te")
                    vector.tensor_mul(out=K["ftmp"][:, 0:TB], in0=qraw[:], in1=cosb[:, sl])
                    vector.drain()
                    vector.scalar_tensor_tensor(out=qq[:, sl], in0=prot[:, :],
                                                scalar=1.0, in1=sinb[:, sl],
                                                op0=ALU.mult, op1=ALU.mult)
                    vector.drain()
                    vector.tensor_add(out=qq[:, sl], in0=qq[:, sl],
                                      in1=K["ftmp"][:, 0:TB])
                    b.mark(vector, b.tagv, "ve", 1)

        for tb in range(NTB):
            sl = slice(tb * TB, (tb + 1) * TB)
            with nc.Block() as block:
                @block.tensor
                def _(tensor):
                    for s in range(2):
                        ins = tensor.matmul(pk[:, :], wku_p[:, s * 64:(s + 1) * 64],
                                            kv[s][:, sl], start=(s == 0), stop=(s == 1))
                    b.inc(ins, "te", 1)

                @block.scalar
                def _(scalar):
                    b.wait(scalar, "te")
                    scalar.copy(out=kkraw[0:32, sl], in_=pk[0:32, :])
                    scalar.copy(out=kkraw[64:96, sl], in_=pk[32:64, :])
                    b.mark(scalar, b.tags, "sc", 1)

            with nc.Block() as block:
                @block.tensor
                def _(tensor):
                    for s in range(NCH):
                        ins = tensor.matmul(pk[:, :], wrk_p[:, s * 64:(s + 1) * 64],
                                            xn[s][:, sl],
                                            start=(s == 0), stop=(s == NCH - 1))
                    b.inc(ins, "te", 1)

                @block.scalar
                def _(scalar):
                    b.wait(scalar, "te")
                    scalar.copy(out=kkraw[32:64, sl], in_=pk[0:32, :])
                    scalar.copy(out=stg96[:], in_=pk[32:64, :])
                    scalar.drain()
                    b.dma(scalar, kkraw[96:128, sl], stg96[:], f"kkd{tb % 2}")
                    b.mark(scalar, b.tags, "sc", 1)

            with nc.Block() as block:
                @block.tensor
                def _(tensor):
                    b.wait(tensor, "kkd0")
                    b.wait(tensor, "kkd1")
                    ins = tensor.matmul(prot[:, :], perm[:], kkraw[:, sl], start=True,
                                        stop=True)
                    b.inc(ins, "te", 1)

                @block.vector
                def _(vector):
                    b.wait(vector, "te")
                    vector.tensor_mul(out=K["ftmp"][:, 0:TB], in0=kkraw[:, sl],
                                      in1=cosb[:, sl])
                    vector.drain()
                    vector.scalar_tensor_tensor(out=kk[:, sl], in0=prot[:, :],
                                                scalar=1.0, in1=sinb[:, sl],
                                                op0=ALU.mult, op1=ALU.mult)
                    vector.drain()
                    vector.tensor_add(out=kk[:, sl], in0=kk[:, sl],
                                      in1=K["ftmp"][:, 0:TB])
                    b.mark(vector, b.tagv, "ve", 1)

        (te0, ve0) = b.base("te", "ve")
        with nc.Block() as block:
            @block.tensor
            def _(tensor):
                for t in range(NT128):
                    if t >= 2:
                        tensor.wait_ge(b.sem("ve"), ve0 + t - 1)
                    for s in range(2):
                        ins = tensor.matmul(pv[t % 2][:, :],
                                            kv[s][:, t * 128:(t + 1) * 128],
                                            wvu_p[:, s * 128:(s + 1) * 128],
                                            start=(s == 0), stop=(s == 1))
                    b.inc(ins, "te", 1)

            @block.vector
            def _(vector):
                for t in range(NT128):
                    vector.wait_ge(b.sem("te"), te0 + t + 1)
                    vector.tensor_copy(out=vpair[t][:, 0:64], in_=pv[t % 2][:, 0:64])
                    vector.tensor_copy(out=vpair[t][:, 65:129], in_=pv[t % 2][:, 64:128])
                    vector.memset(vpair[t][:, 64:65], 1.0)
                    vector.memset(vpair[t][:, 129:130], 1.0)
                    b.mark(vector, b.tagv, "ve", 1)

        with nc.Block() as block:
            @block.vector
            def _(vector):
                vector.memset(n11[0:1, 0:2], 0.0)
                b.mark(vector, b.tagv, "ve", 1)
        for (i, srcb) in [(0, qq), (1, kk)]:
            for tb in range(NTB):
                sl = slice(tb * TB, (tb + 1) * TB)
                with nc.Block() as block:
                    @block.vector
                    def _(vector):
                        vector.tensor_mul(out=sqb[:], in0=srcb[:, sl], in1=srcb[:, sl])
                        b.mark(vector, b.tagv, "ve", 1)
                with nc.Block() as block:
                    @block.tensor
                    def _(tensor):
                        ins = tensor.matmul(pn[0:1, :], K["onesm"][:], sqb[:],
                                            start=True, stop=True)
                        b.inc(ins, "te", 1)

                    @block.vector
                    def _(vector):
                        b.wait(vector, "te")
                        vector.tensor_reduce(out=n11[0:1, 2:3], in_=pn[0:1, :],
                                             axis=AX.X, op=ALU.max)
                        vector.drain()
                        vector.tensor_max(out=n11[0:1, i:i + 1], in0=n11[0:1, i:i + 1],
                                          in1=n11[0:1, 2:3])
                        b.mark(vector, b.tagv, "ve", 1)
        with nc.Block() as block:
            @block.scalar
            def _(scalar):
                scalar.activation(out=n11[0:1, 0:2], in_=n11[0:1, 0:2], func=AF.Sqrt)
                b.mark(scalar, b.tags, "sc", 1)

            @block.vector
            def _(vector):
                b.wait(vector, "sc")
                vector.tensor_mul(out=n11[0:1, 3:4], in0=n11[0:1, 0:1],
                                  in1=n11[0:1, 1:2])
                vector.drain()
                vector.tensor_scalar_mul(out=n11[0:1, 3:4], in0=n11[0:1, 3:4],
                                         scalar1=-128.0)
                b.mark(vector, b.tagv, "ve", 1)
        with nc.Block() as block:
            @block.tensor
            def _(tensor):
                ins = tensor.matmul(pb1[:, 0:1], K["ones_row32"][:], n11[0:1, 3:4],
                                    start=True, stop=True)
                b.inc(ins, "te", 1)

            @block.vector
            def _(vector):
                b.wait(vector, "te")
                vector.tensor_copy(out=bcol[:], in_=pb1[:, 0:1])
                b.mark(vector, b.tagv, "ve", 1)

      for qb in range(NTB):
          _emit_pair_scores(b, K, W, HS, p, qb, qq, kk, vpair, PT, cmask, bcol,
                            rr, rbx, cxst, sc_)


def _emit_pair_scores(b, K, W, HS, p, qb, qq, kk, vpair, PT, cmask, bcol,
                      rr, rbx, cxst, sc_, causal=True, bcols16=None):
    nc = b.nc
    nkt = (4 * qb + 4) if causal else 16
    qsl = slice(qb * TB, (qb + 1) * TB)
    with contextlib.ExitStack() as ss:
        psS = [b.ps(ss, "sq_S", [128, 2 * TB]) for _ in range(2)]
        psA = b.ps(ss, "sq_A", [65, TB])
        psB = b.ps(ss, "sq_B", [65, TB])
        psr = b.ps(ss, "sq_r", [64, TB])
        (te0, sc0, ve0, pv0) = b.base("te", "sc", "ve", "pv")
        with nc.Block() as block:
            @block.tensor
            def _(tensor):
                for kt in range(nkt):
                    ksl = slice(kt * 128, (kt + 1) * 128)
                    if kt >= 2:
                        tensor.wait_ge(b.sem("sc"), sc0 + kt - 1)
                    tensor.matmul(psS[kt % 2][:, 0:TB], kk[0:64, ksl], qq[0:64, qsl],
                                  start=True, stop=True, tile_position=(0, 0))
                    ins = tensor.matmul(psS[kt % 2][:, TB:2 * TB], kk[64:128, ksl],
                                        qq[64:128, qsl], start=True, stop=True,
                                        tile_position=(64, 0))
                    b.inc(ins, "te", 1)
                    tensor.wait_ge(b.sem("ve"), ve0 + kt + 1)
                    tensor.matmul(psA[:, :], vpair[kt][:, 0:65], PT[kt % 2][:, 0:TB],
                                  start=(kt == 0), stop=(kt == nkt - 1))
                    ins = tensor.matmul(psB[:, :], vpair[kt][:, 65:130],
                                        PT[kt % 2][:, TB:2 * TB],
                                        start=(kt == 0), stop=(kt == nkt - 1))
                    b.inc(ins, "pv", 1)

            @block.scalar
            def _(scalar):
                for kt in range(nkt):
                    scalar.wait_ge(b.sem("te"), te0 + kt + 1)
                    if kt >= 2:
                        scalar.wait_ge(b.sem("pv"), pv0 + kt - 1)
                    bias_ap = bcol[:] if bcols16 is None else bcols16[:, kt:kt + 1]
                    scalar.activation(out=PT[kt % 2][:], in_=psS[kt % 2][:, :],
                                      func=AF.Exp, bias=bias_ap, scale=0.125)
                    b.mark(scalar, b.tags, "sc", 1)

            @block.vector
            def _(vector):
                for kt in range(nkt):
                    vector.wait_ge(b.sem("sc"), sc0 + kt + 1)
                    di = (kt - 4 * qb) if causal else -1
                    if 0 <= di <= 3:
                        vector.tensor_mul(out=PT[kt % 2][:], in0=PT[kt % 2][:],
                                          in1=cmask[di][:])
                    b.mark(vector, b.tagv, "ve", 1)

        (pv0b, ve0b, sc0b, te0b) = b.base("pv", "ve", "sc", "te")
        with nc.Block() as block:
            @block.scalar
            def _(scalar):
                scalar.copy(out=rr[0:1, :], in_=psA[64:65, :])
                b.mark(scalar, b.tags, "sc", 1)
                scalar.wait_ge(b.sem("te"), te0b + 1)
                scalar.copy(out=rbx[:, :], in_=psr[:, :])
                b.mark(scalar, b.tags, "sc", 1)

            @block.vector
            def _(vector):
                b.wait(vector, "sc", sc0b + 1)
                vector.reciprocal(out=rr[0:1, :], in_=rr[0:1, :])
                b.mark(vector, b.tagv, "ve", 1)

            @block.tensor
            def _(tensor):
                tensor.wait_ge(b.sem("ve"), ve0b + 1)
                ins = tensor.matmul(psr[:, :], K["ones_row32"][:, 0:64], rr[0:1, :],
                                    start=True, stop=True)
                b.inc(ins, "te", 1)

        with nc.Block() as block:
            @block.vector
            def _(vector):
                b.wait(vector, "cxd0")
                b.wait(vector, "cxd1")
                vector.tensor_mul(out=cxst[0:64, :], in0=psA[0:64, :], in1=rbx[:, :])
                b.mark(vector, b.tagv, "ve", 1)

            @block.scalar
            def _(scalar):
                b.wait(scalar, "ve")
                b.dma(scalar, HS["ctx"][p * 128:p * 128 + 64, qsl], cxst[0:64, :],
                      "cxd0")

        (sc0c, ve0c, te0c) = b.base("sc", "ve", "te")
        with nc.Block() as block:
            @block.scalar
            def _(scalar):
                scalar.copy(out=rr[0:1, :], in_=psB[64:65, :])
                b.mark(scalar, b.tags, "sc", 1)
                scalar.wait_ge(b.sem("te"), te0c + 1)
                scalar.copy(out=rbx[:, :], in_=psr[:, :])
                b.mark(scalar, b.tags, "sc", 1)

            @block.vector
            def _(vector):
                b.wait(vector, "sc", sc0c + 1)
                vector.reciprocal(out=rr[0:1, :], in_=rr[0:1, :])
                b.mark(vector, b.tagv, "ve", 1)

            @block.tensor
            def _(tensor):
                tensor.wait_ge(b.sem("ve"), ve0c + 1)
                ins = tensor.matmul(psr[:, :], K["ones_row32"][:, 0:64], rr[0:1, :],
                                    start=True, stop=True)
                b.inc(ins, "te", 1)

        with nc.Block() as block:
            @block.vector
            def _(vector):
                b.wait(vector, "cxd0")
                b.wait(vector, "cxd1")
                vector.tensor_mul(out=cxst[0:64, :], in0=psB[0:64, :], in1=rbx[:, :])
                b.mark(vector, b.tagv, "ve", 1)

            @block.scalar
            def _(scalar):
                b.wait(scalar, "ve")
                b.dma(scalar, HS["ctx"][p * 128 + 64:(p + 1) * 128, qsl], cxst[0:64, :],
                      "cxd1")


def _emit_l1_after_attn(b, K, W, HS, X, xn2):
    nc = b.nc
    with contextlib.ExitStack() as oa:
        wo = [b.sb(oa, "oa_wo", [128, D], BF16) for _ in range(NCH)]
        cx = [b.sb(oa, "oa_cx", [128, TB], BF16) for _ in range(NCH)]
        zb = [b.sb(oa, "oa_zb", [128, TB], BF16) for _ in range(2)]
        ub = [b.sb(oa, "oa_ub", [128, T], BF16) for _ in range(NCH)]
        po = [b.ps(oa, "oa_po", [128, TB]) for _ in range(2)]
        with nc.Block() as block:
            @block.gpsimd
            def _(gpsimd):
                for s in range(NCH):
                    b.dma(gpsimd, wo[s][:], W["W_o"][s * 128:(s + 1) * 128, :],
                          f"gld{s % 2}")
                b.wait(gpsimd, "gld0")
                b.wait(gpsimd, "gld1")
        for qb in range(NTB):
            qsl = slice(qb * TB, (qb + 1) * TB)
            with nc.Block() as block:
                @block.sync
                def _(sync):
                    if qb == 0:
                        b.wait(sync, "cxd0")
                        b.wait(sync, "cxd1")
                    for s in range(NCH):
                        b.dma(sync, cx[s][:], HS["ctx"][s * 128:(s + 1) * 128, qsl],
                              "dma")
                    b.wait(sync, "dma")
            (te0, sc0, ve0) = b.base("te", "sc", "ve")
            zl0 = [b.base("zld0")[0], b.base("zld1")[0]]
            with nc.Block() as block:
                @block.tensor
                def _(tensor):
                    for oc in range(NCH):
                        if oc >= 2:
                            tensor.wait_ge(b.sem("ve"), ve0 + oc - 1)
                        for s in range(NCH):
                            ins = tensor.matmul(po[oc % 2][:, :],
                                                wo[s][:, oc * 128:(oc + 1) * 128],
                                                cx[s][:], start=(s == 0),
                                                stop=(s == NCH - 1))
                        b.inc(ins, "te", 1)

                @block.scalar
                def _(scalar):
                    for oc in range(NCH):
                        scalar.wait_ge(b.sem("ve"), ve0 + max(0, oc - 1))
                        b.dma(scalar, zb[oc % 2][:],
                              HS["zbf"][oc * 128:(oc + 1) * 128, qsl],
                              f"zld{oc % 2}")

                @block.vector
                def _(vector):
                    for oc in range(NCH):
                        vector.wait_ge(b.sem("te"), te0 + oc + 1)
                        vector.wait_ge(b.sem(f"zld{oc % 2}"),
                                       zl0[oc % 2] + 16 * (oc // 2 + 1))
                        vector.tensor_add(out=ub[oc][:, qsl], in0=po[oc % 2][:, :],
                                          in1=zb[oc % 2][:])
                        vector.drain()
                        vector.scalar_tensor_tensor(out=X[oc][:, qsl],
                                                    in0=ub[oc][:, qsl], scalar=0.5,
                                                    in1=X[oc][:, qsl],
                                                    op0=ALU.mult, op1=ALU.add)
                        b.mark(vector, b.tagv, "ve", 1)

        emit_stats_bf(b, ub, K)
        with contextlib.ExitStack() as px:
            r2b = b.sb(px, "oa_r2b", [128, T], F32)
            emit_bcast_full(b, K, K["R"]["r"], r2b)
            with nc.Block() as block:
                @block.vector
                def _(vector):
                    for c in range(NCH):
                        vector.scalar_tensor_tensor(out=xn2[c][:], in0=ub[c][:],
                                                    scalar=W["w2col_sb"][:, c:c + 1],
                                                    in1=r2b[:], op0=ALU.mult,
                                                    op1=ALU.mult)
                    b.mark(vector, b.tagv, "ve", 1)


def emit_stats_bf(b, U, K):
    nc = b.nc
    with contextlib.ExitStack() as sc:
        usq32 = b.sb(sc, "sb_us32", [128, T], F32)
        usq = [b.sb(sc, "sb_usq", [128, T], BF16)]
        e2bf = [b.sb(sc, "sb_e2", [128, T], BF16)]
        pst = [b.ps(sc, "sb_ps", [1, TB]) for _ in range(NTB)]
        for c in range(NCH):
            (ve0, sc0) = b.base("ve", "sc")
            with nc.Block() as block:
                @block.scalar
                def _(scalar):
                    scalar.activation(out=usq32[:], in_=U[c][:], func=AF.Square)
                    scalar.drain()
                    scalar.copy(out=usq[0][:], in_=usq32[:])
                    b.mark(scalar, b.tags, "sc", 1)

                @block.vector
                def _(vector):
                    vector.wait_ge(b.sem("sc"), sc0 + 1)
                    vector.tensor_sub(out=e2bf[0][:], in0=usq32[:],
                                      in1=usq[0][:])
                    b.mark(vector, b.tagv, "ve", 1)
            with nc.Block() as block:
                @block.tensor
                def _(tensor):
                    for tb in range(NTB):
                        sl = slice(tb * TB, (tb + 1) * TB)
                        tensor.matmul(pst[tb][0:1, :], K["onesm"][:], usq[0][:, sl],
                                      start=(c == 0), stop=False)
                        ins = tensor.matmul(pst[tb][0:1, :], K["onesm"][:],
                                            e2bf[0][:, sl],
                                            start=False, stop=(c == NCH - 1))
                    b.inc(ins, "te", 1)
        with nc.Block() as block:
            @block.vector
            def _(vector):
                for tb in range(NTB):
                    vector.tensor_copy(out=K["R"]["msq"][0:1, tb * TB:(tb + 1) * TB],
                                       in_=pst[tb][0:1, :])
                b.mark(vector, b.tagv, "ve", 1)
    _rms_rows(b, K, K["R"]["msq"][0:1, :], K["R"]["r"][0:1, :])


def _emit_moe(b, K, W, X, xn2, HS, osc):
    nc = b.nc
    with contextlib.ExitStack() as ms:
        wrow = b.sb(ms, "mo_wrow", [1, T], F32)
        _emit_router(b, K, W, xn2, HS, ms)
        wg = [b.sb(ms, "mo_wg", [128, D], BF16) for _ in range(NCH)]
        wu = [b.sb(ms, "mo_wu", [128, D], BF16) for _ in range(NCH)]
        wd = [b.sb(ms, "mo_wd", [128, D], BF16) for _ in range(NCH)]
        gsb = b.sb(ms, "mo_gs", [128, TB], F32)
        usb = b.sb(ms, "mo_us", [128, TB], F32)
        hmid = [b.sb(ms, "mo_hm", [128, TB], BF16) for _ in range(NCH)]
        wb = b.sb(ms, "mo_wb", [128, TB], F32)
        pg = [b.ps(ms, "mo_pg", [128, TB]) for _ in range(2)]
        pd = [b.ps(ms, "mo_pd", [128, TB]) for _ in range(2)]
        pwb = b.ps(ms, "mo_pwb", [128, TB])

        for e in range(8):
            srcs = (W["Wg_s"], W["Wu_s"], W["Wd_s"]) if e == 0 else \
                   (W["Wg_r"][e - 1], W["Wu_r"][e - 1], W["Wd_r"][e - 1])
            with nc.Block() as block:
                @block.sync
                def _(sync):
                    if e > 0:
                        b.wait(sync, "wtd0")
                        b.wait(sync, "wtd1")
                        b.dma(sync, wrow[0:1, :], HS["wts"][e - 1:e, :], "dma")
                        b.wait(sync, "dma")

                @block.gpsimd
                def _(gpsimd):
                    if e == 0:
                        b.wait(gpsimd, "wtd0")
                        b.wait(gpsimd, "wtd1")
                    for s in range(NCH):
                        b.dma(gpsimd, wg[s][:], srcs[0][s * 128:(s + 1) * 128, :],
                              f"gld{s % 2}")
                        b.dma(gpsimd, wu[s][:], srcs[1][s * 128:(s + 1) * 128, :],
                              f"gld{s % 2}")
                        b.dma(gpsimd, wd[s][:], srcs[2][s * 128:(s + 1) * 128, :],
                              f"gld{s % 2}")
                    b.wait(gpsimd, "gld0")
                    b.wait(gpsimd, "gld1")

            for tb in range(NTB):
                sl = slice(tb * TB, (tb + 1) * TB)
                if e > 0:
                    with nc.Block() as block:
                        @block.tensor
                        def _(tensor):
                            ins = tensor.matmul(pwb[:, :], K["ones_row32"][:],
                                                wrow[0:1, sl], start=True,
                                                stop=True)
                            b.inc(ins, "te", 1)

                        @block.scalar
                        def _(scalar):
                            b.wait(scalar, "te")
                            scalar.copy(out=wb[:], in_=pwb[:, :])
                            b.mark(scalar, b.tags, "sc", 1)

                (te0, sc0, ve0) = b.base("te", "sc", "ve")
                with nc.Block() as block:
                    @block.tensor
                    def _(tensor):
                        for ic in range(NCH):
                            it = ic
                            if it >= 1:
                                tensor.wait_ge(b.sem("ve"), ve0 + it)
                            for s in range(NCH):
                                tensor.matmul(pg[0][:, :],
                                              wg[s][:, ic * 128:(ic + 1) * 128],
                                              xn2[s][:, sl],
                                              start=(s == 0), stop=(s == NCH - 1))
                            for s in range(NCH):
                                ins = tensor.matmul(pg[1][:, :],
                                                    wu[s][:, ic * 128:(ic + 1) * 128],
                                                    xn2[s][:, sl],
                                                    start=(s == 0), stop=(s == NCH - 1))
                            b.inc(ins, "te", 1)

                    @block.scalar
                    def _(scalar):
                        for ic in range(NCH):
                            scalar.wait_ge(b.sem("te"), te0 + ic + 1)
                            if ic >= 1:
                                scalar.wait_ge(b.sem("ve"), ve0 + ic)
                            scalar.activation(out=gsb[:], in_=pg[0][:, :], func=AF.Silu)
                            scalar.activation(out=usb[:], in_=pg[1][:, :], func=AF.Copy,
                                              scale=0.5 if e == 0 else 1.0)
                            b.mark(scalar, b.tags, "sc", 1)

                    @block.vector
                    def _(vector):
                        for ic in range(NCH):
                            vector.wait_ge(b.sem("sc"), sc0 + ic + 1)
                            vector.tensor_mul(out=hmid[ic][:], in0=gsb[:], in1=usb[:])
                            vector.drain()
                            if e > 0:
                                vector.tensor_mul(out=hmid[ic][:], in0=hmid[ic][:],
                                                  in1=wb[:])
                            b.mark(vector, b.tagv, "ve", 1)

                (te0, ve0) = b.base("te", "ve")
                with nc.Block() as block:
                    @block.tensor
                    def _(tensor):
                        for hc in range(NCH):
                            if hc >= 2:
                                tensor.wait_ge(b.sem("ve"), ve0 + hc - 1)
                            for ic in range(NCH):
                                ins = tensor.matmul(pd[hc % 2][:, :],
                                                    wd[ic][:, hc * 128:(hc + 1) * 128],
                                                    hmid[ic][:],
                                                    start=(ic == 0), stop=(ic == NCH - 1))
                            b.inc(ins, "te", 1)

                    @block.vector
                    def _(vector):
                        for hc in range(NCH):
                            vector.wait_ge(b.sem("te"), te0 + hc + 1)
                            vector.scalar_tensor_tensor(out=X[hc][:, sl],
                                                        in0=pd[hc % 2][:, :], scalar=1.0,
                                                        in1=X[hc][:, sl],
                                                        op0=ALU.mult, op1=ALU.add)
                            b.mark(vector, b.tagv, "ve", 1)


def _emit_router(b, K, W, xn2, HS, ms):
    nc = b.nc
    with contextlib.ExitStack() as rs:
        wrt = b.sb(rs, "rt_w", [128, NCH * 7], BF16)
        rbias = b.sb(rs, "rt_b", [7, 1], F32)
        probsr = b.sb(rs, "rt_pr", [7, TB], F32)
        pT = b.sb(rs, "rt_pT", [128, 16 * 7], F32)
        m1 = b.sb(rs, "rt_m1", [128, 16], F32)
        m2 = b.sb(rs, "rt_m2", [128, 16], F32)
        eq = b.sb(rs, "rt_eq", [128, 16 * 7], F32)
        first = b.sb(rs, "rt_f1", [128, 16 * 7], F32)
        run = b.sb(rs, "rt_run", [128, 16], F32)
        nr = b.sb(rs, "rt_nr", [128, 16], F32)
        wtsT = b.sb(rs, "rt_wT", [128, 16 * 7], F32)
        masked = b.sb(rs, "rt_mk", [128, 16 * 7], F32)
        prt = b.ps(rs, "rt_ps", [7, TB])
        ptt = b.ps(rs, "rt_pt", [128, 7])
        ptb = b.ps(rs, "rt_ptb", [7, 128])

        with nc.Block() as block:
            @block.gpsimd
            def _(gpsimd):
                b.dma(gpsimd, wrt[:].rearrange("p (s c) -> p s c", c=7),
                      W["W_router"].rearrange("(s p) c -> p s c", p=128), "gld0")
                b.wait(gpsimd, "gld0")

            @block.sync
            def _(sync):
                b.dma(sync, rbias[:], W["rbias"], "dma")
                b.wait(sync, "dma")

        for tb in range(NTB):
            sl = slice(tb * TB, (tb + 1) * TB)
            (te0, sc0, ve0) = b.base("te", "sc", "ve")
            with nc.Block() as block:
                @block.tensor
                def _(tensor):
                    for s in range(NCH):
                        ins = tensor.matmul(prt[:, :], wrt[:, s * 7:(s + 1) * 7],
                                            xn2[s][:, sl],
                                            start=(s == 0), stop=(s == NCH - 1))
                    b.inc(ins, "te", 1)

                @block.scalar
                def _(scalar):
                    b.wait(scalar, "te")
                    scalar.activation(out=probsr[:], in_=prt[:, :], func=AF.Sigmoid,
                                      bias=rbias[:], scale=1.0)
                    b.mark(scalar, b.tags, "sc", 1)

            for j4 in range(4):
                t_idx = tb * 4 + j4
                with nc.Block() as block:
                    @block.tensor
                    def _(tensor):
                        ins = tensor.transpose(ptt[:, :],
                                               probsr[:, j4 * 128:(j4 + 1) * 128],
                                               K["ident32"][0:7, 0:7])
                        b.inc(ins, "te", 1)

                    @block.vector
                    def _(vector):
                        b.wait(vector, "te")
                        vector.tensor_copy(out=pT[:, t_idx * 7:(t_idx + 1) * 7],
                                           in_=ptt[:, :])
                        b.mark(vector, b.tagv, "ve", 1)

        with nc.Block() as block:
            @block.vector
            def _(vector):
                p3 = pT[:].rearrange("p (t e) -> p t e", e=7)
                vector.tensor_reduce(out=m1[:].rearrange("p (t o) -> p t o", o=1),
                                     in_=p3, axis=AX.X, op=ALU.max)
                vector.drain()
                vector.memset(run[:], 0.0)
                vector.drain()
                for e in range(7):
                    sl_e = pT[:, e::7]
                    vector.tensor_tensor(out=eq[:, e::7], in0=sl_e, in1=m1[:],
                                         op=ALU.is_ge)
                    vector.drain()
                for e in range(7):
                    vector.tensor_scalar(out=nr[:], in0=run[:], scalar1=-1.0,
                                         scalar2=1.0, op0=ALU.mult, op1=ALU.add)
                    vector.drain()
                    vector.tensor_mul(out=first[:, e::7], in0=eq[:, e::7], in1=nr[:])
                    vector.drain()
                    vector.tensor_add(out=run[:], in0=run[:], in1=first[:, e::7])
                    vector.drain()
                vector.scalar_tensor_tensor(out=masked[:], in0=first[:], scalar=-1e30,
                                            in1=pT[:], op0=ALU.mult, op1=ALU.add)
                vector.drain()
                vector.tensor_reduce(out=m2[:].rearrange("p (t o) -> p t o", o=1),
                                     in_=masked[:].rearrange("p (t e) -> p t e", e=7),
                                     axis=AX.X, op=ALU.max)
                vector.drain()
                vector.memset(run[:], 0.0)
                vector.drain()
                for e in range(7):
                    vector.tensor_tensor(out=eq[:, e::7], in0=masked[:, e::7], in1=m2[:],
                                         op=ALU.is_ge)
                    vector.drain()
                for e in range(7):
                    vector.tensor_scalar(out=nr[:], in0=run[:], scalar1=-1.0,
                                         scalar2=1.0, op0=ALU.mult, op1=ALU.add)
                    vector.drain()
                    vector.tensor_mul(out=eq[:, e::7], in0=eq[:, e::7], in1=nr[:])
                    vector.drain()
                    vector.tensor_add(out=run[:], in0=run[:], in1=eq[:, e::7])
                    vector.drain()
                vector.tensor_add(out=nr[:], in0=m1[:], in1=m2[:])
                vector.drain()
                vector.reciprocal(out=nr[:], in_=nr[:])
                vector.drain()
                vector.tensor_scalar_mul(out=nr[:], in0=nr[:], scalar1=0.5)
                vector.drain()
                for e in range(7):
                    vector.tensor_mul(out=first[:, e::7], in0=first[:, e::7], in1=m1[:])
                    vector.drain()
                    vector.tensor_mul(out=eq[:, e::7], in0=eq[:, e::7], in1=m2[:])
                    vector.drain()
                vector.tensor_add(out=wtsT[:], in0=first[:], in1=eq[:])
                vector.drain()
                for e in range(7):
                    vector.tensor_mul(out=wtsT[:, e::7], in0=wtsT[:, e::7], in1=nr[:])
                b.mark(vector, b.tagv, "ve", 1)

        wstage = b.sb(rs, "rt_wst", [7, 128], F32)
        for t_idx in range(NT128):
            with nc.Block() as block:
                @block.tensor
                def _(tensor):
                    ins = tensor.transpose(ptb[0:7, 0:128],
                                           wtsT[:, t_idx * 7:(t_idx + 1) * 7],
                                           K["ident32"][:])
                    b.inc(ins, "te", 1)

                @block.vector
                def _(vector):
                    b.wait(vector, "te")
                    b.wait(vector, "wtd0")
                    b.wait(vector, "wtd1")
                    vector.tensor_copy(out=wstage[:], in_=ptb[0:7, 0:128])
                    b.mark(vector, b.tagv, "ve", 1)

                @block.scalar
                def _(scalar):
                    b.wait(scalar, "ve")
                    b.dma(scalar, HS["wts"][:, t_idx * 128:(t_idx + 1) * 128],
                          wstage[:], f"wtd{t_idx % 2}")




def emit_l1_attn(b, K, X, W2, HS):
    nc = b.nc
    emit_stats(b, X, K, "ln")
    with contextlib.ExitStack() as osc:
        xn = [b.sb(osc, "la_xn", [128, T], BF16) for _ in range(NCH)]
        maskcols = b.sb(osc, "la_mc", [128, 16], F32)
        with contextlib.ExitStack() as nsc:
            rbs = b.sb(nsc, "la_rbs", [128, T], F32)
            mbs = b.sb(nsc, "la_mbs", [128, T], F32)
            mrow = b.sb(nsc, "la_mrow", [1, T], F32)
            pmk = b.ps(nsc, "la_pmk", [128, 1])
            emit_bcast_full(b, K, K["R"]["r"], rbs)
            emit_bcast_full(b, K, K["R"]["mur"], mbs)
            lnft = b.sb(nsc, "la_ft", [128, T], F32)
            with nc.Block() as block:
                @block.vector
                def _(vector):
                    for c in range(NCH):
                        vector.tensor_mul(out=lnft[:], in0=X[c][:], in1=rbs[:])
                        vector.drain()
                        vector.tensor_sub(out=xn[c][:], in0=lnft[:], in1=mbs[:])
                    b.mark(vector, b.tagv, "ve", 1)
            with nc.Block() as block:
                @block.vector
                def _(vector):
                    vector.tensor_mul(out=mrow[0:1, :], in0=X[0][0:1, :], in1=rbs[0:1, :])
                    vector.drain()
                    vector.tensor_sub(out=mrow[0:1, :], in0=mrow[0:1, :], in1=mbs[0:1, :])
                    vector.drain()
                    vector.tensor_single_scalar(out=mrow[0:1, :], in_=mrow[0:1, :],
                                                scalar=0.0, op=ALU.not_equal)
                    vector.drain()
                    vector.tensor_scalar(out=mrow[0:1, :], in0=mrow[0:1, :],
                                         scalar1=1.0, scalar2=1e30,
                                         op0=ALU.subtract, op1=ALU.mult)
                    b.mark(vector, b.tagv, "ve", 1)
            for kt in range(16):
                with nc.Block() as block:
                    @block.tensor
                    def _(tensor):
                        ins = tensor.transpose(pmk[:, 0:1],
                                               mrow[0:1, kt * 128:(kt + 1) * 128],
                                               K["ident32"][0:1, 0:1])
                        b.inc(ins, "te", 1)

                    @block.vector
                    def _(vector):
                        b.wait(vector, "te")
                        vector.tensor_copy(out=maskcols[:, kt:kt + 1], in_=pmk[:, 0:1])
                        b.mark(vector, b.tagv, "ve", 1)

        with contextlib.ExitStack() as pl:
            qp = b.sb(pl, "la_qp", [128, T], BF16)
            kp = b.sb(pl, "la_kp", [128, T], BF16)
            vpair = [b.sb(pl, "la_vp", [128, 130], BF16) for _ in range(NT128)]
            PT = [b.sb(pl, "la_pt", [128, 2 * TB], BF16) for _ in range(2)]
            wq_p = b.sb(pl, "la_wqp", [128, NCH * 128], BF16)
            wk_p = b.sb(pl, "la_wkp", [128, NCH * 128], BF16)
            wv_p = b.sb(pl, "la_wvp", [128, NCH * 128], BF16)
            bvt = b.sb(pl, "la_bvt", [128, 128], F32)
            sqb = b.sb(pl, "la_sqb", [128, TB], BF16)
            n11 = b.sb(pl, "la_n11", [1, 4], F32)
            bcol = b.sb(pl, "la_bcol", [128, 1], F32)
            bcols16 = b.sb(pl, "la_bc16", [128, 16], F32)
            rr = b.sb(pl, "la_rr", [1, TB], F32)
            rbx = b.sb(pl, "la_rbx", [64, TB], F32)
            cxst = b.sb(pl, "la_cxst", [128, TB], BF16)
            bqc = b.sb(pl, "la_bqc", [128, NCH], F32)
            bkc = b.sb(pl, "la_bkc", [128, NCH], F32)
            with nc.Block() as block:
                @block.sync
                def _(sync):
                    b.dma(sync, bqc[:], W2["bq"], "dma")
                    b.dma(sync, bkc[:], W2["bk"], "dma")
                    b.wait(sync, "dma")
            for p in range(NCH):
                _emit_mha_pair(b, K, W2, HS, p, xn, qp, kp, vpair, PT, wq_p, wk_p,
                               wv_p, bvt, sqb, n11, bcol, bcols16, maskcols,
                               rr, rbx, cxst, bqc, bkc)

        with contextlib.ExitStack() as oa:
            wo = [b.sb(oa, "la_wo", [128, D], BF16) for _ in range(NCH)]
            cx = [b.sb(oa, "la_cx", [128, TB], BF16) for _ in range(NCH)]
            boc = b.sb(oa, "la_boc", [128, NCH], F32)
            po = [b.ps(oa, "la_po", [128, TB]) for _ in range(2)]
            with nc.Block() as block:
                @block.gpsimd
                def _(gpsimd):
                    for s in range(NCH):
                        b.dma(gpsimd, wo[s][:], W2["woT"][s * 128:(s + 1) * 128, :],
                              f"gld{s % 2}")
                    b.wait(gpsimd, "gld0")
                    b.wait(gpsimd, "gld1")

                @block.sync
                def _(sync):
                    b.dma(sync, boc[:], W2["bo"], "dma")
                    b.wait(sync, "dma")
            for qb in range(NTB):
                qsl = slice(qb * TB, (qb + 1) * TB)
                with nc.Block() as block:
                    @block.sync
                    def _(sync):
                        if qb == 0:
                            b.wait(sync, "cxd0")
                            b.wait(sync, "cxd1")
                        for s in range(NCH):
                            b.dma(sync, cx[s][:], HS["ctx"][s * 128:(s + 1) * 128, qsl],
                                  "dma")
                        b.wait(sync, "dma")
                (te0, sc0, ve0) = b.base("te", "sc", "ve")
                with nc.Block() as block:
                    @block.tensor
                    def _(tensor):
                        for oc in range(NCH):
                            if oc >= 2:
                                tensor.wait_ge(b.sem("ve"), ve0 + oc - 1)
                            for s in range(NCH):
                                ins = tensor.matmul(po[oc % 2][:, :],
                                                    wo[s][:, oc * 128:(oc + 1) * 128],
                                                    cx[s][:], start=(s == 0),
                                                    stop=(s == NCH - 1))
                            b.inc(ins, "te", 1)

                    @block.scalar
                    def _(scalar):
                        for oc in range(NCH):
                            scalar.wait_ge(b.sem("te"), te0 + oc + 1)
                            if oc >= 1:
                                scalar.wait_ge(b.sem("ve"), ve0 + oc)
                            scalar.activation(out=K["ftmp32"][:, 0:TB], in_=po[oc % 2][:, :],
                                              func=AF.Identity, bias=boc[:, oc:oc + 1],
                                              scale=1.0)
                            b.mark(scalar, b.tags, "sc", 1)

                    @block.vector
                    def _(vector):
                        for oc in range(NCH):
                            vector.wait_ge(b.sem("sc"), sc0 + oc + 1)
                            vector.tensor_add(out=X[oc][:, qsl], in0=X[oc][:, qsl],
                                              in1=K["ftmp32"][:, 0:TB])
                            b.mark(vector, b.tagv, "ve", 1)


def _emit_mha_pair(b, K, W2, HS, p, xn, qp, kp, vpair, PT, wq_p, wk_p, wv_p,
                   bvt, sqb, n11, bcol, bcols16, maskcols, rr, rbx, cxst, bqc, bkc):
    nc = b.nc
    with contextlib.ExitStack() as sc_:
        with contextlib.ExitStack() as proj_sc:
            pq = b.ps(proj_sc, "ma_pq", [128, TB])
            pv = [b.ps(proj_sc, "ma_pv", [128, 128]) for _ in range(2)]
            pn = b.ps(proj_sc, "ma_pn", [1, TB])
            pb1 = b.ps(proj_sc, "ma_pb1", [128, 1])
            pbv = b.ps(proj_sc, "ma_pbv", [128, 128])

            with nc.Block() as block:
                @block.gpsimd
                def _(gpsimd):
                    b.dma(gpsimd, wq_p[:].rearrange("p (s c) -> p s c", c=128),
                          W2["wqT"][:, p * 128:(p + 1) * 128]
                          .rearrange("(s p) c -> p s c", p=128), "gld0")
                    b.dma(gpsimd, wk_p[:].rearrange("p (s c) -> p s c", c=128),
                          W2["wkT"][:, p * 128:(p + 1) * 128]
                          .rearrange("(s p) c -> p s c", p=128), "gld1")
                    b.dma(gpsimd, wv_p[:].rearrange("p (s c) -> p s c", c=128),
                          W2["wvT"][:, p * 128:(p + 1) * 128]
                          .rearrange("(s p) c -> p s c", p=128), "gld0")
                    b.wait(gpsimd, "gld0")
                    b.wait(gpsimd, "gld1")

            with nc.Block() as block:
                @block.tensor
                def _(tensor):
                    ins = tensor.matmul(pbv[:, :], K["ones_row32"][:],
                                        W2["bv_sb"][0:1, p * 128:(p + 1) * 128],
                                        start=True, stop=True)
                    b.inc(ins, "te", 1)

                @block.scalar
                def _(scalar):
                    b.wait(scalar, "te")
                    scalar.copy(out=bvt[:], in_=pbv[:, :])
                    b.mark(scalar, b.tags, "sc", 1)

            for (dst, wsrc, bc) in [(qp, wq_p, bqc), (kp, wk_p, bkc)]:
                for tb in range(NTB):
                    sl = slice(tb * TB, (tb + 1) * TB)
                    with nc.Block() as block:
                        @block.tensor
                        def _(tensor):
                            for s in range(NCH):
                                ins = tensor.matmul(pq[:, :],
                                                    wsrc[:, s * 128:(s + 1) * 128],
                                                    xn[s][:, sl],
                                                    start=(s == 0), stop=(s == NCH - 1))
                            b.inc(ins, "te", 1)

                        @block.scalar
                        def _(scalar):
                            b.wait(scalar, "te")
                            scalar.activation(out=dst[:, sl], in_=pq[:, :],
                                              func=AF.Identity, bias=bc[:, p:p + 1],
                                              scale=1.0)
                            b.mark(scalar, b.tags, "sc", 1)

            (te0, ve0) = b.base("te", "ve")
            with nc.Block() as block:
                @block.tensor
                def _(tensor):
                    for t in range(NT128):
                        if t >= 2:
                            tensor.wait_ge(b.sem("ve"), ve0 + t - 1)
                        for s in range(NCH):
                            ins = tensor.matmul(pv[t % 2][:, :],
                                                xn[s][:, t * 128:(t + 1) * 128],
                                                wv_p[:, s * 128:(s + 1) * 128],
                                                start=(s == 0), stop=(s == NCH - 1))
                        b.inc(ins, "te", 1)

                @block.vector
                def _(vector):
                    for t in range(NT128):
                        vector.wait_ge(b.sem("te"), te0 + t + 1)
                        vector.tensor_add(out=vpair[t][:, 0:64], in0=pv[t % 2][:, 0:64],
                                          in1=bvt[:, 0:64])
                        vector.drain()
                        vector.tensor_add(out=vpair[t][:, 65:129],
                                          in0=pv[t % 2][:, 64:128], in1=bvt[:, 64:128])
                        vector.drain()
                        vector.memset(vpair[t][:, 64:65], 1.0)
                        vector.memset(vpair[t][:, 129:130], 1.0)
                        b.mark(vector, b.tagv, "ve", 1)

            with nc.Block() as block:
                @block.vector
                def _(vector):
                    vector.memset(n11[0:1, 0:2], 0.0)
                    b.mark(vector, b.tagv, "ve", 1)
            for (i, srcb) in [(0, qp), (1, kp)]:
                for tb in range(NTB):
                    sl = slice(tb * TB, (tb + 1) * TB)
                    with nc.Block() as block:
                        @block.vector
                        def _(vector):
                            vector.tensor_mul(out=sqb[:], in0=srcb[:, sl],
                                              in1=srcb[:, sl])
                            b.mark(vector, b.tagv, "ve", 1)
                    with nc.Block() as block:
                        @block.tensor
                        def _(tensor):
                            ins = tensor.matmul(pn[0:1, :], K["onesm"][:], sqb[:],
                                                start=True, stop=True)
                            b.inc(ins, "te", 1)

                        @block.vector
                        def _(vector):
                            b.wait(vector, "te")
                            vector.tensor_reduce(out=n11[0:1, 2:3], in_=pn[0:1, :],
                                                 axis=AX.X, op=ALU.max)
                            vector.drain()
                            vector.tensor_max(out=n11[0:1, i:i + 1],
                                              in0=n11[0:1, i:i + 1], in1=n11[0:1, 2:3])
                            b.mark(vector, b.tagv, "ve", 1)
            with nc.Block() as block:
                @block.scalar
                def _(scalar):
                    scalar.activation(out=n11[0:1, 0:2], in_=n11[0:1, 0:2], func=AF.Sqrt)
                    b.mark(scalar, b.tags, "sc", 1)

                @block.vector
                def _(vector):
                    b.wait(vector, "sc")
                    vector.tensor_mul(out=n11[0:1, 3:4], in0=n11[0:1, 0:1],
                                      in1=n11[0:1, 1:2])
                    vector.drain()
                    vector.tensor_scalar_mul(out=n11[0:1, 3:4], in0=n11[0:1, 3:4],
                                             scalar1=-128.0)
                    b.mark(vector, b.tagv, "ve", 1)
            with nc.Block() as block:
                @block.tensor
                def _(tensor):
                    ins = tensor.matmul(pb1[:, 0:1], K["ones_row32"][:], n11[0:1, 3:4],
                                        start=True, stop=True)
                    b.inc(ins, "te", 1)

                @block.vector
                def _(vector):
                    b.wait(vector, "te")
                    vector.tensor_copy(out=bcol[:], in_=pb1[:, 0:1])
                    vector.drain()
                    vector.tensor_single_scalar(out=bcols16[:], in_=maskcols[:],
                                                scalar=bcol[:], op=ALU.add)
                    b.mark(vector, b.tagv, "ve", 1)

        for qb in range(NTB):
            _emit_pair_scores(b, K, W2, HS, p, qb, qp, kp, vpair, PT, None, bcol,
                              rr, rbx, cxst, sc_, causal=False, bcols16=bcols16)


def kernel(**inputs):
    import numpy as np
    from concourse.bass_utils import run_bass_kernel_spmd
    nc = build_program(n_cores=8, stages=6)
    in_maps = [make_inputs(inputs, c) for c in range(8)]
    res = run_bass_kernel_spmd(nc, in_maps, core_ids=list(range(8)))
    out = np.stack([np.ascontiguousarray(res.results[c]["out_fm"].T)
                    for c in range(8)], axis=0)
    return out.astype(np.float32)
